# version 07
# baseline (speedup 1.0000x reference)
import numpy as np
import ml_dtypes
import concourse.bass as bass
import concourse.bacc as bacc_mod
import concourse.mybir as mybir
import bass_rust as _bass_rust
from concourse.inst_simplify_rust import inst_simplify as _rust_inst_simplify
from contextlib import ExitStack

F32 = mybir.dt.float32
BF16 = mybir.dt.bfloat16
AF = mybir.ActivationFunctionType
ALU = mybir.AluOpType
AX = mybir.AxisListType.X
BF = ml_dtypes.bfloat16
ts = bass.ts

ENGINES = ("sync", "scalar", "vector", "tensor", "gpsimd")
ESEM = {"sync": "sSY", "scalar": "sSC", "vector": "sVE", "tensor": "sPE", "gpsimd": "sPL"}
QSEM = {"sync": "qS", "scalar": "qA", "gpsimd": "qG"}
SEM_OWNER = {}
for e, s in ESEM.items():
    SEM_OWNER[s] = e
for e, s in QSEM.items():
    SEM_OWNER[s] = e


class Tok:
    __slots__ = ("sem", "base", "off", "loop")

    def __init__(self, sem, base, off, loop=None):
        self.sem, self.base, self.off, self.loop = sem, base, off, loop


class Cfg:
    def __init__(self, TB=16384, TO=4096):
        self.TB = TB
        self.TO = TO
        self.D = 2048
        self.DL = 512
        self.DC = 256
        self.E = 16
        self.FE = 1024
        self.FS = 2048
        self.LN_EPS = 1e-5


class Prog:
    def __init__(self, cfg: Cfg):
        self.cfg = cfg
        self.nc = bacc_mod.Bacc("TRN2", target_bir_lowering=False)
        self.ctx = ExitStack()
        self.sections = {e: [] for e in ENGINES}
        self.C = {}
        self.sems = {}
        self.dram = {}
        self.sb = {}
        self.sb_guards = []
        self.ps = {}
        for s in list(ESEM.values()) + list(QSEM.values()):
            self.sem(s)

    def sem(self, name):
        if name not in self.sems:
            self.sems[name] = self.ctx.enter_context(self.nc.semaphore(name))
            self.C[name] = 0
        return self.sems[name]

    def inp(self, name, shape, dtype=F32):
        t = self.nc.dram_tensor(name, list(shape), dtype, kind="ExternalInput").ap()
        self.dram[name] = t
        return t

    def out(self, name, shape, dtype=F32):
        t = self.nc.dram_tensor(name, list(shape), dtype, kind="ExternalOutput").ap()
        self.dram[name] = t
        return t

    def scratch(self, name, shape, dtype=BF16):
        t = self.nc.dram_tensor(name, list(shape), dtype, kind="Internal").ap()
        self.dram[name] = t
        return t

    def sbuf(self, name, shape, dtype):
        g = self.nc.sbuf_tensor(name, list(shape), dtype)
        t = g.__enter__()
        self.sb[name] = t
        self.sb_guards.append((name, g))
        return t

    def free_sbufs(self, prefixes):
        while self.sb_guards and any(
                self.sb_guards[-1][0].startswith(pf) for pf in prefixes):
            name, g = self.sb_guards.pop()
            g.__exit__(None, None, None)
            del self.sb[name]

    def psum(self, name, shape, dtype=F32):
        t = self.ctx.enter_context(self.nc.psum_tensor(name, list(shape), dtype))
        self.ps[name] = t
        return t

    def barrier(self):
        self.sem("sBAR")
        self.C["sBAR"] += 5
        target = self.C["sBAR"]
        for e in ENGINES:
            own = [ESEM[e]]
            if e in QSEM:
                q = QSEM[e]
                own += [s for s in self.C if s.startswith(q + "_")]
            vals = [(s, self.C[s]) for s in own]
            def f(eng, vals=vals, target=target):
                for s, vl in vals:
                    eng.wait_ge(self.sems[s], vl)
                eng.nop().then_inc(self.sems["sBAR"], 1)
                eng.wait_ge(self.sems["sBAR"], target)
            self.sections[e].append(f)

    def straight(self, engine):
        em = Em(self, engine, loop=None)
        self.sections[engine].append(lambda eng, em=em: em._run(eng))
        return em

    def finalize(self):
        nc = self.nc
        block = self.ctx.enter_context(nc.Block())
        with self.ctx:
            @block.sync
            def _(eng):
                for f in self.sections["sync"]:
                    f(eng)

            @block.scalar
            def _(eng):
                for f in self.sections["scalar"]:
                    f(eng)

            @block.vector
            def _(eng):
                for f in self.sections["vector"]:
                    f(eng)

            @block.tensor
            def _(eng):
                for f in self.sections["tensor"]:
                    f(eng)

            @block.gpsimd
            def _(eng):
                for f in self.sections["gpsimd"]:
                    f(eng)
        import os
        if os.environ.get('FULLCOMPILE'):
            nc.compile()
        else:
            self.safe_compile()
        return nc

    def safe_compile(self):
        nc = self.nc
        _bass_rust.merge_adjacent_extseq(nc)
        _bass_rust.insert_bir_kernel_barrier_sem_inc(nc)
        _bass_rust.generate_event_semaphores(nc)
        _bass_rust.remove_dead_instructions_after_branch(nc.m)
        nc.validate_blocks()
        _bass_rust.dce_regs(nc)
        _bass_rust.remove_dead_allocations(nc.m)
        nc.verify_switch_hints()
        _bass_rust.alloc_regs(nc)
        _bass_rust.replace_nops_with_events(nc)
        _bass_rust.fuse_nops(nc.main_func)
        _bass_rust.remove_dead_nops(nc.main_func)
        _bass_rust.remove_dangling_data(nc.m)
        _bass_rust.generate_event_semaphores(nc)
        nc.insert_library_loads()
        _bass_rust.insert_act_table_loads(nc)
        _bass_rust.insert_hostgen_rebases(nc)
        mybir.codegen_inst_isa_subclasses(nc)


class Em:

    def __init__(self, p, engine, loop):
        self.p = p
        self.engine = engine
        self.loop = loop
        self.esem = ESEM[engine]
        self.qsem = QSEM.get(engine)
        self._ops = []
        self._i = None
        self._counting = False
        self._eng = None

    def _mk_tok(self, sem_name, step):
        if self.loop is None:
            self.p.C[sem_name] += step
            return Tok(sem_name, self.p.C[sem_name], 0)
        off = self.loop.off.get(sem_name, 0) + step
        self.loop.off[sem_name] = off
        return Tok(sem_name, None, off, loop=self.loop)

    SELF_SYNC = ("vector", "scalar", "gpsimd")

    def do(self, fn, inc=True):
        selfw = self.engine in Em.SELF_SYNC
        if self.loop is None:
            prev = self.p.C[self.esem]
            tok = self._mk_tok(self.esem, 1) if inc else None
            def cl(eng, fn=fn, inc=inc, prev=prev, selfw=selfw):
                if selfw and prev > 0:
                    eng.wait_ge(self.p.sems[self.esem], prev)
                ins = fn(eng)
                if inc:
                    ins.then_inc(self.p.sems[self.esem], 1)
            self._ops.append(cl)
            return tok
        off_before = self.loop.off.get(self.esem, 0)
        tok = self._mk_tok(self.esem, 1) if inc else None
        if self._counting:
            if selfw:
                self.loop.waited.setdefault(self.engine, set()).add(self.esem)
            return tok
        if selfw:
            per = self.loop.per[self.esem]
            base = self.loop.base[self.esem]
            self._eng.wait_ge(self.p.sems[self.esem],
                              self._i * per + base + off_before)
        ins = fn(self._eng)
        if inc:
            ins.then_inc(self.p.sems[self.esem], 1)
        return tok

    def dma(self, fn, ch="d"):
        sem = f"{self.qsem}_{ch}"
        self.p.sem(sem)
        if self.loop is not None and sem not in self.loop.extra_sems:
            self.loop.extra_sems.append(sem)
        tok = self._mk_tok(sem, 16)
        if self.loop is None:
            def cl(eng, fn=fn, sem=sem):
                fn(eng).then_inc(self.p.sems[sem], 16)
            self._ops.append(cl)
        elif not self._counting:
            fn(self._eng).then_inc(self.p.sems[sem], 16)
        return tok

    def wait(self, tok, lag=0):
        if tok is None:
            return
        if tok.loop is None:
            val = tok.base
            if self.loop is None:
                self._ops.append(lambda eng, val=val, s=tok.sem:
                                 eng.wait_ge(self.p.sems[s], val))
            elif not self._counting:
                self._eng.wait_ge(self.p.sems[tok.sem], val)
            return
        if self.loop is not tok.loop:
            val = tok.loop.final[tok.sem]
            if self.loop is None:
                self._ops.append(lambda eng, val=val, s=tok.sem:
                                 eng.wait_ge(self.p.sems[s], val))
            elif not self._counting:
                self._eng.wait_ge(self.p.sems[tok.sem], val)
            return
        if self._counting:
            if lag > 0:
                self.loop.lagreq.append((tok.sem, tok.off, lag))
            self.loop.waited.setdefault(self.engine, set()).add(tok.sem)
            return
        per = self.loop.per[tok.sem]
        base = self.loop.base[tok.sem]
        const = base + tok.off - lag * per
        assert const >= 0, f"negative wait base {const} for {tok.sem}"
        if per == 0:
            self._eng.wait_ge(self.p.sems[tok.sem], const)
        else:
            self._eng.wait_ge(self.p.sems[tok.sem], self._i * per + const)

    def w(self, tk, name, lag=0):
        tok = tk.get(name)
        if tok is None:
            assert self._counting or self.loop is None, f"missing token {name}"
            if not (self._counting or self.loop is None):
                raise KeyError(name)
            return
        self.wait(tok, lag=lag)

    def wait_abs(self, sem, val):
        if self.loop is None:
            self._ops.append(lambda eng, sem=sem, val=val:
                             eng.wait_ge(self.p.sems[sem], val))
        elif not self._counting:
            self._eng.wait_ge(self.p.sems[sem], val)

    def wait_prev_iters(self, tok):
        if tok is None:
            return
        assert self.loop is tok.loop and self.loop is not None
        if self._counting:
            self.loop.waited.setdefault(self.engine, set()).add(tok.sem)
            return
        per = self.loop.per[tok.sem]
        base = self.loop.base[tok.sem]
        self._eng.wait_ge(self.p.sems[tok.sem], self._i * per + base)

    def _run(self, eng):
        for cl in self._ops:
            cl(eng)


class Loop:
    def __init__(self, p, trip, name=""):
        self.p = p
        self.trip = trip
        self.name = name
        self.bodies = []
        self.off = {}
        self.per = {}
        self.base = {}
        self.final = {}
        self.lagreq = []
        self.extra_sems = []
        self.fab = {}
        self.waited = {}

    def sec(self, engine):
        def deco(f):
            self.bodies.append((engine, f))
            return f
        return deco

    def _engine_sems(self, engine):
        out = [ESEM[engine]]
        if engine in QSEM:
            q = QSEM[engine]
            out.append(q)
            out += [s for s in self.extra_sems if s.startswith(q + "_")]
        return out

    def close(self):
        p = self.p
        for pas in range(2):
            self.lagreq = []
            for e, f in self.bodies:
                for s in self._engine_sems(e):
                    self.off[s] = 0
                em = Em(p, e, loop=self)
                em._counting = True
                f(0, em)
        self.per = dict(self.off)
        self.per.update(self.fab)
        need = {}
        for sem, off, lag in self.lagreq:
            per = self.per.get(sem, 0)
            short = lag * per - off - p.C[sem]
            if short > 0:
                need[sem] = max(need.get(sem, 0), short)
        for sem, P in need.items():
            owner = SEM_OWNER[sem]
            assert sem not in QSEM.values(), f"cannot prime dma sem {sem}"
            cur = p.C[sem]
            def prime(eng, sem=sem, cur=cur, P=P):
                eng.wait_ge(self.p.sems[sem] if False else p.sems[sem], cur)
                eng.sem_inc(p.sems[sem], P)
            p.sections[owner].append(prime)
            p.C[sem] += P
        self.base = dict(p.C)
        for e, f in self.bodies:
            def section(eng, e=e, f=f):
                for s in self._engine_sems(e):
                    self.off[s] = 0
                em = Em(p, e, loop=self)
                em._eng = eng
                with eng.Fori(0, self.trip) as i:
                    em._i = i
                    f(i, em)
            p.sections[e].append(section)
        for s, per in self.per.items():
            if per and s not in self.fab:
                p.C[s] += per * self.trip
        self.final = dict(p.C)
        return self


def emit_zero_xs(p):
    XS = p.dram["XS"]
    nrow = XS.shape[0]
    zb = p.sb["zb"]
    v = p.straight("vector")
    tz = v.do(lambda e: e.memset(zb[:], 0.0))
    g = p.straight("gpsimd")
    g.wait(tz)
    toks = {}
    nb = nrow // 128
    for r in range(nb):
        ch = "cxr" if r % 2 == 0 else "csc"
        g.wait(toks.get(r - 2))
        toks[r] = g.dma(lambda e, r=r: e.dma_start(
            out=XS[r * 128:(r + 1) * 128, :], in_=zb[:]), ch=ch)
    g.wait(toks[nb - 1])
    g.wait(toks[nb - 2])
    return None


def emit_A1(p, src_name, T, xn_name):
    cfg = p.cfg
    src = p.dram[src_name]
    xn = p.dram[xn_name]
    NT = T // 128
    assert NT % 2 == 0
    x_ = [p.sb.get(f"a1_x{s}") or p.sbuf(f"a1_x{s}", [128, 2048], F32) for s in (0, 1)]
    o_ = [p.sb.get(f"a1_o{s}") or p.sbuf(f"a1_o{s}", [128, 2048], BF16) for s in (0, 1)]
    sq = p.sb.get("a1_sq") or p.sbuf("a1_sq", [128, 2048], F32)
    st = p.sb.get("a1_st") or p.sbuf("a1_st", [128, 12], F32)
    DINV = 1.0 / cfg.D
    L = Loop(p, NT // 2, "A1")
    toks = {}

    @L.sec("sync")
    def _(h, em):
        for s in (0, 1):
            em.wait(toks.get(f"xn{s}"), lag=1)
            toks[f"ld{s}"] = em.dma(
                lambda e, s=s: e.dma_start(out=x_[s][:], in_=src[ts(h * 2 + s, 128)]),
                ch=f"a1x{s}")

    @L.sec("vector")
    def _(h, em):
        for s in (0, 1):
            c = s * 5
            xt, ot = x_[s], o_[s]
            em.wait(toks[f"ld{s}"])
            em.wait(toks.get(f"stdone{s}"), lag=1)
            t_sq = em.do(lambda e, xt=xt, c=c: e.scalar_tensor_tensor(
                sq[:], xt[:], 1.0, xt[:], ALU.bypass, ALU.mult,
                accum_out=st[:, c + 1:c + 2]))
            t_sum = em.do(lambda e, xt=xt, c=c: e.reduce_sum(
                st[:, c:c + 1], xt[:], axis=AX))
            em.wait(t_sum)
            t1 = em.do(lambda e, c=c: e.tensor_scalar(
                st[:, c + 2:c + 3], st[:, c:c + 1], -DINV, None, ALU.mult))
            em.wait(t1)
            t2 = em.do(lambda e, c=c: e.scalar_tensor_tensor(
                st[:, c + 3:c + 4], st[:, c:c + 1], -DINV, st[:, c + 2:c + 3],
                ALU.mult, ALU.mult))
            em.wait(t_sq); em.wait(t2)
            t3 = em.do(lambda e, c=c: e.tensor_scalar(
                st[:, c + 3:c + 4], st[:, c + 1:c + 2], DINV, st[:, c + 3:c + 4],
                ALU.mult, ALU.subtract))
            em.wait(t3)
            t3b = em.do(lambda e, c=c: e.tensor_scalar(
                st[:, c + 3:c + 4], st[:, c + 3:c + 4], cfg.LN_EPS, None, ALU.add))
            em.wait(t3b)
            toks[f"var{s}"] = t3b
            em.wait(toks.get(f"rstd{s}"))
            t4 = em.do(lambda e, s=s: e.reciprocal(
                st[:, 10 + s:11 + s], st[:, 10 + s:11 + s]))
            em.wait(t4)
            toks[f"xn{s}"] = em.do(lambda e, xt=xt, ot=ot, c=c, s=s: e.tensor_scalar(
                ot[:], xt[:], st[:, c + 2:c + 3], st[:, 10 + s:11 + s],
                ALU.add, ALU.mult))

    @L.sec("scalar")
    def _(h, em):
        for s in (0, 1):
            c = s * 5
            em.wait(toks[f"var{s}"])
            toks[f"rstd{s}"] = em.do(lambda e, c=c, s=s: e.activation(
                st[:, 10 + s:11 + s], st[:, c + 3:c + 4], AF.Sqrt))

    @L.sec("gpsimd")
    def _(h, em):
        for s in (0, 1):
            em.wait(toks[f"xn{s}"])
            t = em.dma(lambda e, s=s: e.dma_start(
                out=xn[ts(h * 2 + s, 128)], in_=o_[s][:]), ch=f"a1o{s}")
            em.wait(t)
            toks[f"stdone{s}"] = em.do(lambda e: e.engine_nop())

    return L.close()


def emit_P0(p):
    cfg = p.cfg
    gb = p.sbuf("gb", [128, 4], F32)
    bo = p.sbuf("bo", [128, 16], F32)
    idb = p.sbuf("idb", [128, 128], BF16)
    idf = p.sbuf("idf", [128, 128], F32)
    ones = p.sbuf("ones", [128, 1], BF16)
    ones1 = p.sbuf("ones1", [1, 128], BF16)
    p.sbuf("zb", [128, 2048], BF16)
    specs = [("wkvT", 16 * 4), ("wcT", 4 * 2), ("wqT", 2 * 2), ("wkvcT", 2 * 4),
             ("wc2lT", 2 * 4)]
    g = p.straight("gpsimd")
    toks = {}
    for name, n in specs:
        sb = p.sbuf(name, [128, n * 128], BF16)
        inp = p.dram[name + "_t"]
        toks[name] = g.dma(lambda e, sb=sb, inp=inp: e.dma_start(
            out=sb[:].rearrange("p (n f) -> p n f", f=128),
            in_=inp.rearrange("a b p f -> p (a b) f")), ch="w")
    wsum = p.sbuf("wsum", [128, 64 * 128], BF16)
    wsA = p.sbuf("wsA", [128, 16 * 128], F32)
    wsB = p.sbuf("wsB", [128, 16 * 128], F32)
    sy = p.straight("sync")
    v = p.straight("vector")
    for z in range(4):
        ta = sy.dma(lambda e, z=z: e.dma_start(
            out=wsA[:].rearrange("p (n f) -> p n f", f=128),
            in_=p.dram["wsumA_t"][z].rearrange("b p f -> p b f")), ch="wsA")
        tb = sy.dma(lambda e, z=z: e.dma_start(
            out=wsB[:].rearrange("p (n f) -> p n f", f=128),
            in_=p.dram["wsumB_t"][z].rearrange("b p f -> p b f")), ch="wsB")
        v.wait(ta); v.wait(tb)
        tw = v.do(lambda e, z=z: e.tensor_add(
            wsum[:, z * 2048:(z + 1) * 2048], wsA[:], wsB[:]))
        if z < 3:
            sy.wait(tw)
    toks["wsum"] = tw
    toks["gb"] = sy.dma(lambda e: e.dma_start(out=gb[:], in_=p.dram["gb_h"]), ch="gb")
    toks["bo"] = sy.dma(lambda e: e.dma_start(out=bo[:], in_=p.dram["bo_h"]), ch="gb")
    toks["idb"] = sy.dma(lambda e: e.dma_start(out=idb[:], in_=p.dram["ident_b"]), ch="id")
    toks["idf"] = sy.dma(lambda e: e.dma_start(out=idf[:], in_=p.dram["ident_f"]), ch="id")
    toks["ones"] = v.do(lambda e: e.memset(ones[:], 1.0))
    toks["ones1"] = v.do(lambda e: e.memset(ones1[:], 1.0))
    return toks


def emit_A2(p, xn_name, T, cos_name, sin_name, do_kv, kf_name=None,
            vt_name=None, qf_name=None):
    cfg = p.cfg
    NG = T // 1024
    xnt = p.dram[xn_name]
    cost_, sint_ = p.dram[cos_name], p.dram[sin_name]
    wkvT, wcT, wqT, wkvcT = p.sb["wkvT"], p.sb["wcT"], p.sb["wqT"], p.sb["wkvcT"]
    gb, ones, ones1, idb = p.sb["gb"], p.sb["ones"], p.sb["ones1"], p.sb["idb"]
    DCI = 1.0 / cfg.DC

    def get(name, shape, dt):
        return p.sb.get(name) or p.sbuf(name, shape, dt)
    xnf = get("a2_xnf", [128, 16 * 1024], BF16)
    kvl = get("a2_kvl", [128, 4 * 1024], BF16)
    c32 = get("a2_c32", [128, 2 * 1024], F32)
    cbf = get("a2_cbf", [128, 2 * 1024], BF16)
    csq = get("a2_csq", [128, 2 * 1024], BF16)
    rows = get("a2_rows", [1, 8 * 512], F32)
    rowb = get("a2_rowb", [1, 4 * 512], BF16)
    cos_s = get("a2_cos", [128, 2 * 1024], F32)
    sin_s = get("a2_sin", [128, 2 * 1024], F32)
    tA = get("a2_tA", [128, 512], F32)
    tB = get("a2_tB", [128, 512], F32)
    tC = get("a2_tC", [128, 512], F32)
    tD = get("a2_tD", [128, 512], F32)
    ch_s = get("a2_ch", [128, 2 * 1024], BF16)
    kq = get("a2_kq", [128, 2 * 1024], BF16)
    vbf = get("a2_vbf", [128, 2 * 1024], BF16)
    vtm = get("a2_vtm", [128, 4 * 256], BF16)
    PK = [p.ps["pf0"], p.ps["pf1"]]
    PC = [p.ps["pf2"], p.ps["pf3"]]
    PS = [p.ps["pf4"], p.ps["pf5"]]
    PT = [p.ps["pt0"], p.ps["pt1"]]

    def rcol(u, k):
        c = (u * 4 + k) * 512
        return rows[:, c:c + 512]

    def rbcol(u, k):
        c = (u * 2 + k) * 512
        return rowb[:, c:c + 512]

    L = Loop(p, NG, "A2kv" if do_kv else "A2q")
    tk = {}
    nout = 4 if do_kv else 2
    wproj = wkvcT if do_kv else wqT
    pfx = ""
    for cc in range(2):
        for nm in ("cos", "sin"):
            ch = f"qS_{pfx}{nm}{cc}"
            p.sem(ch)
            L.fab[ch] = 16
            tk[f"{nm}{cc}"] = Tok(ch, None, 16, loop=L)
    for dc in range(16):
        ch = f"qS_{pfx}xn{dc}"
        p.sem(ch)
        L.fab[ch] = 16
        tk[f"xn{dc}"] = Tok(ch, None, 16, loop=L)
    nrelay = 10 if do_kv else 2
    L.fab["sSY"] = nrelay
    tk["kqst0"] = Tok("sSY", None, 1, loop=L)
    tk["kqst1"] = Tok("sSY", None, 2, loop=L)
    if do_kv:
        for j in range(6):
            tk[f"vtmrelay{j}"] = Tok("sSY", None, 3 + j, loop=L)
        tk["vtmrelay6"] = Tok("sSY", None, 9, loop=L)
        tk["vtmrelay7"] = Tok("sSY", None, 10, loop=L)

    @L.sec("scalar")
    def _(g, em):
        for u in range(2):
            em.w(tk, f"vareps{u}")
            tk[f"sqrt{u}"] = em.do(lambda e, u=u: e.activation(
                rcol(u, 3), rcol(u, 2), AF.Sqrt))

    @L.sec("tensor")
    def _(g, em):
        for idx in range(8):
            l, u = divmod(idx, 2)
            bank = PK[idx % 2]
            if idx >= 2:
                em.w(tk, f"kvlevac{idx - 2}" if "{" in "kvlevac{idx - 2}" else "kvlevac{idx - 2}")
            else:
                em.w(tk, f"ropePK{idx % 2}_1" if "{" in "ropePK{idx % 2}_1" else "ropePK{idx % 2}_1", lag=1)
            for dc in range(16):
                em.w(tk, f"xn{dc}" if "{" in "xn{dc}" else "xn{dc}")
                mm = em.do(lambda e, l=l, u=u, dc=dc, bank=bank: e.matmul(
                    bank[:], wkvT[:, (dc * 4 + l) * 128:(dc * 4 + l + 1) * 128],
                    xnf[:, dc * 1024 + u * 512:dc * 1024 + (u + 1) * 512],
                    start=(dc == 0), stop=(dc == 15)), inc=(dc == 15))
            tk[f"kvlmm{idx}"] = mm
        for idx in range(4):
            cc, u = divmod(idx, 2)
            bank = PC[idx % 2]
            if idx >= 2:
                em.w(tk, f"cevac{idx - 2}" if "{" in "cevac{idx - 2}" else "cevac{idx - 2}")
            elif do_kv:
                em.w(tk, f"vbfe1_{idx % 2}" if "{" in "vbfe1_{idx % 2}" else "vbfe1_{idx % 2}", lag=1)
            else:
                em.w(tk, f"cevac{idx + 2}" if "{" in "cevac{idx + 2}" else "cevac{idx + 2}", lag=1)
            for l in range(4):
                em.w(tk, f"kvlevac{l * 2 + u}" if "{" in "kvlevac{l * 2 + u}" else "kvlevac{l * 2 + u}")
                mm = em.do(lambda e, cc=cc, u=u, l=l, bank=bank: e.matmul(
                    bank[:], wcT[:, (l * 2 + cc) * 128:(l * 2 + cc + 1) * 128],
                    kvl[:, l * 1024 + u * 512:l * 1024 + (u + 1) * 512],
                    start=(l == 0), stop=(l == 3)), inc=(l == 3))
            tk[f"cmm{idx}"] = mm
        for u in range(2):
            if u == 0:
                em.w(tk, f"chPS0_1" if "{" in "chPS0_1" else "chPS0_1", lag=1)
                em.w(tk, f"chPS1_1" if "{" in "chPS1_1" else "chPS1_1", lag=1)
            else:
                em.w(tk, f"psS0r0" if "{" in "psS0r0" else "psS0r0")
                em.w(tk, f"psS1r0" if "{" in "psS1r0" else "psS1r0")
            em.w(tk, f"cbf{u}" if "{" in "cbf{u}" else "cbf{u}")
            em.do(lambda e, u=u: e.matmul(
                PS[0][0:1, :], ones[:], cbf[:, u * 512:u * 512 + 512],
                start=True, stop=False), inc=False)
            tk[f"S1{u}"] = em.do(lambda e, u=u: e.matmul(
                PS[0][0:1, :], ones[:], cbf[:, 1024 + u * 512:1024 + u * 512 + 512],
                start=False, stop=True))
            em.w(tk, f"csq{u}" if "{" in "csq{u}" else "csq{u}")
            em.do(lambda e, u=u: e.matmul(
                PS[1][0:1, :], ones[:], csq[:, u * 512:u * 512 + 512],
                start=True, stop=False), inc=False)
            tk[f"S2{u}"] = em.do(lambda e, u=u: e.matmul(
                PS[1][0:1, :], ones[:], csq[:, 1024 + u * 512:1024 + u * 512 + 512],
                start=False, stop=True))
        for u in range(2):
            em.w(tk, f"rowb{u}" if "{" in "rowb{u}" else "rowb{u}")
            if u == 0:
                em.w(tk, f"psS0r1" if "{" in "psS0r1" else "psS0r1")
                em.w(tk, f"psS1r1" if "{" in "psS1r1" else "psS1r1")
            else:
                em.w(tk, f"chPS0_0" if "{" in "chPS0_0" else "chPS0_0")
                em.w(tk, f"chPS1_0" if "{" in "chPS1_0" else "chPS1_0")
            tk[f"bcmu{u}"] = em.do(lambda e, u=u: e.matmul(
                PS[0][:], ones1[:], rbcol(u, 0), start=True, stop=True))
            tk[f"bcr{u}"] = em.do(lambda e, u=u: e.matmul(
                PS[1][:], ones1[:], rbcol(u, 1), start=True, stop=True))
        for u in range(2):
            for oc in range(2):
                if u == 0:
                    em.w(tk, f"kvlevac{6 + oc}" if "{" in "kvlevac{6 + oc}" else "kvlevac{6 + oc}")
                else:
                    em.w(tk, f"ropePK{oc}_0" if "{" in "ropePK{oc}_0" else "ropePK{oc}_0")
                for cc in range(2):
                    em.w(tk, f"ch{cc}_{u}" if "{" in "ch{cc}_{u}" else "ch{cc}_{u}")
                    mm = em.do(lambda e, oc=oc, u=u, cc=cc: e.matmul(
                        PK[oc][:], wproj[:, (cc * nout + oc) * 128:(cc * nout + oc + 1) * 128],
                        ch_s[:, cc * 1024 + u * 512:cc * 1024 + (u + 1) * 512],
                        start=(cc == 0), stop=(cc == 1)), inc=(cc == 1))
                tk[f"projk{oc}_{u}"] = mm
            if do_kv:
                for vc in range(2):
                    if u == 0:
                        em.w(tk, f"cevac{2 + vc}" if "{" in "cevac{2 + vc}" else "cevac{2 + vc}")
                    else:
                        em.w(tk, f"vbfe0_{vc}" if "{" in "vbfe0_{vc}" else "vbfe0_{vc}")
                    for cc in range(2):
                        em.w(tk, f"ch{cc}_{u}" if "{" in "ch{cc}_{u}" else "ch{cc}_{u}")
                        mm = em.do(lambda e, vc=vc, u=u, cc=cc: e.matmul(
                            PC[vc][:], wproj[:, (cc * 4 + 2 + vc) * 128:(cc * 4 + 3 + vc) * 128],
                            ch_s[:, cc * 1024 + u * 512:cc * 1024 + (u + 1) * 512],
                            start=(cc == 0), stop=(cc == 1)), inc=(cc == 1))
                    tk[f"projv{vc}_{u}"] = mm
        if do_kv:
            for i8 in range(8):
                u, tt = divmod(i8, 4)
                bank = PT[i8 % 2]
                em.w(tk, f"vbfe{u}_1" if "{" in "vbfe{u}_1" else "vbfe{u}_1")
                if i8 >= 2:
                    em.w(tk, f"vtmevac{i8 - 2}" if "{" in "vtmevac{i8 - 2}" else "vtmevac{i8 - 2}")
                else:
                    em.w(tk, f"vtmevac{6 + i8 % 2}" if "{" in "vtmevac{6 + i8 % 2}" else "vtmevac{6 + i8 % 2}", lag=1)
                for vc in range(2):
                    mm = em.do(lambda e, u=u, tt=tt, vc=vc, bank=bank: e.matmul(
                        bank[:, vc * 128:(vc + 1) * 128],
                        vbf[:, vc * 1024 + u * 512 + tt * 128:vc * 1024 + u * 512 + (tt + 1) * 128],
                        idb[:], start=(vc == 0), stop=(vc == 1), is_transpose=True),
                        inc=(vc == 1))
                tk[f"vT{i8}"] = mm

    @L.sec("vector")
    def _(g, em):
        for idx in range(8):
            l, u = divmod(idx, 2)
            em.w(tk, f"kvlmm{idx}" if "{" in "kvlmm{idx}" else "kvlmm{idx}")
            tk[f"kvlevac{idx}"] = em.do(lambda e, l=l, u=u, idx=idx: e.tensor_copy(
                kvl[:, l * 1024 + u * 512:l * 1024 + (u + 1) * 512], PK[idx % 2][:]))
        for idx in range(4):
            cc, u = divmod(idx, 2)
            em.w(tk, f"cmm{idx}" if "{" in "cmm{idx}" else "cmm{idx}")
            em.do(lambda e, cc=cc, u=u, idx=idx: e.tensor_copy(
                c32[:, cc * 1024 + u * 512:cc * 1024 + (u + 1) * 512], PC[idx % 2][:]))
            t2 = em.do(lambda e, cc=cc, u=u, idx=idx: e.tensor_copy(
                cbf[:, cc * 1024 + u * 512:cc * 1024 + (u + 1) * 512], PC[idx % 2][:]))
            t3 = em.do(lambda e, cc=cc, u=u, idx=idx: e.scalar_tensor_tensor(
                csq[:, cc * 1024 + u * 512:cc * 1024 + (u + 1) * 512],
                c32[:, cc * 1024 + u * 512:cc * 1024 + (u + 1) * 512], 1.0,
                PC[idx % 2][:], ALU.bypass, ALU.mult))
            tk[f"cevac{idx}"] = t3
            if cc == 1:
                tk[f"cbf{u}"] = t2
                tk[f"csq{u}"] = t3
        for u in range(2):
            em.w(tk, f"S1{u}" if "{" in "S1{u}" else "S1{u}")
            t1 = em.do(lambda e, u=u: e.tensor_scalar(
                rcol(u, 1), PS[0][0:1, :], -DCI, None, ALU.mult))
            em.wait(t1)
            t2 = em.do(lambda e, u=u: e.scalar_tensor_tensor(
                rcol(u, 2), PS[0][0:1, :], -DCI, rcol(u, 1), ALU.mult, ALU.mult))
            tk[f"psS0r{u}"] = t2
            em.wait(t2); em.w(tk, f"S2{u}" if "{" in "S2{u}" else "S2{u}")
            t3 = em.do(lambda e, u=u: e.scalar_tensor_tensor(
                rcol(u, 2), PS[1][0:1, :], DCI, rcol(u, 2), ALU.mult, ALU.subtract))
            tk[f"psS1r{u}"] = t3
            em.wait(t3)
            t3b = em.do(lambda e, u=u: e.tensor_scalar(
                rcol(u, 2), rcol(u, 2), cfg.LN_EPS, None, ALU.add))
            tk[f"vareps{u}"] = t3b
            em.w(tk, f"sqrt{u}" if "{" in "sqrt{u}" else "sqrt{u}")
            t4 = em.do(lambda e, u=u: e.reciprocal(rcol(u, 3), rcol(u, 3)))
            em.wait(t4)
            t5 = em.do(lambda e, u=u: e.tensor_copy(rbcol(u, 0), rcol(u, 1)))
            t6 = em.do(lambda e, u=u: e.tensor_copy(rbcol(u, 1), rcol(u, 3)))
            em.wait(t5); em.wait(t6)
            tk[f"rowb{u}"] = t6
        for u in range(2):
            for cc in range(2):
                em.w(tk, f"bcmu{u}" if "{" in "bcmu{u}" else "bcmu{u}"); em.w(tk, f"bcr{u}" if "{" in "bcr{u}" else "bcr{u}")
                em.w(tk, f"cevac{cc * 2 + u}" if "{" in "cevac{cc * 2 + u}" else "cevac{cc * 2 + u}")
                em.w(tk, f"projk1_{u}" if "{" in "projk1_{u}" else "projk1_{u}", lag=1)
                t1 = em.do(lambda e, cc=cc, u=u: e.scalar_tensor_tensor(
                    tA[:], c32[:, cc * 1024 + u * 512:cc * 1024 + (u + 1) * 512],
                    1.0, PS[0][:], ALU.bypass, ALU.add))
                em.wait(t1)
                t2 = em.do(lambda e, cc=cc, u=u: e.scalar_tensor_tensor(
                    tB[:], tA[:], 1.0, PS[1][:], ALU.bypass, ALU.mult))
                em.wait(t2)
                t3 = em.do(lambda e, cc=cc, u=u: e.tensor_scalar(
                    ch_s[:, cc * 1024 + u * 512:cc * 1024 + (u + 1) * 512],
                    tB[:], gb[:, cc:cc + 1], gb[:, 2 + cc:3 + cc], ALU.mult, ALU.add))
                tk[f"ch{cc}_{u}"] = t3
                if cc == 1:
                    tk[f"chPS0_{u}"] = t1
                    tk[f"chPS1_{u}"] = t2
        for u in range(2):
            em.w(tk, f"projk0_{u}" if "{" in "projk0_{u}" else "projk0_{u}"); em.w(tk, f"projk1_{u}" if "{" in "projk1_{u}" else "projk1_{u}")
            for cc in range(2):
                em.w(tk, f"cos{cc}" if "{" in "cos{cc}" else "cos{cc}"); em.w(tk, f"sin{cc}" if "{" in "sin{cc}" else "sin{cc}")
            em.w(tk, f"kqst{u}" if "{" in "kqst{u}" else "kqst{u}", lag=1)
            t1 = em.do(lambda e, u=u: e.scalar_tensor_tensor(
                tA[:], cos_s[:, u * 512:(u + 1) * 512], 1.0, PK[0][:],
                ALU.bypass, ALU.mult))
            t2 = em.do(lambda e, u=u: e.scalar_tensor_tensor(
                tB[:], sin_s[:, u * 512:(u + 1) * 512], 1.0, PK[1][:],
                ALU.bypass, ALU.mult))
            em.wait(t1); em.wait(t2)
            em.do(lambda e, u=u: e.tensor_sub(
                kq[:, u * 512:(u + 1) * 512], tA[:], tB[:]))
            t4 = em.do(lambda e, u=u: e.scalar_tensor_tensor(
                tC[:], cos_s[:, 1024 + u * 512:1024 + (u + 1) * 512], 1.0, PK[1][:],
                ALU.bypass, ALU.mult))
            tk[f"ropePK1_{u}"] = t4
            t5 = em.do(lambda e, u=u: e.scalar_tensor_tensor(
                tD[:], sin_s[:, 1024 + u * 512:1024 + (u + 1) * 512], 1.0, PK[0][:],
                ALU.bypass, ALU.mult))
            tk[f"ropePK0_{u}"] = t5
            em.wait(t4); em.wait(t5)
            t6 = em.do(lambda e, u=u: e.tensor_add(
                kq[:, 1024 + u * 512:1024 + (u + 1) * 512], tC[:], tD[:]))
            tk[f"rope{u}"] = t6
        if do_kv:
            for u in range(2):
                for vc in range(2):
                    em.w(tk, f"projv{vc}_{u}" if "{" in "projv{vc}_{u}" else "projv{vc}_{u}")
                    em.w(tk, f"vT{u * 4 + 3}" if "{" in "vT{u * 4 + 3}" else "vT{u * 4 + 3}", lag=1)
                    tk[f"vbfe{u}_{vc}"] = em.do(lambda e, u=u, vc=vc: e.tensor_copy(
                        vbf[:, vc * 1024 + u * 512:vc * 1024 + (u + 1) * 512],
                        PC[vc][:]))
            for i8 in range(8):
                em.w(tk, f"vT{i8}")
                if i8 >= 4:
                    em.w(tk, f"vtmrelay{i8 - 4}")
                else:
                    em.w(tk, f"vtmrelay{i8 + 4}", lag=1)
                tk[f"vtmevac{i8}"] = em.do(lambda e, i8=i8: e.tensor_copy(
                    vtm[:, (i8 % 4) * 256:(i8 % 4 + 1) * 256], PT[i8 % 2][:, 0:256]))

    L.close()

    sy = p.straight("sync")
    dst = p.dram[kf_name] if do_kv else p.dram[qf_name]
    vt = p.dram[vt_name] if do_kv else None

    def tokval(name, g):
        t = tk[name]
        return L.base[t.sem] + g * L.per[t.sem] + t.off

    for g in range(NG):
        if g >= 1:
            sy.wait_abs("sVE", tokval("rope1", g - 1))
            sy.wait_abs("sPE", tokval("kvlmm7", g - 1))
        for cc in range(2):
            sy.dma(lambda e, g=g, cc=cc: e.dma_start(
                out=cos_s[:, cc * 1024:(cc + 1) * 1024],
                in_=cost_[cc][:, g * 1024:(g + 1) * 1024]), ch=f"{pfx}cos{cc}")
            sy.dma(lambda e, g=g, cc=cc: e.dma_start(
                out=sin_s[:, cc * 1024:(cc + 1) * 1024],
                in_=sint_[cc][:, g * 1024:(g + 1) * 1024]), ch=f"{pfx}sin{cc}")
        for dc in range(16):
            sy.dma(lambda e, g=g, dc=dc: e.dma_start(
                out=xnf[:, dc * 1024:(dc + 1) * 1024],
                in_=xnt[g * 1024:(g + 1) * 1024, dc * 128:(dc + 1) * 128],
                transpose=True), ch=f"{pfx}xn{dc}")
        kqtoks = []
        for u in range(2):
            sy.wait_abs("sVE", tokval(f"rope{u}", g))
            kqtoks.append(sy.dma(lambda e, g=g, u=u: e.dma_start(
                out=dst[(g * 2 + u) * 128:(g * 2 + u + 1) * 128].rearrange(
                    "p (cc f) -> p cc f", f=512),
                in_=kq[:].rearrange("p (cc g2 f) -> p cc (g2 f)", cc=2, g2=2)[
                    :, :, u * 512:(u + 1) * 512]),
                ch=f"{pfx}kq{u}"))
        for t_ in kqtoks:
            sy.wait(t_)
        sy.do(lambda e: e.nop())
        sy.do(lambda e: e.nop())
        if do_kv:
            vtoks = {}
            for i8 in range(8):
                u, tt = divmod(i8, 4)
                sy.wait_abs("sVE", tokval(f"vtmevac{i8}", g))
                vtoks[i8] = sy.dma(lambda e, g=g, u=u, tt=tt, i8=i8: e.dma_start(
                    out=vt[g * 1024 + u * 512 + tt * 128:g * 1024 + u * 512 + (tt + 1) * 128],
                    in_=vtm[:, (i8 % 4) * 256:(i8 % 4 + 1) * 256]), ch=f"vt{i8 % 4}")
                if i8 >= 2:
                    sy.wait(vtoks[i8 - 2])
                    sy.do(lambda e: e.nop())
            for i8 in (6, 7):
                sy.wait(vtoks[i8])
                sy.do(lambda e: e.nop())
    return L


def emit_B(p, qf_name, kf_name, vt_name, xo_name, out_name, NKC, NQT):
    cfg = p.cfg
    QF, KF, VT = p.dram[qf_name], p.dram[kf_name], p.dram[vt_name]
    XO, OUT = p.dram[xo_name], p.dram[out_name]
    DT, DL = p.dram["dtile"], p.dram["dltab"]
    wc2lT, wsum, bo, idb, idf = (p.sb["wc2lT"], p.sb["wsum"], p.sb["bo"],
                                 p.sb["idb"], p.sb["idf"])

    def get(name, shape, dt):
        return p.sb.get(name) or p.sbuf(name, shape, dt)
    q_sb = get("b_q", [128, 1024], BF16)
    k_sb = get("b_k", [128, 4 * 1024], BF16)
    v_sb = get("b_v", [128, 16 * 256], BF16)
    d_sb = get("b_d", [128, 2048], F32)
    dl_sb = get("b_dl", [128, 8], F32)
    p_sb = get("b_p", [128, 2048], BF16)
    msk = get("b_m", [128, 2048], BF16)
    pm_sb = get("b_pm", [128, 2048], BF16)
    pt_sb = get("b_pt", [128, 2048], BF16)
    o_acc = get("b_o", [128, 4 * 256], F32)
    bst = get("b_st", [128, 4 * 8], F32)
    ob = get("b_ob", [128, 1024], BF16)
    o_fm = get("b_ofm", [128, 1024], BF16)
    z_sb = get("b_z", [128, 4 * 512], BF16)
    att = get("b_att", [128, 16 * 512], F32)
    xtm = get("b_xtm", [128, 2048], F32)
    otm = get("b_otm", [128, 2048], F32)
    PO = [p.ps["pf2"], p.ps["pf3"]]
    psS_h = p.ps["psS"]
    PZ = [psS_h[:, 0:512], psS_h[:, 512:1024]]
    PT = [p.ps["pt0"], p.ps["pt1"]]

    def col(sub, k):
        return bst[:, sub * 8 + k:sub * 8 + k + 1]

    for qt in range(NQT):
        sy = p.straight("sync")
        v0 = p.straight("vector")
        tq = sy.dma(lambda e, qt=qt: e.dma_start(
            out=q_sb[:], in_=QF[qt * 128:(qt + 1) * 128]), ch="bq")
        tdl = None
        if qt == 0:
            emit_B.dt_tok = sy.dma(lambda e: e.dma_start(out=d_sb[:], in_=DT), ch="bdt")
        tz0 = v0.do(lambda e: e.memset(o_acc[:], 0.0))
        tz1 = v0.do(lambda e: e.memset(bst[:], 0.0))
        te0 = p.straight("tensor")
        te0.wait_abs("sVE", p.C["sVE"])
        tz2 = v0.do(lambda e: e.tensor_scalar(
            bst[:].rearrange("p (s c) -> p s c", c=8)[:, :, 0:1],
            bst[:].rearrange("p (s c) -> p s c", c=8)[:, :, 0:1],
            0.0, 1.0e30, ALU.mult, ALU.add))
        v0.wait(tz2)
        tk = {"q": tq, "z": tz2}

        NKB = NKC // 4
        L = Loop(p, NKB, f"B{qt}")
        SBIG = p.ps["psS"]

        @L.sec("gpsimd")
        def _(kb_, em, tk=tk):
            em.wait(tk["q"])
            em.wait(emit_B.dt_tok)
            em.w(tk, "mg3", lag=1)
            tk["dl4"] = em.dma(lambda e, qt=qt: e.dma_start(
                out=dl_sb[:, 0:4],
                in_=DL[:, bass.ds((qt * NKB + kb_) * 4, 4)]), ch="bdl0")
            em.w(tk, "Smm3", lag=1)
            tk["kld"] = em.dma(lambda e: e.dma_start(
                out=k_sb[:].rearrange("p (kc f) -> p kc f", f=1024),
                in_=KF[bass.ds(kb_ * 512, 512)].rearrange("(kc p) f -> p kc f", p=128)),
                ch="bk0")
            em.w(tk, "omm3", lag=1)
            tk["vld"] = em.dma(lambda e: e.dma_start(
                out=v_sb[:].rearrange("p (tt c) -> p tt c", c=256),
                in_=VT.rearrange("(kb tt p) c -> kb p tt c", tt=16, p=128)[
                    bass.ds(kb_, 1)]), ch="bv0")
        @L.sec("tensor")
        def _(kb_, em, tk=tk):
            em.w(tk, "kld")
            em.w(tk, "vld")
            for sub in range(4):
                em.w(tk, f"exp{(sub + 3) % 4}", lag=1 if sub == 0 else 0)
                for q4 in range(4):
                    for dc in range(2):
                        mm = em.do(lambda e, sub=sub, q4=q4, dc=dc: e.matmul(
                            SBIG[:, q4 * 512:(q4 + 1) * 512],
                            q_sb[:, dc * 512 + sub * 128:dc * 512 + (sub + 1) * 128],
                            k_sb[:, q4 * 1024 + dc * 512:q4 * 1024 + (dc + 1) * 512],
                            start=(dc == 0), stop=(dc == 1)),
                            inc=(dc == 1 and q4 == 3))
                tk[f"Smm{sub}"] = mm
                em.w(tk, f"pm{sub}")
                for quarter in range(4):
                    pq = sub * 4 + quarter
                    em.w(tk, f"pte{(pq + 14) % 16}", lag=1 if pq < 2 else 0)
                    for j4 in range(4):
                        s4 = quarter * 4 + j4
                        mm = em.do(lambda e, s4=s4, pq=pq: e.matmul(
                            PT[pq % 2][:, (s4 % 4) * 128:(s4 % 4 + 1) * 128],
                            pm_sb[:, s4 * 128:(s4 + 1) * 128],
                            idb[:], start=(j4 == 0), stop=(j4 == 3),
                            is_transpose=True), inc=(j4 == 3))
                    tk[f"ptmm{pq}"] = mm
                em.w(tk, f"ost{(sub + 2) % 4}", lag=1 if sub < 2 else 0)
                for s4 in range(16):
                    if s4 % 4 == 0:
                        em.w(tk, f"pte{sub * 4 + s4 // 4}")
                    mm = em.do(lambda e, sub=sub, s4=s4: e.matmul(
                        PO[sub % 2][:, 0:256],
                        pt_sb[:, s4 * 128:(s4 + 1) * 128],
                        v_sb[:, s4 * 256:(s4 + 1) * 256],
                        start=(s4 == 0), stop=(s4 == 15)), inc=(s4 == 15))
                tk[f"omm{sub}"] = mm

        @L.sec("vector")
        def _(kb_, em, tk=tk):
            for sub in range(4):
                em.w(tk, f"Smm{sub}")
                t1 = em.do(lambda e, sub=sub: e.reduce_max(
                    col(sub, 2), SBIG[:], axis=AX, negate=True))
                em.wait(t1)
                t2 = em.do(lambda e, sub=sub: e.tensor_scalar(
                    col(sub, 2), col(sub, 2), 1.0 / 16, None, ALU.mult))
                em.wait(t2)
                t3 = em.do(lambda e, sub=sub: e.tensor_tensor(
                    col(sub, 3), col(sub, 0), col(sub, 2), ALU.min))
                em.wait(t3)
                t4 = em.do(lambda e, sub=sub: e.tensor_sub(
                    col(sub, 4), col(sub, 3), col(sub, 0)))
                em.wait(t4)
                t5 = em.do(lambda e, sub=sub: e.tensor_copy(col(sub, 0), col(sub, 3)))
                tk[f"nm{sub}"] = t5
                em.wait(tk["dl4"])
                tmg = em.do(lambda e, sub=sub: e.tensor_single_scalar(
                    msk[:], d_sb[:], dl_sb[:, sub:sub + 1], ALU.is_le))
                tk[f"mg{sub}"] = tmg
                em.wait(tmg)
                em.w(tk, f"exp{sub}")
                em.w(tk, f"ptmm{((sub + 3) % 4) * 4 + 3}", lag=1 if sub == 0 else 0)
                t7 = em.do(lambda e, sub=sub: e.scalar_tensor_tensor(
                    pm_sb[:], msk[:], 1.0, p_sb[:],
                    ALU.bypass, ALU.mult, accum_out=col(sub, 6)))
                tk[f"pm{sub}"] = t7
                em.w(tk, f"alpha{sub}")
                t8 = em.do(lambda e, sub=sub: e.scalar_tensor_tensor(
                    col(sub, 1), col(sub, 1), col(sub, 5), col(sub, 6),
                    ALU.mult, ALU.add))
                tk[f"l{sub}"] = t8
                em.w(tk, f"omm{sub}")
                t10 = em.do(lambda e, sub=sub: e.scalar_tensor_tensor(
                    o_acc[:, sub * 256:(sub + 1) * 256],
                    o_acc[:, sub * 256:(sub + 1) * 256], col(sub, 5),
                    PO[sub % 2][:, 0:256], ALU.mult, ALU.add))
                tk[f"ost{sub}"] = t10

        @L.sec("scalar")
        def _(kb_, em, tk=tk):
            for sub in range(4):
                em.w(tk, f"nm{sub}")
                em.w(tk, f"pm{(sub + 3) % 4}", lag=1 if sub == 0 else 0)
                t1 = em.do(lambda e, sub=sub: e.activation(
                    p_sb[:], SBIG[:], AF.Exp, bias=col(sub, 3), scale=1.0 / 16))
                tk[f"exp{sub}"] = t1
                t2 = em.do(lambda e, sub=sub: e.activation(
                    col(sub, 5), col(sub, 4), AF.Exp))
                tk[f"alpha{sub}"] = t2
                for quarter in range(4):
                    pq = sub * 4 + quarter
                    em.w(tk, f"ptmm{pq}")
                    if quarter == 0:
                        em.w(tk, f"omm{(sub + 3) % 4}", lag=1 if sub == 0 else 0)
                    t9 = em.do(lambda e, quarter=quarter, pq=pq: e.activation(
                        pt_sb[:, quarter * 512:(quarter + 1) * 512],
                        PT[pq % 2][:], AF.Copy))
                    tk[f"pte{pq}"] = t9

        L.close()





        v = p.straight("vector")
        sc = p.straight("scalar")
        te = p.straight("tensor")
        sy2 = p.straight("sync")
        g2 = p.straight("gpsimd")
        Ltk = tk
        for sub in range(4):
            v.wait(Tok("sVE", L.final["sVE"], 0))
            t = v.do(lambda e, sub=sub: e.reciprocal(col(sub, 7), col(sub, 1)))
            v.wait(t)
            t = v.do(lambda e, sub=sub: e.tensor_scalar(
                ob[:, sub * 256:(sub + 1) * 256],
                o_acc[:, sub * 256:(sub + 1) * 256], col(sub, 7), None, ALU.mult))
            v.wait(t)
        tob = t
        te.wait(tob)
        for sub in range(4):
            for dv in range(2):
                tt_ = te.do(lambda e, sub=sub, dv=dv: e.matmul(
                    p.ps["pt0"][:, dv * 128:(dv + 1) * 128],
                    ob[:, sub * 256 + dv * 128:sub * 256 + (dv + 1) * 128],
                    idb[:], start=(dv == 0), stop=(dv == 1), is_transpose=True),
                    inc=(dv == 1))
            v.wait(tt_)
            t = v.do(lambda e, sub=sub: e.tensor_copy(
                o_fm[:].rearrange("p (dv f) -> p dv f", dv=2)[:, :, sub * 128:(sub + 1) * 128],
                p.ps["pt0"][:].rearrange("p (dv f) -> p dv f", f=128)[:, 0:2, :]))
            v.wait(t)
            te.wait(t)
        tofm = t
        for zc in range(4):
            te.wait(tofm)
            for dv in range(2):
                mm = te.do(lambda e, zc=zc, dv=dv: e.matmul(
                    PZ[zc % 2][:], wc2lT[:, (dv * 4 + zc) * 128:(dv * 4 + zc + 1) * 128],
                    o_fm[:, dv * 512:(dv + 1) * 512],
                    start=(dv == 0), stop=(dv == 1)), inc=(dv == 1))
            v.wait(mm)
            t = v.do(lambda e, zc=zc: e.tensor_copy(
                z_sb[:, zc * 512:(zc + 1) * 512], PZ[zc % 2][:]))
            v.wait(t)
            te.wait(t)
        tz = t
        for j in range(16):
            bank = PZ[j % 2]
            for zc in range(4):
                mm = te.do(lambda e, j=j, zc=zc, bank=bank: e.matmul(
                    bank[:], wsum[:, (zc * 16 + j) * 128:(zc * 16 + j + 1) * 128],
                    z_sb[:, zc * 512:(zc + 1) * 512],
                    start=(zc == 0), stop=(zc == 3)), inc=(zc == 3))
            v.wait(mm)
            t = v.do(lambda e, j=j: e.tensor_scalar(
                att[:, j * 512:(j + 1) * 512], bank[:] if False else PZ[j % 2][:],
                bo[:, j:j + 1], None, ALU.add))
            v.wait(t)
            te.wait(t)
        tatt = t
        for sub in range(4):
            tx = sy2.dma(lambda e, qt=qt, sub=sub: e.dma_start(
                out=xtm[:], in_=XO[(qt * 512 + sub * 128) * 1:(qt * 512 + sub * 128 + 128),
                                   :]), ch="bx")
            for jg in range(4):
                bank = psS_h[:, 1024:1536] if jg % 2 == 0 else psS_h[:, 1536:2048]
                te.wait(tatt)
                te.wait(Tok("sVE", p.C["sVE"], 0))
                for jj in range(4):
                    j = jg * 4 + jj
                    mm = te.do(lambda e, j=j, jj=jj, sub=sub, bank=bank: e.matmul(
                        bank[:, jj * 128:(jj + 1) * 128],
                        att[:, j * 512 + sub * 128:j * 512 + (sub + 1) * 128],
                        idf[:], start=(jj == 0), stop=(jj == 3), is_transpose=True),
                        inc=(jj == 3))
                v.wait(mm)
                v.wait(tx)
                t = v.do(lambda e, jg=jg, bank=bank: e.scalar_tensor_tensor(
                    otm[:, jg * 512:(jg + 1) * 512], xtm[:, jg * 512:(jg + 1) * 512],
                    1.0, bank[:], ALU.bypass, ALU.add))
                v.wait(t)
            g2.wait(t)
            tst = g2.dma(lambda e, qt=qt, sub=sub: e.dma_start(
                out=OUT[qt * 512 + sub * 128:qt * 512 + sub * 128 + 128, :],
                in_=otm[:]), ch="bout")
            g2.wait(tst)
            v.wait(tst)
            sy2.wait(tst)
    return None


def emit_C(p, xno_name, out_name, TO):
    cfg = p.cfg
    NCG = TO // 1024
    XNO, OUT = p.dram[xno_name], p.dram[out_name]
    gu_t = p.dram["gu_t"]
    d_t = p.dram["d_t"]
    shg_t = p.dram["shg_t"]
    shu_t = p.dram["shu_t"]
    shd_t = p.dram["shd_t"]
    rt_t = p.dram["rt_t"]
    oh_t = p.dram["oh_t"]
    idf = p.sb["idf"]

    def get(name, shape, dt):
        return p.sb.get(name) or p.sbuf(name, shape, dt)
    acc = get("c_acc", [128, 16 * 1024], F32)
    xnf = get("c_xnf", [128, 16 * 1024], BF16)
    ch_ = get("c_h", [128, 16 * 1024], BF16)
    gu_s = [get(f"c_gu{s}", [128, 32 * 128], BF16) for s in range(2)]
    dd_s = [get(f"c_dd{s}", [128, 16 * 128], BF16) for s in range(2)]
    sh_s = [gu_s[0][:, 0:2048], gu_s[1][:, 0:2048],
            gu_s[0][:, 2048:4096], gu_s[1][:, 2048:4096]]
    sd_s = dd_s
    be = get("c_be", [128, 2 * 1024], BF16)
    oh = get("c_oh", [16, 2 * 128], F32)
    rt = get("c_rt", [128, 16 * 16], BF16)
    lgf = get("c_lgf", [16, 1024], F32)
    lgt = get("c_lgt", [128, 8 * 16], F32)
    wtm = get("c_wtm", [128, 16], F32)
    wf = get("c_wf", [16, 1024], F32)
    rst = get("c_rst", [128, 12], F32)
    tS = get("c_tS", [128, 512], F32)
    tT = get("c_tT", [128, 512], F32)
    tU = get("c_tU", [128, 512], F32)
    otm = get("c_otm", [128, 512], F32)
    PA, PB, PC_, PD, PE_, PF_ = (p.ps["pf0"], p.ps["pf1"], p.ps["pf2"],
                                 p.ps["pf3"], p.ps["pf4"], p.ps["pf5"])

    sy0 = p.straight("gpsimd")
    trt = sy0.dma(lambda e: e.dma_start(
        out=rt[:].rearrange("p (c f) -> p c f", f=16),
        in_=rt_t.rearrange("c p f -> p c f")), ch="crt")

    for g in range(NCG):
        t0 = g * 1024
        syg = p.straight("sync")
        syg.wait_abs("sPE", p.C["sPE"])
        xnf_tok = {}
        for dc in range(16):
            xnf_tok[dc] = syg.dma(lambda e, t0=t0, dc=dc: e.dma_start(
                out=xnf[:, dc * 1024:(dc + 1) * 1024],
                in_=XNO[t0:t0 + 1024, dc * 128:(dc + 1) * 128],
                transpose=True), ch=f"xn{dc}")

        te = p.straight("tensor")
        v = p.straight("vector")
        sc = p.straight("scalar")
        te.wait(trt)
        te.wait_abs("sVE", p.C["sVE"])
        for u in range(2):
            for dc in range(16):
                te.wait(xnf_tok[dc])
                mm = te.do(lambda e, u=u, dc=dc: e.matmul(
                    PA[0:16, :], rt[:, dc * 16:(dc + 1) * 16],
                    xnf[:, dc * 1024 + u * 512:dc * 1024 + (u + 1) * 512],
                    start=(dc == 0), stop=(dc == 15)), inc=(dc == 15))
            v.wait(mm)
            tcp = v.do(lambda e, u=u: e.tensor_copy(lgf[:, u * 512:(u + 1) * 512],
                                                    PA[0:16, :]))
            v.wait(tcp)
            te.wait(tcp)
        for sub in range(8):
            tmm = te.do(lambda e, sub=sub: e.matmul(
                PB[0:128, 0:16], lgf[:, sub * 128:(sub + 1) * 128], idf[0:16, 0:16],
                start=True, stop=True, is_transpose=True))
            v.wait(tmm)
            tcp = v.do(lambda e, sub=sub: e.tensor_copy(
                lgt[:, sub * 16:(sub + 1) * 16], PB[0:128, 0:16]))
            v.wait(tcp)
            te.wait(tcp)
        for sub in range(8):
            sl = lgt[:, sub * 16:(sub + 1) * 16]
            for fn in (
                lambda e, sl=sl: e.reduce_max(rst[:, 0:1], sl, axis=AX),
                lambda e, sl=sl: e.tensor_single_scalar(wtm[:], sl, rst[:, 0:1], ALU.is_ge),
                lambda e, sl=sl: e.scalar_tensor_tensor(
                    tS[:, 0:16], wtm[:], -1.0e30, sl, ALU.mult, ALU.add),
                lambda e: e.reduce_max(rst[:, 1:2], tS[:, 0:16], axis=AX),
                lambda e: e.tensor_single_scalar(tS[:, 16:32], tS[:, 0:16],
                                                 rst[:, 1:2], ALU.is_ge),
                lambda e: e.tensor_sub(rst[:, 2:3], rst[:, 0:1], rst[:, 1:2]),
            ):
                tlast = v.do(fn)
                v.wait(tlast)
            sc.wait(tlast)
            t7 = sc.do(lambda e: e.activation(rst[:, 3:4], rst[:, 2:3], AF.Tanh,
                                              scale=0.5))
            sc.wait(t7)
            v.wait(t7)
            for fn in (
                lambda e: e.tensor_scalar(rst[:, 5:6], rst[:, 3:4], 0.5, 0.5,
                                          ALU.mult, ALU.add),
                lambda e: e.tensor_scalar(rst[:, 6:7], rst[:, 5:6], -1.0, 1.0,
                                          ALU.mult, ALU.add),
                lambda e: e.tensor_scalar(wtm[:], wtm[:], rst[:, 5:6], None, ALU.mult),
                lambda e: e.scalar_tensor_tensor(
                    wtm[:], tS[:, 16:32], rst[:, 6:7], wtm[:], ALU.mult, ALU.add),
            ):
                tlast = v.do(fn)
                v.wait(tlast)
            te.wait(tlast)
            tmm = te.do(lambda e: e.matmul(PB[0:16, 0:128], wtm[:], idf[:],
                                           start=True, stop=True, is_transpose=True))
            v.wait(tmm)
            tcp = v.do(lambda e, sub=sub: e.tensor_copy(
                wf[:, sub * 128:(sub + 1) * 128], PB[0:16, 0:128]))
            v.wait(tcp)
            te.wait(tcp)
        rtr_pe_count = p.C["sPE"]
        rtr_ve_count = p.C["sVE"]

        pe_gate_h = p.C["sPE"]
        Lm = Loop(p, 8, f"Csm{g}")
        km = {}

        @Lm.sec("gpsimd")
        def _(mh, em, km=km, pe_gate_h=pe_gate_h):
            em.wait_abs("sPE", pe_gate_h)
            for s in range(2):
                em.w(km, f"hmul{s}", lag=1)
                km[f"wg{s}"] = em.dma(lambda e, s=s: e.dma_start(
                    out=sh_s[s].rearrange("p (n f) -> p n f", f=128),
                    in_=shg_t[bass.ds(mh * 2 + s, 1)].rearrange(
                        "a c p f -> p (a c) f")), ch=f"cgu{s}")
                em.w(km, f"hmul{s}", lag=1)
                km[f"wu{s}"] = em.dma(lambda e, s=s: e.dma_start(
                    out=sh_s[2 + s].rearrange("p (n f) -> p n f", f=128),
                    in_=shu_t[bass.ds(mh * 2 + s, 1)].rearrange(
                        "a c p f -> p (a c) f")), ch=f"cshu{s}")

        @Lm.sec("tensor")
        def _(mh, em, km=km, rtr_ve_count=rtr_ve_count):
            em.wait_abs("sVE", rtr_ve_count)
            for s in range(2):
                for u in range(2):
                    k4 = s * 2 + u
                    ps_, pu_ = divmod((k4 + 3) % 4, 2)
                    em.w(km, f"silu{ps_}{pu_}", lag=1 if k4 == 0 else 0)
                    em.w(km, f"wg{s}")
                    for dc in range(16):
                        mm = em.do(lambda e, s=s, u=u, dc=dc: e.matmul(
                            PA[:], sh_s[s][:, dc * 128:(dc + 1) * 128],
                            xnf[:, dc * 1024 + u * 512:dc * 1024 + (u + 1) * 512],
                            start=(dc == 0), stop=(dc == 15)), inc=(dc == 15))
                    km[f"gmm{s}{u}"] = mm
                    em.w(km, f"tmul{ps_}{pu_}", lag=1 if k4 == 0 else 0)
                    em.w(km, f"wu{s}")
                    for dc in range(16):
                        mm = em.do(lambda e, s=s, u=u, dc=dc: e.matmul(
                            PB[:], sh_s[2 + s][:, dc * 128:(dc + 1) * 128],
                            xnf[:, dc * 1024 + u * 512:dc * 1024 + (u + 1) * 512],
                            start=(dc == 0), stop=(dc == 15)), inc=(dc == 15))
                    km[f"umm{s}{u}"] = mm

        @Lm.sec("scalar")
        def _(mh, em, km=km):
            for s in range(2):
                for u in range(2):
                    em.w(km, f"tmul{(s + 1) % 2}{u}", lag=1 if s == 0 else 0)
                    em.w(km, f"gmm{s}{u}")
                    km[f"silu{s}{u}"] = em.do(lambda e, s=s, u=u: e.activation(
                        tS[:] if u == 0 else tT[:], PA[:], AF.Silu))

        @Lm.sec("vector")
        def _(mh, em, km=km, pe_gate_h=pe_gate_h):
            for s in range(2):
                for u in range(2):
                    em.w(km, f"silu{s}{u}")
                    em.w(km, f"umm{s}{u}")
                    em.wait_abs("sPE", pe_gate_h)
                    km[f"tmul{s}{u}"] = em.do(lambda e, s=s, u=u: e.scalar_tensor_tensor(
                        ch_[:, bass.ds((0 if True else 0) + 0, 1)][0:0] if False else
                        ch_[:, bass.ds((2 * 0 + 0) * 0, 512)] if False else
                        ch_[:, bass.ds(((mh * 2 + s) * 1024 + u * 512), 512)],
                        (tS if u == 0 else tT)[:], 1.0, PB[:],
                        ALU.bypass, ALU.mult))
                km[f"hmul{s}"] = km[f"tmul{s}1"]

        Lm.close()

        Lj = Loop(p, 8, f"Csj{g}")
        kj = {}

        @Lj.sec("gpsimd")
        def _(jh, em, kj=kj):
            for s in range(2):
                em.w(kj, f"dnev{s}1", lag=1)
                kj[f"wd{s}"] = em.dma(lambda e, s=s: e.dma_start(
                    out=sd_s[s][:].rearrange("p (n f) -> p n f", f=128),
                    in_=shd_t[bass.ds(jh * 2 + s, 1)].rearrange(
                        "a c p f -> p (a c) f")), ch=f"cdd{s}")

        @Lj.sec("tensor")
        def _(jh, em, kj=kj):
            for s in range(2):
                bank = PC_ if s == 0 else PD
                for u in range(2):
                    if u == 1:
                        em.w(kj, f"dnev{s}0")
                    else:
                        em.w(kj, f"dnev{s}1", lag=1)
                    em.w(kj, f"wd{s}")
                    for m in range(16):
                        mm = em.do(lambda e, s=s, u=u, m=m, bank=bank: e.matmul(
                            bank[:], sd_s[s][:, m * 128:(m + 1) * 128],
                            ch_[:, m * 1024 + u * 512:m * 1024 + (u + 1) * 512],
                            start=(m == 0), stop=(m == 15)), inc=(m == 15))
                    kj[f"dnmm{s}{u}"] = mm

        @Lj.sec("vector")
        def _(jh, em, kj=kj):
            for s in range(2):
                bank = PC_ if s == 0 else PD
                for u in range(2):
                    em.w(kj, f"dnmm{s}{u}")
                    kj[f"dnev{s}{u}"] = em.do(lambda e, s=s, u=u, bank=bank: e.tensor_copy(
                        acc[:, bass.ds((jh * 2 + s) * 1024 + u * 512, 512)], bank[:]))

        Lj.close()

        Lp = Loop(p, 8, f"Cp{g}")
        kp = {}
        shj_pe_final = Lj.final["sPE"]

        @Lp.sec("sync")
        def _(p8, em, kp=kp):
            for mem in range(2):
                em.w(kp, f"beev{mem}", lag=1)
                kp[f"ohld{mem}"] = em.dma(lambda e, mem=mem: e.dma_start(
                    out=oh[:, mem * 128:(mem + 1) * 128],
                    in_=oh_t[bass.ds(p8 * 2 + mem, 1)]), ch=f"coh{mem}")

        @Lp.sec("gpsimd")
        def _(p8, em, kp=kp, shj_pe_final=shj_pe_final):
            em.wait_abs("sPE", shj_pe_final)
            for st in range(16):
                mem, m = divmod(st, 8)
                sl = st % 2
                em.w(kp, f"humm{(st + 14) % 16}", lag=1 if st < 2 else 0)
                kp[f"guld{st}"] = em.dma(lambda e, st=st, sl=sl: e.dma_start(
                    out=gu_s[sl][:].rearrange("p (n f) -> p n f", f=128),
                    in_=gu_t.rearrange("e m a c p f -> (e m) a c p f")[
                        bass.ds(p8 * 16 + st, 1)].rearrange(
                        "x a c p f -> p (x a c) f")), ch=f"cgu{sl}")
            for j in range(16):
                sl = j % 2
                em.w(kp, f"dnmm{(j + 14) % 16}", lag=1 if j < 2 else 0)
                kp[f"dld{j}"] = em.dma(lambda e, j=j, sl=sl: e.dma_start(
                    out=dd_s[sl][:].rearrange("p (n f) -> p n f", f=128),
                    in_=d_t[bass.ds(p8, 1), j].rearrange(
                        "x a b p f -> p (x a b) f")), ch=f"cdd{sl}")

        @Lp.sec("tensor")
        def _(p8, em, kp=kp, shj_pe_final=shj_pe_final):
            em.wait_abs("sPE", shj_pe_final)
            for mem in range(2):
                em.w(kp, f"ohld{mem}")
                for u in range(2):
                    em.w(kp, f"beevu{mem}{u}", lag=1)
                    if u == 1:
                        em.w(kp, f"beevu{mem}0")
                    elif mem == 1:
                        em.w(kp, "beevu01" if False else "beevu0" + "1")
                    mm = em.do(lambda e, mem=mem, u=u: e.matmul(
                        PE_[:], oh[:, mem * 128:(mem + 1) * 128],
                        wf[:, u * 512:(u + 1) * 512],
                        start=True, stop=True), inc=True)
                    kp[f"bemm{mem}{u}"] = mm
            for st in range(16):
                mem, m = divmod(st, 8)
                sl = st % 2
                em.w(kp, f"guld{st}")
                for u in range(2):
                    if u == 1:
                        em.w(kp, f"esilu{st}0")
                    else:
                        em.w(kp, f"esilu{(st + 15) % 16}1", lag=1 if st < 1 else 0)
                    for dc in range(16):
                        mm = em.do(lambda e, sl=sl, u=u, dc=dc: e.matmul(
                            PA[:], gu_s[sl][:, dc * 128:(dc + 1) * 128],
                            xnf[:, dc * 1024 + u * 512:dc * 1024 + (u + 1) * 512],
                            start=(dc == 0), stop=(dc == 15)), inc=(dc == 15))
                    kp[f"egmm{st}{u}"] = mm
                    if u == 1:
                        em.w(kp, f"etmul{st}0")
                    else:
                        em.w(kp, f"etmul{(st + 15) % 16}1", lag=1 if st < 1 else 0)
                    for dc in range(16):
                        mm = em.do(lambda e, sl=sl, u=u, dc=dc: e.matmul(
                            PB[:], gu_s[sl][:, (16 + dc) * 128:(17 + dc) * 128],
                            xnf[:, dc * 1024 + u * 512:dc * 1024 + (u + 1) * 512],
                            start=(dc == 0), stop=(dc == 15)), inc=(dc == 15))
                    kp[f"eumm{st}{u}"] = mm
                kp[f"humm{st}"] = kp[f"eumm{st}1"]
            for j in range(16):
                sl = j % 2
                bank = PC_ if sl == 0 else PD
                em.w(kp, f"dld{j}")
                for u in range(2):
                    if u == 1:
                        em.w(kp, f"dnev{j}0")
                    else:
                        em.w(kp, f"dnev{(j + 14) % 16}1", lag=1 if j < 2 else 0)
                    for st in range(16):
                        em.w(kp, f"ehmul{st}{u}") if j == 0 else None
                        mm = em.do(lambda e, sl=sl, u=u, st=st, bank=bank: e.matmul(
                            bank[:], dd_s[sl][:, st * 128:(st + 1) * 128],
                            ch_[:, st * 1024 + u * 512:st * 1024 + (u + 1) * 512],
                            start=(st == 0), stop=(st == 15)), inc=(st == 15))
                    kp[f"ednmm{j}{u}"] = mm
                kp[f"dnmm{j}"] = kp[f"ednmm{j}1"]

        @Lp.sec("scalar")
        def _(p8, em, kp=kp):
            for st in range(16):
                for u in range(2):
                    em.w(kp, f"egmm{st}{u}")
                    em.w(kp, f"etmul{(st + 15) % 16}{u}", lag=1 if st < 1 else 0)
                    kp[f"esilu{st}{u}"] = em.do(lambda e, u=u: e.activation(
                        tS[:] if u == 0 else tT[:], PA[:], AF.Silu))

        @Lp.sec("vector")
        def _(p8, em, kp=kp):
            for mem in range(2):
                for u in range(2):
                    em.w(kp, f"bemm{mem}{u}")
                    tb = em.do(lambda e, mem=mem, u=u: e.tensor_copy(
                        be[:, mem * 1024 + u * 512:mem * 1024 + (u + 1) * 512],
                        PE_[:]))
                    kp[f"beevu{mem}{u}"] = tb
                kp[f"beev{mem}"] = tb
            for st in range(16):
                for u in range(2):
                    em.w(kp, f"esilu{st}{u}")
                    em.w(kp, f"eumm{st}{u}")
                    kp[f"etmul{st}{u}"] = em.do(lambda e, u=u: e.scalar_tensor_tensor(
                        tU[:], (tS if u == 0 else tT)[:], 1.0, PB[:],
                        ALU.bypass, ALU.mult))
                    em.w(kp, "dnmm15", lag=1)
                    mem = st // 8
                    kp[f"ehmul{st}{u}"] = em.do(lambda e, st=st, u=u, mem=mem: e.tensor_mul(
                        ch_[:, st * 1024 + u * 512:st * 1024 + (u + 1) * 512],
                        tU[:], be[:, mem * 1024 + u * 512:mem * 1024 + (u + 1) * 512]))
            for j in range(16):
                bank = PC_ if j % 2 == 0 else PD
                for u in range(2):
                    em.w(kp, f"ednmm{j}{u}")
                    kp[f"dnev{j}{u}"] = em.do(lambda e, j=j, u=u, bank=bank: e.scalar_tensor_tensor(
                        acc[:, bass.ds(j * 1024 + u * 512, 512)],
                        acc[:, bass.ds(j * 1024 + u * 512, 512)], 1.0, bank[:],
                        ALU.bypass, ALU.add))

        Lp.close()

        acc_done_ve = p.C["sVE"]
        te2 = p.straight("tensor")
        v2 = p.straight("vector")
        g2 = p.straight("gpsimd")
        te2.wait_abs("sVE", acc_done_ve)
        prev_store = None
        for sub in range(8):
            for jg in range(4):
                bank = PE_ if (sub * 4 + jg) % 2 == 0 else PF_
                for jj in range(4):
                    j = jg * 4 + jj
                    mm = te2.do(lambda e, j=j, jj=jj, sub=sub, bank=bank: e.matmul(
                        bank[:, jj * 128:(jj + 1) * 128],
                        acc[:, j * 1024 + sub * 128:j * 1024 + (sub + 1) * 128],
                        idf[:], start=(jj == 0), stop=(jj == 3), is_transpose=True),
                        inc=(jj == 3))
                v2.wait(mm)
                if prev_store is not None:
                    v2.wait(prev_store)
                tev = v2.do(lambda e, bank=bank: e.tensor_copy(otm[:], bank[:]))
                v2.wait(tev)
                te2.wait(tev)
                g2.wait(tev)
                prev_store = g2.dma(lambda e, t0=t0, sub=sub, jg=jg: e.dma_start(
                    out=OUT[t0 + sub * 128:t0 + sub * 128 + 128,
                            jg * 512:(jg + 1) * 512],
                    in_=otm[:], accum_op=ALU.add), ch="cacc")
        g2.wait(prev_store)
    return None


import concourse.bass_core_rust as _bcr
IOX = _bcr.IndirectOffsetOnAxis
I32 = mybir.dt.int32


def emit_C2(p, xno_name, out_name, TO):
    import os as _os
    _stop = _os.environ.get("C2STOP", "full")
    cfg = p.cfg
    NPAIR = TO // 2048
    CAP = 512
    NSLOT = 16 * CAP
    PARK = NPAIR * NSLOT
    XNO, OUT = p.dram[xno_name], p.dram[out_name]
    XS, DN = p.dram["XS"], p.dram["DN"]
    gu_t, d2_t = p.dram["gu_t"], p.dram["d2_t"]
    shg_t, shu_t, shd_t = p.dram["shg_t"], p.dram["shu_t"], p.dram["shd_t"]
    rt_t = p.dram["rt_t"]
    idf, idb = p.sb["idf"], p.sb["idb"]

    def get(name, shape, dt):
        return p.sb.get(name) or p.sbuf(name, shape, dt)
    acc = get("c_acc", [128, 16 * 1024], F32)
    xnf = get("c_xnf", [128, 16 * 1024], BF16)
    cbig = get("c_big", [128, 16 * 1024], BF16)
    gu_s = [get(f"c_gu{s}", [128, 32 * 128], BF16) for s in range(2)]
    dd_s = [get(f"c_dd{s}", [128, 16 * 128], BF16) for s in range(2)]
    sh_s = [gu_s[0][:, 0:2048], gu_s[1][:, 0:2048],
            gu_s[0][:, 2048:4096], gu_s[1][:, 2048:4096]]
    sd_s = dd_s
    che = get("c_he", [128, 8 * 512], BF16)
    dntm = get("c_dntm", [128, 2 * 2048], BF16)
    otm = get("c_otm", [128, 2 * 512], F32)
    tS = get("c_tS", [128, 512], F32)
    tT = get("c_tT", [128, 512], F32)
    rt = get("c_rt", [128, 16 * 16], BF16)
    lgf = get("c_lgf", [16, 1024], F32)
    lgt = get("c_lgt", [128, 8 * 16], F32)
    e1t = get("c_e1t", [128, 16 * 16], F32)
    e2t = get("c_e2t", [128, 16 * 16], F32)
    wcol = get("c_wcol", [128, 32], F32)
    sbtm = get("c_sbtm", [128, 16], F32)
    scol = get("c_scol", [128, 4], F32)
    sidx = get("c_sidx", [128, 32], I32)
    ecap = get("c_ecap", [16, 1], F32)
    rst = get("c_rst", [128, 12], F32)
    zro = acc[0:16, 0:2048]
    afm = acc[0:16, 2048:4096]
    xsf = cbig
    g1v = cbig[:, 8192:10240]
    g2v = cbig[:, 10240:12288]
    xrow = cbig[:, 12288:14336]
    PA, PB = p.ps["pf0"], p.ps["pf1"]
    PC_, PD = p.ps["pf2"], p.ps["pf3"]
    PE_, PF_ = p.ps["pf4"], p.ps["pf5"]
    PT = [p.ps["pt0"], p.ps["pt1"]]

    p.sem("qG_cg1")
    p.sem("qG_cg2")
    sy0 = p.straight("gpsimd")
    trt = sy0.dma(lambda e: e.dma_start(
        out=rt[:].rearrange("p (c f) -> p c f", f=16),
        in_=rt_t.rearrange("c p f -> p c f")), ch="crt")
    tec = sy0.dma(lambda e: e.dma_start(out=ecap[:], in_=p.dram["ecap_h"]),
                  ch="cxr")

    for gp in range(NPAIR):
        if gp and _os.environ.get("C2BAR"):
            p.barrier()
        t0p = gp * 2048
        sbase = gp * NSLOT
        for gh in range(2):
            t0 = t0p + gh * 1024
            syg = p.straight("sync")
            syg.wait_abs("sPE", p.C["sPE"])
            xnf_tok = {}
            for dc in range(16):
                xnf_tok[dc] = syg.dma(lambda e, t0=t0, dc=dc: e.dma_start(
                    out=xnf[:, dc * 1024:(dc + 1) * 1024],
                    in_=XNO[t0:t0 + 1024, dc * 128:(dc + 1) * 128],
                    transpose=True), ch=f"xn{dc}")
            te = p.straight("tensor")
            v = p.straight("vector")
            sc = p.straight("scalar")
            te.wait(trt)
            te.wait_abs("sVE", p.C["sVE"])
            v.wait_abs("sPE", p.C["sPE"])
            for u in range(2):
                for dc in range(16):
                    te.wait(xnf_tok[dc])
                    mm = te.do(lambda e, u=u, dc=dc: e.matmul(
                        PA[0:16, :], rt[:, dc * 16:(dc + 1) * 16],
                        xnf[:, dc * 1024 + u * 512:dc * 1024 + (u + 1) * 512],
                        start=(dc == 0), stop=(dc == 15)), inc=(dc == 15))
                v.wait(mm)
                tcp = v.do(lambda e, u=u: e.tensor_copy(lgf[:, u * 512:(u + 1) * 512],
                                                        PA[0:16, :]))
                v.wait(tcp)
                te.wait(tcp)
            for sub in range(8):
                tmm = te.do(lambda e, sub=sub: e.matmul(
                    PB[0:128, 0:16], lgf[:, sub * 128:(sub + 1) * 128], idf[0:16, 0:16],
                    start=True, stop=True, is_transpose=True))
                v.wait(tmm)
                tcp = v.do(lambda e, sub=sub: e.tensor_copy(
                    lgt[:, sub * 16:(sub + 1) * 16], PB[0:128, 0:16]))
                v.wait(tcp)
                te.wait(tcp)
            for sub in range(8):
                s16 = gh * 8 + sub
                sl = lgt[:, sub * 16:(sub + 1) * 16]
                for fn in (
                    lambda e, sl=sl: e.reduce_max(rst[:, 0:1], sl, axis=AX),
                    lambda e, sl=sl, s16=s16: e.tensor_single_scalar(
                        e1t[:, s16 * 16:(s16 + 1) * 16], sl, rst[:, 0:1], ALU.is_ge),
                    lambda e, sl=sl, s16=s16: e.scalar_tensor_tensor(
                        tS[:, 0:16], e1t[:, s16 * 16:(s16 + 1) * 16], -1.0e30, sl,
                        ALU.mult, ALU.add),
                    lambda e: e.reduce_max(rst[:, 1:2], tS[:, 0:16], axis=AX),
                    lambda e, s16=s16: e.tensor_single_scalar(
                        e2t[:, s16 * 16:(s16 + 1) * 16], tS[:, 0:16],
                        rst[:, 1:2], ALU.is_ge),
                    lambda e: e.tensor_sub(rst[:, 2:3], rst[:, 0:1], rst[:, 1:2]),
                ):
                    tlast = v.do(fn)
                    v.wait(tlast)
                sc.wait(tlast)
                t7 = sc.do(lambda e: e.activation(rst[:, 3:4], rst[:, 2:3], AF.Tanh,
                                                  scale=0.5))
                sc.wait(t7)
                v.wait(t7)
                for fn in (
                    lambda e, s16=s16: e.tensor_scalar(
                        wcol[:, s16:s16 + 1], rst[:, 3:4], 0.5, 0.5, ALU.mult, ALU.add),
                    lambda e, s16=s16: e.tensor_scalar(
                        wcol[:, 16 + s16:17 + s16], wcol[:, s16:s16 + 1], -1.0, 1.0,
                        ALU.mult, ALU.add),
                    lambda e, s16=s16: e.tensor_add(
                        tS[:, 16:32], e1t[:, s16 * 16:(s16 + 1) * 16],
                        e2t[:, s16 * 16:(s16 + 1) * 16]),
                ):
                    tlast = v.do(fn)
                    v.wait(tlast)
                te.wait(tlast)
                tmm = te.do(lambda e: e.matmul(PB[0:16, 0:128], tS[:, 16:32], idf[:],
                                               start=True, stop=True, is_transpose=True))
                v.wait(tmm)
                tcp = v.do(lambda e, s16=s16: e.tensor_copy(
                    afm[:, s16 * 128:(s16 + 1) * 128], PB[0:16, 0:128]))
                v.wait(tcp)
                te.wait(tcp)

        if _stop == "r":
            continue
        v2 = p.straight("vector")
        te2 = p.straight("tensor")
        g2s = p.straight("gpsimd")
        v2.wait_abs("sPE", p.C["sPE"])
        v2.wait_abs("qG_cg1", p.C.get("qG_cg1", 0))
        v2.wait_abs("qG_cg2", p.C.get("qG_cg2", 0))
        cum = acc[0:16, 4096:6144]
        tz = v2.do(lambda e: e.memset(acc[0:16, 0:2048], 0.0))
        v2.wait(tz)
        tsc = v2.do(lambda e, cum=cum: e.tensor_tensor_scan(
            cum, afm, zro, 0.0, ALU.add, ALU.add))
        v2.wait(tsc)
        tex = v2.do(lambda e, cum=cum: e.tensor_sub(cum, cum, afm))
        v2.wait(tex)
        v2.wait(tec)
        tsb = v2.do(lambda e, cum=cum: e.scalar_tensor_tensor(
            cum, cum, ecap[:], cum, ALU.add, ALU.bypass))
        v2.wait(tsb)
        tcl = v2.do(lambda e, cum=cum, sbase=sbase: e.tensor_scalar(
            cum, cum, float(sbase), float(PARK + 127), ALU.add, ALU.min))
        v2.wait(tcl)
        te2.wait(tcl)
        prev_sc = None
        for s16 in range(16):
            tmm = te2.do(lambda e, s16=s16: e.matmul(
                PB[0:128, 0:16], acc[0:16, 4096 + s16 * 128:4096 + (s16 + 1) * 128],
                idf[0:16, 0:16], start=True, stop=True, is_transpose=True))
            v2.wait(tmm)
            tsb2 = v2.do(lambda e: e.tensor_copy(sbtm[:], PB[0:128, 0:16]))
            v2.wait(tsb2)
            te2.wait(tsb2)
            t4 = None
            for k2, et in ((0, e1t), (1, e2t)):
                t1 = v2.do(lambda e, s16=s16, et=et: e.tensor_mul(
                    lgt[:, 0:16], et[:, s16 * 16:(s16 + 1) * 16], sbtm[:]))
                v2.wait(t1)
                t2 = v2.do(lambda e, k2=k2: e.reduce_sum(
                    scol[:, k2:k2 + 1], lgt[:, 0:16], axis=AX))
                v2.wait(t2)
                t4 = v2.do(lambda e, s16=s16, k2=k2: e.tensor_copy(
                    sidx[:, k2 * 16 + s16:k2 * 16 + s16 + 1], scol[:, k2:k2 + 1]))
                v2.wait(t4)
            g2s.wait(t4)
            if prev_sc is not None:
                g2s.wait(prev_sc)
            txr = g2s.dma(lambda e, s16=s16, t0p=t0p: e.dma_start(
                out=xrow, in_=XNO[t0p + s16 * 128:t0p + (s16 + 1) * 128, :]),
                ch="cxr")
            g2s.wait(txr)
            s1 = g2s.dma(lambda e, s16=s16: e.indirect_dma_start(
                XS, IOX(sidx[:, s16:s16 + 1], 0), xrow, None), ch="csc")
            g2s.wait(s1)
            prev_sc = g2s.dma(lambda e, s16=s16: e.indirect_dma_start(
                XS, IOX(sidx[:, 16 + s16:17 + s16], 0), xrow, None), ch="csc")
        g2s.wait(prev_sc)
        if gp == 0 and _os.environ.get("C2DBG"):
            d1 = g2s.dma(lambda e: e.dma_start(out=p.dram["DBGI"], in_=sidx[:]),
                         ch="cxr")
            d2 = g2s.dma(lambda e: e.dma_start(out=p.dram["DBGC"],
                                               in_=acc[0:16, 0:6144]), ch="cxr")
            g2s.wait(d2)

        if _stop == "slot":
            continue
        slot_pe = p.C["sPE"]
        slot_ve = p.C["sVE"]
        csc_final = p.C["qG_csc"]
        Le = Loop(p, 16, f"Ce{gp}")
        ke = {}

        @Le.sec("sync")
        def _(ei, em, ke=ke, slot_pe=slot_pe, csc_final=csc_final, sbase=sbase):
            em.wait_abs("sPE", slot_pe)
            em.wait_abs("qG_csc", csc_final)
            em.w(ke, "umm7", lag=1)
            for dc in range(16):
                ke[f"xsT{dc}"] = em.dma(lambda e, ei=ei, dc=dc, sbase=sbase: e.dma_start(
                    out=cbig[:, dc * 512:(dc + 1) * 512],
                    in_=XS[bass.ds(sbase + ei * 512, 512), dc * 128:(dc + 1) * 128],
                    transpose=True), ch=f"xn{dc}")

        @Le.sec("gpsimd")
        def _(ei, em, ke=ke, slot_pe=slot_pe):
            em.wait_abs("sPE", slot_pe)
            for m in range(8):
                em.w(ke, f"umm{(m + 6) % 8}", lag=1 if m < 2 else 0)
                ke[f"guld{m}"] = em.dma(lambda e, ei=ei, m=m: e.dma_start(
                    out=gu_s[m % 2][:].rearrange("p (n f) -> p n f", f=128),
                    in_=gu_t[bass.ds(ei, 1), m].rearrange("x a c p f -> p (x a c) f")),
                    ch=f"cgu{m % 2}")
            for j in range(16):
                em.w(ke, f"dnmm{(j + 14) % 16}", lag=1 if j < 2 else 0)
                ke[f"dld{j}"] = em.dma(lambda e, ei=ei, j=j: e.dma_start(
                    out=dd_s[j % 2][:, 0:1024].rearrange("p (n f) -> p n f", f=128),
                    in_=d2_t[bass.ds(ei, 1), j].rearrange("x a p f -> p (x a) f")),
                    ch=f"cdd{j % 2}")

        @Le.sec("tensor")
        def _(ei, em, ke=ke, slot_ve=slot_ve):
            em.wait_abs("sVE", slot_ve)
            for m in range(8):
                em.w(ke, f"guld{m}")
                if m == 0:
                    for dc in range(16):
                        em.w(ke, f"xsT{dc}")
                em.w(ke, f"silu{(m + 7) % 8}", lag=1 if m == 0 else 0)
                for dc in range(16):
                    mm = em.do(lambda e, m=m, dc=dc: e.matmul(
                        PA[:], gu_s[m % 2][:, dc * 128:(dc + 1) * 128],
                        cbig[:, dc * 512:(dc + 1) * 512],
                        start=(dc == 0), stop=(dc == 15)), inc=(dc == 15))
                ke[f"gumm{m}"] = mm
                em.w(ke, f"hstt{(m + 7) % 8}", lag=1 if m == 0 else 0)
                for dc in range(16):
                    mm = em.do(lambda e, m=m, dc=dc: e.matmul(
                        PB[:], gu_s[m % 2][:, (16 + dc) * 128:(17 + dc) * 128],
                        cbig[:, dc * 512:(dc + 1) * 512],
                        start=(dc == 0), stop=(dc == 15)), inc=(dc == 15))
                ke[f"umm{m}"] = mm
            for j in range(16):
                em.w(ke, f"dld{j}")
                if j == 0:
                    for m in range(8):
                        em.w(ke, f"hstt{m}")
                em.w(ke, f"dnev{(j + 14) % 16}", lag=1 if j < 2 else 0)
                bank = PC_ if j % 2 == 0 else PD
                for m in range(8):
                    mm = em.do(lambda e, j=j, m=m, bank=bank: e.matmul(
                        bank[:], dd_s[j % 2][:, m * 128:(m + 1) * 128],
                        che[:, m * 512:(m + 1) * 512],
                        start=(m == 0), stop=(m == 7)), inc=(m == 7))
                ke[f"dnmm{j}"] = mm
            for k in range(4):
                for jg in range(4):
                    q = k * 4 + jg
                    em.w(ke, f"dtev{(q + 14) % 16}", lag=1 if q < 2 else 0)
                    for jj in range(4):
                        j = jg * 4 + jj
                        if k == 0:
                            em.w(ke, f"dnev{j}")
                        mm = em.do(lambda e, j=j, jj=jj, k=k, q=q: e.matmul(
                            PT[q % 2][:, jj * 128:(jj + 1) * 128],
                            cbig[:, 8192 + j * 512 + k * 128:8192 + j * 512 + (k + 1) * 128],
                            idb[:], start=(jj == 0), stop=(jj == 3),
                            is_transpose=True), inc=(jj == 3))
                    ke[f"trq{q}"] = mm
            ke["trlast"] = ke["trq15"]

        @Le.sec("scalar")
        def _(ei, em, ke=ke, sbase=sbase):
            for m in range(8):
                em.w(ke, f"hstt{(m + 7) % 8}", lag=1 if m == 0 else 0)
                em.w(ke, f"gumm{m}")
                ke[f"silu{m}"] = em.do(lambda e: e.activation(tS[:], PA[:], AF.Silu))
            for k in range(4):
                for jg in range(4):
                    em.w(ke, f"dtev{k * 4 + jg}")
                if k >= 2:
                    em.w(ke, f"dnst{k - 2}")
                else:
                    em.wait_prev_iters(ke.get(f"dnst{k + 2}"))
                ke[f"dnst{k}"] = em.dma(lambda e, ei=ei, k=k, sbase=sbase: e.dma_start(
                    out=DN[bass.ds(sbase + ei * 512 + k * 128, 128), :],
                    in_=dntm[:, (k % 2) * 2048:(k % 2 + 1) * 2048]),
                    ch=f"dnst{k % 2}")

        @Le.sec("vector")
        def _(ei, em, ke=ke, csc_final=csc_final):
            em.w(ke, "dnmm15", lag=1)
            for m in range(8):
                em.w(ke, f"silu{m}")
                em.w(ke, f"umm{m}")
                ke[f"hstt{m}"] = em.do(lambda e, m=m: e.scalar_tensor_tensor(
                    che[:, m * 512:(m + 1) * 512], tS[:], 1.0, PB[:],
                    ALU.bypass, ALU.mult))
            em.w(ke, "trlast", lag=1)
            em.wait_abs("qG_csc", csc_final)
            for j in range(16):
                em.w(ke, f"dnmm{j}")
                bank = PC_ if j % 2 == 0 else PD
                ke[f"dnev{j}"] = em.do(lambda e, j=j, bank=bank: e.tensor_copy(
                    cbig[:, 8192 + j * 512:8192 + (j + 1) * 512], bank[:]))
            for k in range(4):
                for jg in range(4):
                    q = k * 4 + jg
                    em.w(ke, f"trq{q}")
                    if jg == 0:
                        if k >= 2:
                            em.w(ke, f"dnst{k - 2}")
                        else:
                            em.wait_prev_iters(ke.get(f"dnst{k + 2}"))
                    ke[f"dtev{q}"] = em.do(lambda e, k=k, jg=jg, q=q: e.tensor_copy(
                        dntm[:, (k % 2) * 2048 + jg * 512:(k % 2) * 2048 + (jg + 1) * 512],
                        PT[q % 2][:]))

        Le.close()

        if _stop == "e":
            continue
        for gh in range(2):
            t0 = t0p + gh * 1024
            sy2 = p.straight("sync")
            if gh == 1:
                sy2.wait_abs("sPE", p.C["sPE"])
            xnf_tok2 = {}
            for dc in range(16):
                sy2.wait_abs(f"qS_xn{dc}", Le.final[f"qS_xn{dc}"])
                xnf_tok2[dc] = sy2.dma(lambda e, t0=t0, dc=dc: e.dma_start(
                    out=xnf[:, dc * 1024:(dc + 1) * 1024],
                    in_=XNO[t0:t0 + 1024, dc * 128:(dc + 1) * 128],
                    transpose=True), ch=f"xn{dc}")
            pe_gate_h = p.C["sPE"]
            ve_gate_h = p.C["sVE"]
            sc_gate_h = p.C["sSC"]
            Lm = Loop(p, 8, f"Csm{gp}{gh}")
            km = {}

            @Lm.sec("gpsimd")
            def _(mh, em, km=km, pe_gate_h=pe_gate_h):
                em.wait_abs("sPE", pe_gate_h)
                for s in range(2):
                    em.w(km, f"hmul{s}", lag=1)
                    km[f"wg{s}"] = em.dma(lambda e, s=s, mh=mh: e.dma_start(
                        out=sh_s[s].rearrange("p (n f) -> p n f", f=128),
                        in_=shg_t[bass.ds(mh * 2 + s, 1)].rearrange(
                            "a c p f -> p (a c) f")), ch=f"cgu{s}")
                    em.w(km, f"hmul{s}", lag=1)
                    km[f"wu{s}"] = em.dma(lambda e, s=s, mh=mh: e.dma_start(
                        out=sh_s[2 + s].rearrange("p (n f) -> p n f", f=128),
                        in_=shu_t[bass.ds(mh * 2 + s, 1)].rearrange(
                            "a c p f -> p (a c) f")), ch=f"cshu{s}")

            @Lm.sec("tensor")
            def _(mh, em, km=km, ve_gate_h=ve_gate_h, sc_gate_h=sc_gate_h,
                  xnf_tok2=xnf_tok2):
                em.wait_abs("sVE", ve_gate_h)
                em.wait_abs("sSC", sc_gate_h)
                for dc in range(16):
                    em.wait(xnf_tok2[dc])
                for s in range(2):
                    for u in range(2):
                        k4 = s * 2 + u
                        ps_, pu_ = divmod((k4 + 3) % 4, 2)
                        em.w(km, f"silu{ps_}{pu_}", lag=1 if k4 == 0 else 0)
                        em.w(km, f"wg{s}")
                        for dc in range(16):
                            mm = em.do(lambda e, s=s, u=u, dc=dc: e.matmul(
                                PA[:], sh_s[s][:, dc * 128:(dc + 1) * 128],
                                xnf[:, dc * 1024 + u * 512:dc * 1024 + (u + 1) * 512],
                                start=(dc == 0), stop=(dc == 15)), inc=(dc == 15))
                        km[f"gmm{s}{u}"] = mm
                        em.w(km, f"tmul{ps_}{pu_}", lag=1 if k4 == 0 else 0)
                        em.w(km, f"wu{s}")
                        for dc in range(16):
                            mm = em.do(lambda e, s=s, u=u, dc=dc: e.matmul(
                                PB[:], sh_s[2 + s][:, dc * 128:(dc + 1) * 128],
                                xnf[:, dc * 1024 + u * 512:dc * 1024 + (u + 1) * 512],
                                start=(dc == 0), stop=(dc == 15)), inc=(dc == 15))
                        km[f"umm{s}{u}"] = mm

            @Lm.sec("scalar")
            def _(mh, em, km=km):
                for s in range(2):
                    for u in range(2):
                        em.w(km, f"tmul{(s + 1) % 2}{u}", lag=1 if s == 0 else 0)
                        em.w(km, f"gmm{s}{u}")
                        km[f"silu{s}{u}"] = em.do(lambda e, s=s, u=u: e.activation(
                            tS[:] if u == 0 else tT[:], PA[:], AF.Silu))

            @Lm.sec("vector")
            def _(mh, em, km=km, pe_gate_h=pe_gate_h):
                for s in range(2):
                    for u in range(2):
                        em.w(km, f"silu{s}{u}")
                        em.w(km, f"umm{s}{u}")
                        em.wait_abs("sPE", pe_gate_h)
                        km[f"tmul{s}{u}"] = em.do(lambda e, s=s, u=u, mh=mh:
                            e.scalar_tensor_tensor(
                                cbig[:, bass.ds(((mh * 2 + s) * 1024 + u * 512), 512)],
                                (tS if u == 0 else tT)[:], 1.0, PB[:],
                                ALU.bypass, ALU.mult))
                    km[f"hmul{s}"] = km[f"tmul{s}1"]

            Lm.close()

            Lj = Loop(p, 8, f"Csj{gp}{gh}")
            kj = {}

            @Lj.sec("gpsimd")
            def _(jh, em, kj=kj, pe_gate_h=pe_gate_h):
                em.wait_abs("sPE", pe_gate_h)
                for s in range(2):
                    em.w(kj, f"dnev{s}1", lag=1)
                    kj[f"wd{s}"] = em.dma(lambda e, s=s, jh=jh: e.dma_start(
                        out=sd_s[s][:].rearrange("p (n f) -> p n f", f=128),
                        in_=shd_t[bass.ds(jh * 2 + s, 1)].rearrange(
                            "a c p f -> p (a c) f")), ch=f"cdd{s}")

            @Lj.sec("tensor")
            def _(jh, em, kj=kj, ve_gate_h=ve_gate_h):
                em.wait_abs("sVE", ve_gate_h)
                for s in range(2):
                    bank = PC_ if s == 0 else PD
                    for u in range(2):
                        if u == 1:
                            em.w(kj, f"dnev{s}0")
                        else:
                            em.w(kj, f"dnev{s}1", lag=1)
                        em.w(kj, f"wd{s}")
                        for m in range(16):
                            mm = em.do(lambda e, s=s, u=u, m=m, bank=bank: e.matmul(
                                bank[:], sd_s[s][:, m * 128:(m + 1) * 128],
                                cbig[:, m * 1024 + u * 512:m * 1024 + (u + 1) * 512],
                                start=(m == 0), stop=(m == 15)), inc=(m == 15))
                        kj[f"dnmm{s}{u}"] = mm

            @Lj.sec("vector")
            def _(jh, em, kj=kj):
                for s in range(2):
                    bank = PC_ if s == 0 else PD
                    for u in range(2):
                        em.w(kj, f"dnmm{s}{u}")
                        kj[f"dnev{s}{u}"] = em.do(lambda e, s=s, u=u, bank=bank, jh=jh:
                            e.tensor_copy(
                                acc[:, bass.ds((jh * 2 + s) * 1024 + u * 512, 512)],
                                bank[:]))

            Lj.close()

            if _stop == "s":
                continue
            _fin = _os.environ.get("C2FIN", "full")
            lj_ve = Lj.final["sVE"]
            dnst_fin = (p.C["qA_dnst0"], p.C["qA_dnst1"])
            g3 = p.straight("gpsimd")
            te3 = p.straight("tensor")
            v3 = p.straight("vector")
            g3.wait_abs("qA_dnst0", dnst_fin[0])
            g3.wait_abs("qA_dnst1", dnst_fin[1])
            g3.wait_abs("sPE", Lj.final["sPE"])
            te3.wait_abs("sVE", lj_ve)
            kf = {}
            for i in range(8):
                s16 = gh * 8 + i
                if _fin != "sh":
                    g3.wait(kf.get(("wstt", i - 1, 3)))
                    kf[("g1", i)] = g3.dma(lambda e, s16=s16: e.indirect_dma_start(
                        g1v, None, DN, IOX(sidx[:, s16:s16 + 1], 0)), ch="cg1")
                    g3.wait(kf.get(("ostt", i - 1, 3)))
                    kf[("g2", i)] = g3.dma(lambda e, s16=s16: e.indirect_dma_start(
                        g2v, None, DN, IOX(sidx[:, 16 + s16:17 + s16], 0)), ch="cg2")
                if _fin == "g":
                    kf[("wstt", i, 3)] = kf[("g1", i)]
                    kf[("ostt", i, 3)] = kf[("g2", i)]
                    continue
                for jg in range(4):
                    bank = PE_ if jg % 2 == 0 else PF_
                    pg = (i, jg - 2) if jg >= 2 else (i - 1, jg + 2)
                    te3.wait(kf.get(("ostt",) + pg))
                    for jj in range(4):
                        j = jg * 4 + jj
                        mm = te3.do(lambda e, i=i, j=j, jj=jj, bank=bank: e.matmul(
                            bank[:, jj * 128:(jj + 1) * 128],
                            acc[:, j * 1024 + i * 128:j * 1024 + (i + 1) * 128],
                            idf[:], start=(jj == 0), stop=(jj == 3),
                            is_transpose=True), inc=(jj == 3))
                    kf[("trf", i, jg)] = mm
                    if _fin == "t":
                        kf[("ostt", i, jg)] = mm
                        kf[("wstt", i, jg)] = mm
                        continue
                    sl = (jg % 2) * 512
                    v3.wait(mm)
                    if jg == 0 and _fin not in ("sh",):
                        v3.wait(kf[("g1", i)])
                        v3.wait(kf[("g2", i)])
                    v3.wait(kf.get(("ost",) + pg))
                    if _fin in ("sh", "shg"):
                        t1 = v3.do(lambda e, bank=bank, sl=sl: e.tensor_copy(
                            otm[:, sl:sl + 512], bank[:]))
                        kf[("wstt", i, jg)] = t1
                        kf[("ostt", i, jg)] = t1
                        t2 = t1
                    else:
                        t1 = v3.do(lambda e, jg=jg, s16=s16, bank=bank, sl=sl:
                                   e.scalar_tensor_tensor(
                            otm[:, sl:sl + 512], g1v[:, jg * 512:(jg + 1) * 512],
                            wcol[:, s16:s16 + 1], bank[:], ALU.mult, ALU.add))
                        kf[("wstt", i, jg)] = t1
                        v3.wait(t1)
                        t2 = v3.do(lambda e, jg=jg, s16=s16, sl=sl:
                                   e.scalar_tensor_tensor(
                            otm[:, sl:sl + 512], g2v[:, jg * 512:(jg + 1) * 512],
                            wcol[:, 16 + s16:17 + s16], otm[:, sl:sl + 512],
                            ALU.mult, ALU.add))
                        kf[("ostt", i, jg)] = t2
                    if _fin == "ts":
                        continue
                    g3.wait(t2)
                    g3.wait(kf.get(("ost",) + pg))
                    kf[("ost", i, jg)] = g3.dma(lambda e, i=i, jg=jg, t0=t0:
                        e.dma_start(
                            out=OUT[t0 + i * 128:t0 + (i + 1) * 128,
                                    jg * 512:(jg + 1) * 512],
                            in_=otm[:, (jg % 2) * 512:(jg % 2 + 1) * 512],
                            accum_op=ALU.add), ch=f"cacc{jg % 2}")
            if _fin == "full":
                for jg in range(4):
                    g3.wait(kf[("ost", 7, jg)])

    return None


TB_FULL = 16384
TO_FULL = 4096
NKC_FULL = TB_FULL // 512
NQT_FULL = TO_FULL // 512


def _tile2(W):
    K, M = W.shape
    return np.ascontiguousarray(
        W.reshape(K // 128, 128, M // 128, 128).transpose(0, 2, 1, 3)).astype(np.float32)


def _tile4(W):
    K, M = W.shape
    return np.ascontiguousarray(W.reshape(K // 128, 128, M // 128, 128).transpose(0, 2, 1, 3))


def build_program():
    cfg = Cfg(TB=TB_FULL, TO=TO_FULL)
    p = Prog(cfg)
    psS_h = p.psum("psS", [128, 2048], F32)
    for i in (2, 3):
        p.psum(f"pf{i}", [128, 512], F32)
    for i in range(2):
        p.psum(f"pt{i}", [128, 512], BF16)
    p.ps["pf0"] = psS_h[:, 0:512]
    p.ps["pf1"] = psS_h[:, 512:1024]
    p.ps["pf4"] = psS_h[:, 1024:1536]
    p.ps["pf5"] = psS_h[:, 1536:2048]
    p.inp("xb", [TB_FULL, 2048], F32)
    p.inp("xo", [TO_FULL, 2048], F32)
    for nm, sh in (("wkvT_t", [16, 4, 128, 128]), ("wcT_t", [4, 2, 128, 128]),
                   ("wqT_t", [2, 2, 128, 128]), ("wkvcT_t", [2, 4, 128, 128]),
                   ("wc2lT_t", [2, 4, 128, 128]), ("wsumA_t", [4, 16, 128, 128]),
                   ("wsumB_t", [4, 16, 128, 128])):
        p.inp(nm, sh, F32)
    p.inp("gb_h", [128, 4], F32)
    p.inp("bo_h", [128, 16], F32)
    p.inp("ident_b", [128, 128], BF16)
    p.inp("ident_f", [128, 128], F32)
    p.inp("cosb", [2, 128, TB_FULL], F32)
    p.inp("sinb", [2, 128, TB_FULL], F32)
    p.inp("coso", [2, 128, TO_FULL], F32)
    p.inp("sino", [2, 128, TO_FULL], F32)
    p.inp("dtile", [128, 2048], F32)
    p.inp("dltab", [128, NQT_FULL * (NKC_FULL // 4) * 4], F32)
    p.inp("gu_t", [16, 8, 2, 16, 128, 128], F32)
    p.inp("d2_t", [16, 16, 8, 128, 128], F32)
    p.inp("shg_t", [16, 16, 128, 128], F32)
    p.inp("shu_t", [16, 16, 128, 128], F32)
    p.inp("shd_t", [16, 16, 128, 128], F32)
    p.inp("rt_t", [16, 128, 16], F32)
    p.inp("ecap_h", [16, 1], F32)
    p.scratch("XNB", [TB_FULL, 2048], BF16)
    p.scratch("XNO", [TO_FULL, 2048], BF16)
    p.scratch("KF", [NKC_FULL * 128, 1024], BF16)
    p.scratch("VT", [TB_FULL, 256], BF16)
    p.scratch("QF", [NQT_FULL * 128, 1024], BF16)
    p.scratch("XS", [(TO_FULL // 2048) * 8192 + 128, 2048], BF16)
    p.scratch("DN", [(TO_FULL // 2048) * 8192 + 128, 2048], BF16)
    p.out("OUT", [TO_FULL, 2048], F32)

    emit_P0(p)
    emit_zero_xs(p)
    emit_A1(p, "xb", TB_FULL, "XNB")
    p.barrier()
    emit_A1(p, "xo", TO_FULL, "XNO")
    p.barrier()
    emit_A2(p, "XNB", TB_FULL, "cosb", "sinb", do_kv=True, kf_name="KF", vt_name="VT")
    p.barrier()
    emit_A2(p, "XNO", TO_FULL, "coso", "sino", do_kv=False, qf_name="QF")
    p.barrier()
    p.free_sbufs(("a1_", "a2_", "wsA", "wsB"))
    emit_B(p, "QF", "KF", "VT", "xo", "OUT", NKC_FULL, NQT_FULL)
    p.barrier()
    p.free_sbufs(("b_", "wkvT", "wcT", "wqT", "wkvcT", "wc2lT", "wsum"))
    emit_C2(p, "XNO", "OUT", TO_FULL)
    p.barrier()
    return p.finalize()


def kernel(L0_attn_x, L0_attn_W_to_kv, L0_attn_W_compress, L0_attn_ln_gamma,
           L0_attn_ln_beta, L0_attn_W_q_comp, L0_attn_W_kv_comp, L0_attn_W_comp2lat,
           L0_attn_W_out_res, L0_attn_b_out_res, L0_ffn_router_w, L0_ffn_gate_w,
           L0_ffn_up_w, L0_ffn_down_w, L0_ffn_shared_gate_w, L0_ffn_shared_up_w,
           L0_ffn_shared_down_w):
    from concourse.bass_utils import run_bass_kernel_spmd
    x = np.ascontiguousarray(np.asarray(L0_attn_x, dtype=np.float32))
    gamma = np.asarray(L0_attn_ln_gamma, np.float32)
    beta = np.asarray(L0_attn_ln_beta, np.float32)
    b_out = np.asarray(L0_attn_b_out_res, np.float32)

    shared = dict(
        wkvT_t=_tile2(np.asarray(L0_attn_W_to_kv, np.float32).T),
        wcT_t=_tile2(np.asarray(L0_attn_W_compress, np.float32).T),
        wqT_t=_tile2(np.asarray(L0_attn_W_q_comp, np.float32).T),
        wkvcT_t=_tile2(np.asarray(L0_attn_W_kv_comp, np.float32).T),
        wc2lT_t=_tile2(np.asarray(L0_attn_W_comp2lat, np.float32).T),
        wsumA_t=_tile2(np.asarray(L0_attn_W_to_kv, np.float32)),
        wsumB_t=_tile2(np.asarray(L0_attn_W_out_res, np.float32).T),
        gb_h=np.ascontiguousarray(
            np.stack([gamma[:128], gamma[128:], beta[:128], beta[128:]], 1)).astype(np.float32),
        bo_h=np.ascontiguousarray(b_out.reshape(16, 128).T).astype(np.float32),
        ident_b=np.eye(128, dtype=BF),
        ident_f=np.eye(128, dtype=np.float32),
        dtile=(np.arange(2048)[None, :] - np.arange(128)[:, None]).astype(np.float32),
        rt_t=np.ascontiguousarray(
            np.asarray(L0_ffn_router_w, np.float32).T.reshape(16, 128, 16)),
    )
    T = TB_FULL
    inv = 1.0 / 10000.0 ** (np.arange(0, 256, 2, dtype=np.float64) / 256.0)
    fr = np.arange(T, dtype=np.float64)[:, None] * inv[None, :]
    cos = np.concatenate([np.cos(fr), np.cos(fr)], 1).T.astype(np.float32)
    sin = np.concatenate([np.sin(fr), np.sin(fr)], 1).T.astype(np.float32)
    shared["cosb"] = np.ascontiguousarray(cos.reshape(2, 128, T))
    shared["sinb"] = np.ascontiguousarray(sin.reshape(2, 128, T))
    gate_w = np.asarray(L0_ffn_gate_w, np.float32)
    up_w = np.asarray(L0_ffn_up_w, np.float32)
    down_w = np.asarray(L0_ffn_down_w, np.float32)
    gu_t = np.zeros((16, 8, 2, 16, 128, 128), np.float32)
    d2_t = np.zeros((16, 16, 8, 128, 128), np.float32)
    for e in range(16):
        gu_t[e, :, 0] = _tile4(gate_w[e]).transpose(1, 0, 2, 3)
        gu_t[e, :, 1] = _tile4(up_w[e]).transpose(1, 0, 2, 3)
        d2_t[e] = _tile4(down_w[e]).transpose(1, 0, 2, 3)
    shared["gu_t"] = gu_t
    shared["d2_t"] = d2_t
    shared["ecap_h"] = (np.arange(16, dtype=np.float32) * 512.0).reshape(16, 1)
    shared["shg_t"] = np.ascontiguousarray(
        _tile4(np.asarray(L0_ffn_shared_gate_w, np.float32)).transpose(1, 0, 2, 3)).astype(np.float32)
    shared["shu_t"] = np.ascontiguousarray(
        _tile4(np.asarray(L0_ffn_shared_up_w, np.float32)).transpose(1, 0, 2, 3)).astype(np.float32)
    shared["shd_t"] = np.ascontiguousarray(
        _tile4(np.asarray(L0_ffn_shared_down_w, np.float32)).transpose(1, 0, 2, 3)).astype(np.float32)
    in_maps = []
    for c in range(8):
        b, q = divmod(c, 4)
        m = dict(shared)
        m["xb"] = np.ascontiguousarray(x[b])
        m["xo"] = np.ascontiguousarray(x[b, q * TO_FULL:(q + 1) * TO_FULL])
        m["coso"] = np.ascontiguousarray(shared["cosb"][:, :, q * TO_FULL:(q + 1) * TO_FULL])
        m["sino"] = np.ascontiguousarray(shared["sinb"][:, :, q * TO_FULL:(q + 1) * TO_FULL])
        NKB = NKC_FULL // 4
        dl = np.zeros((128, NQT_FULL * NKB * 4), np.float32)
        for qt in range(NQT_FULL):
            for kb_ in range(NKB):
                for sub in range(4):
                    dl[:, (qt * NKB + kb_) * 4 + sub] = (
                        q * TO_FULL + qt * 512 + sub * 128 - kb_ * 2048)
        m["dltab"] = dl
        in_maps.append(m)

    nc = build_program()
    res = run_bass_kernel_spmd(nc, in_maps, core_ids=list(range(8)))
    out = np.empty((2, TB_FULL, 2048), dtype=np.float32)
    for c in range(8):
        b, q = divmod(c, 4)
        out[b, q * TO_FULL:(q + 1) * TO_FULL] = res.results[c]["OUT"]
    return out


# version 08
# speedup vs baseline: 1.0287x; 1.0287x over previous
import numpy as np
import ml_dtypes
import concourse.bass as bass
import concourse.bacc as bacc_mod
import concourse.mybir as mybir
import bass_rust as _bass_rust
from concourse.inst_simplify_rust import inst_simplify as _rust_inst_simplify
from contextlib import ExitStack

F32 = mybir.dt.float32
BF16 = mybir.dt.bfloat16
AF = mybir.ActivationFunctionType
ALU = mybir.AluOpType
AX = mybir.AxisListType.X
BF = ml_dtypes.bfloat16
ts = bass.ts

ENGINES = ("sync", "scalar", "vector", "tensor", "gpsimd")
ESEM = {"sync": "sSY", "scalar": "sSC", "vector": "sVE", "tensor": "sPE", "gpsimd": "sPL"}
QSEM = {"sync": "qS", "scalar": "qA", "gpsimd": "qG"}
SEM_OWNER = {}
for e, s in ESEM.items():
    SEM_OWNER[s] = e
for e, s in QSEM.items():
    SEM_OWNER[s] = e


class Tok:
    __slots__ = ("sem", "base", "off", "loop")

    def __init__(self, sem, base, off, loop=None):
        self.sem, self.base, self.off, self.loop = sem, base, off, loop


class Cfg:
    def __init__(self, TB=16384, TO=4096):
        self.TB = TB
        self.TO = TO
        self.D = 2048
        self.DL = 512
        self.DC = 256
        self.E = 16
        self.FE = 1024
        self.FS = 2048
        self.LN_EPS = 1e-5


class Prog:
    def __init__(self, cfg: Cfg):
        self.cfg = cfg
        self.nc = bacc_mod.Bacc("TRN2", target_bir_lowering=False)
        self.ctx = ExitStack()
        self.sections = {e: [] for e in ENGINES}
        self.C = {}
        self.sems = {}
        self.dram = {}
        self.sb = {}
        self.sb_guards = []
        self.ps = {}
        for s in list(ESEM.values()) + list(QSEM.values()):
            self.sem(s)

    def sem(self, name):
        if name not in self.sems:
            self.sems[name] = self.ctx.enter_context(self.nc.semaphore(name))
            self.C[name] = 0
        return self.sems[name]

    def inp(self, name, shape, dtype=F32):
        t = self.nc.dram_tensor(name, list(shape), dtype, kind="ExternalInput").ap()
        self.dram[name] = t
        return t

    def out(self, name, shape, dtype=F32):
        t = self.nc.dram_tensor(name, list(shape), dtype, kind="ExternalOutput").ap()
        self.dram[name] = t
        return t

    def scratch(self, name, shape, dtype=BF16):
        t = self.nc.dram_tensor(name, list(shape), dtype, kind="Internal").ap()
        self.dram[name] = t
        return t

    def sbuf(self, name, shape, dtype):
        g = self.nc.sbuf_tensor(name, list(shape), dtype)
        t = g.__enter__()
        self.sb[name] = t
        self.sb_guards.append((name, g))
        return t

    def free_sbufs(self, prefixes):
        while self.sb_guards and any(
                self.sb_guards[-1][0].startswith(pf) for pf in prefixes):
            name, g = self.sb_guards.pop()
            g.__exit__(None, None, None)
            del self.sb[name]

    def psum(self, name, shape, dtype=F32):
        t = self.ctx.enter_context(self.nc.psum_tensor(name, list(shape), dtype))
        self.ps[name] = t
        return t

    def barrier(self):
        self.sem("sBAR")
        self.C["sBAR"] += 5
        target = self.C["sBAR"]
        for e in ENGINES:
            own = [ESEM[e]]
            if e in QSEM:
                q = QSEM[e]
                own += [s for s in self.C if s.startswith(q + "_")]
            vals = [(s, self.C[s]) for s in own]
            def f(eng, vals=vals, target=target):
                for s, vl in vals:
                    eng.wait_ge(self.sems[s], vl)
                eng.nop().then_inc(self.sems["sBAR"], 1)
                eng.wait_ge(self.sems["sBAR"], target)
            self.sections[e].append(f)

    def straight(self, engine):
        em = Em(self, engine, loop=None)
        self.sections[engine].append(lambda eng, em=em: em._run(eng))
        return em

    def finalize(self):
        nc = self.nc
        block = self.ctx.enter_context(nc.Block())
        with self.ctx:
            @block.sync
            def _(eng):
                for f in self.sections["sync"]:
                    f(eng)

            @block.scalar
            def _(eng):
                for f in self.sections["scalar"]:
                    f(eng)

            @block.vector
            def _(eng):
                for f in self.sections["vector"]:
                    f(eng)

            @block.tensor
            def _(eng):
                for f in self.sections["tensor"]:
                    f(eng)

            @block.gpsimd
            def _(eng):
                for f in self.sections["gpsimd"]:
                    f(eng)
        import os
        if os.environ.get('FULLCOMPILE'):
            nc.compile()
        else:
            self.safe_compile()
        return nc

    def safe_compile(self):
        nc = self.nc
        _bass_rust.merge_adjacent_extseq(nc)
        _bass_rust.insert_bir_kernel_barrier_sem_inc(nc)
        _bass_rust.generate_event_semaphores(nc)
        _bass_rust.remove_dead_instructions_after_branch(nc.m)
        nc.validate_blocks()
        _bass_rust.dce_regs(nc)
        _bass_rust.remove_dead_allocations(nc.m)
        nc.verify_switch_hints()
        _bass_rust.alloc_regs(nc)
        _bass_rust.replace_nops_with_events(nc)
        _bass_rust.fuse_nops(nc.main_func)
        _bass_rust.remove_dead_nops(nc.main_func)
        _bass_rust.remove_dangling_data(nc.m)
        _bass_rust.generate_event_semaphores(nc)
        nc.insert_library_loads()
        _bass_rust.insert_act_table_loads(nc)
        _bass_rust.insert_hostgen_rebases(nc)
        mybir.codegen_inst_isa_subclasses(nc)


class Em:

    def __init__(self, p, engine, loop):
        self.p = p
        self.engine = engine
        self.loop = loop
        self.esem = ESEM[engine]
        self.qsem = QSEM.get(engine)
        self._ops = []
        self._i = None
        self._counting = False
        self._eng = None

    def _mk_tok(self, sem_name, step):
        if self.loop is None:
            self.p.C[sem_name] += step
            return Tok(sem_name, self.p.C[sem_name], 0)
        off = self.loop.off.get(sem_name, 0) + step
        self.loop.off[sem_name] = off
        return Tok(sem_name, None, off, loop=self.loop)

    SELF_SYNC = ("vector", "scalar", "gpsimd")

    def do(self, fn, inc=True):
        selfw = self.engine in Em.SELF_SYNC
        if self.loop is None:
            prev = self.p.C[self.esem]
            tok = self._mk_tok(self.esem, 1) if inc else None
            def cl(eng, fn=fn, inc=inc, prev=prev, selfw=selfw):
                if selfw and prev > 0:
                    eng.wait_ge(self.p.sems[self.esem], prev)
                ins = fn(eng)
                if inc:
                    ins.then_inc(self.p.sems[self.esem], 1)
            self._ops.append(cl)
            return tok
        off_before = self.loop.off.get(self.esem, 0)
        tok = self._mk_tok(self.esem, 1) if inc else None
        if self._counting:
            if selfw:
                self.loop.waited.setdefault(self.engine, set()).add(self.esem)
            return tok
        if selfw:
            per = self.loop.per[self.esem]
            base = self.loop.base[self.esem]
            self._eng.wait_ge(self.p.sems[self.esem],
                              self._i * per + base + off_before)
        ins = fn(self._eng)
        if inc:
            ins.then_inc(self.p.sems[self.esem], 1)
        return tok

    def dma(self, fn, ch="d"):
        sem = f"{self.qsem}_{ch}"
        self.p.sem(sem)
        if self.loop is not None and sem not in self.loop.extra_sems:
            self.loop.extra_sems.append(sem)
        tok = self._mk_tok(sem, 16)
        if self.loop is None:
            def cl(eng, fn=fn, sem=sem):
                fn(eng).then_inc(self.p.sems[sem], 16)
            self._ops.append(cl)
        elif not self._counting:
            fn(self._eng).then_inc(self.p.sems[sem], 16)
        return tok

    def wait(self, tok, lag=0):
        if tok is None:
            return
        if tok.loop is None:
            val = tok.base
            if self.loop is None:
                self._ops.append(lambda eng, val=val, s=tok.sem:
                                 eng.wait_ge(self.p.sems[s], val))
            elif not self._counting:
                self._eng.wait_ge(self.p.sems[tok.sem], val)
            return
        if self.loop is not tok.loop:
            val = tok.loop.final[tok.sem]
            if self.loop is None:
                self._ops.append(lambda eng, val=val, s=tok.sem:
                                 eng.wait_ge(self.p.sems[s], val))
            elif not self._counting:
                self._eng.wait_ge(self.p.sems[tok.sem], val)
            return
        if self._counting:
            if lag > 0:
                self.loop.lagreq.append((tok.sem, tok.off, lag))
            self.loop.waited.setdefault(self.engine, set()).add(tok.sem)
            return
        per = self.loop.per[tok.sem]
        base = self.loop.base[tok.sem]
        const = base + tok.off - lag * per
        assert const >= 0, f"negative wait base {const} for {tok.sem}"
        if per == 0:
            self._eng.wait_ge(self.p.sems[tok.sem], const)
        else:
            self._eng.wait_ge(self.p.sems[tok.sem], self._i * per + const)

    def w(self, tk, name, lag=0):
        tok = tk.get(name)
        if tok is None:
            assert self._counting or self.loop is None, f"missing token {name}"
            if not (self._counting or self.loop is None):
                raise KeyError(name)
            return
        self.wait(tok, lag=lag)

    def wait_abs(self, sem, val):
        if self.loop is None:
            self._ops.append(lambda eng, sem=sem, val=val:
                             eng.wait_ge(self.p.sems[sem], val))
        elif not self._counting:
            self._eng.wait_ge(self.p.sems[sem], val)

    def wait_prev_iters(self, tok):
        if tok is None:
            return
        assert self.loop is tok.loop and self.loop is not None
        if self._counting:
            self.loop.waited.setdefault(self.engine, set()).add(tok.sem)
            return
        per = self.loop.per[tok.sem]
        base = self.loop.base[tok.sem]
        self._eng.wait_ge(self.p.sems[tok.sem], self._i * per + base)

    def _run(self, eng):
        for cl in self._ops:
            cl(eng)


class Loop:
    def __init__(self, p, trip, name=""):
        self.p = p
        self.trip = trip
        self.name = name
        self.bodies = []
        self.off = {}
        self.per = {}
        self.base = {}
        self.final = {}
        self.lagreq = []
        self.extra_sems = []
        self.fab = {}
        self.waited = {}

    def sec(self, engine):
        def deco(f):
            self.bodies.append((engine, f))
            return f
        return deco

    def _engine_sems(self, engine):
        out = [ESEM[engine]]
        if engine in QSEM:
            q = QSEM[engine]
            out.append(q)
            out += [s for s in self.extra_sems if s.startswith(q + "_")]
        return out

    def close(self):
        p = self.p
        for pas in range(2):
            self.lagreq = []
            for e, f in self.bodies:
                for s in self._engine_sems(e):
                    self.off[s] = 0
                em = Em(p, e, loop=self)
                em._counting = True
                f(0, em)
        self.per = dict(self.off)
        self.per.update(self.fab)
        need = {}
        for sem, off, lag in self.lagreq:
            per = self.per.get(sem, 0)
            short = lag * per - off - p.C[sem]
            if short > 0:
                need[sem] = max(need.get(sem, 0), short)
        for sem, P in need.items():
            owner = SEM_OWNER[sem]
            assert sem not in QSEM.values(), f"cannot prime dma sem {sem}"
            cur = p.C[sem]
            def prime(eng, sem=sem, cur=cur, P=P):
                eng.wait_ge(self.p.sems[sem] if False else p.sems[sem], cur)
                eng.sem_inc(p.sems[sem], P)
            p.sections[owner].append(prime)
            p.C[sem] += P
        self.base = dict(p.C)
        for e, f in self.bodies:
            def section(eng, e=e, f=f):
                for s in self._engine_sems(e):
                    self.off[s] = 0
                em = Em(p, e, loop=self)
                em._eng = eng
                with eng.Fori(0, self.trip) as i:
                    em._i = i
                    f(i, em)
            p.sections[e].append(section)
        for s, per in self.per.items():
            if per and s not in self.fab:
                p.C[s] += per * self.trip
        self.final = dict(p.C)
        return self


def emit_A1(p, src_name, T, xn_name):
    cfg = p.cfg
    src = p.dram[src_name]
    xn = p.dram[xn_name]
    NT = T // 128
    assert NT % 2 == 0
    x_ = [p.sb.get(f"a1_x{s}") or p.sbuf(f"a1_x{s}", [128, 2048], F32) for s in (0, 1)]
    o_ = [p.sb.get(f"a1_o{s}") or p.sbuf(f"a1_o{s}", [128, 2048], BF16) for s in (0, 1)]
    sq = p.sb.get("a1_sq") or p.sbuf("a1_sq", [128, 2048], F32)
    st = p.sb.get("a1_st") or p.sbuf("a1_st", [128, 12], F32)
    DINV = 1.0 / cfg.D
    L = Loop(p, NT // 2, "A1")
    toks = {}

    @L.sec("sync")
    def _(h, em):
        for s in (0, 1):
            em.wait(toks.get(f"xn{s}"), lag=1)
            toks[f"ld{s}"] = em.dma(
                lambda e, s=s: e.dma_start(out=x_[s][:], in_=src[ts(h * 2 + s, 128)]),
                ch=f"a1x{s}")

    @L.sec("vector")
    def _(h, em):
        for s in (0, 1):
            c = s * 5
            xt, ot = x_[s], o_[s]
            em.wait(toks[f"ld{s}"])
            em.wait(toks.get(f"stdone{s}"), lag=1)
            t_sq = em.do(lambda e, xt=xt, c=c: e.scalar_tensor_tensor(
                sq[:], xt[:], 1.0, xt[:], ALU.bypass, ALU.mult,
                accum_out=st[:, c + 1:c + 2]))
            t_sum = em.do(lambda e, xt=xt, c=c: e.reduce_sum(
                st[:, c:c + 1], xt[:], axis=AX))
            em.wait(t_sum)
            t1 = em.do(lambda e, c=c: e.tensor_scalar(
                st[:, c + 2:c + 3], st[:, c:c + 1], -DINV, None, ALU.mult))
            em.wait(t1)
            t2 = em.do(lambda e, c=c: e.scalar_tensor_tensor(
                st[:, c + 3:c + 4], st[:, c:c + 1], -DINV, st[:, c + 2:c + 3],
                ALU.mult, ALU.mult))
            em.wait(t_sq); em.wait(t2)
            t3 = em.do(lambda e, c=c: e.tensor_scalar(
                st[:, c + 3:c + 4], st[:, c + 1:c + 2], DINV, st[:, c + 3:c + 4],
                ALU.mult, ALU.subtract))
            em.wait(t3)
            t3b = em.do(lambda e, c=c: e.tensor_scalar(
                st[:, c + 3:c + 4], st[:, c + 3:c + 4], cfg.LN_EPS, None, ALU.add))
            em.wait(t3b)
            toks[f"var{s}"] = t3b
            em.wait(toks.get(f"rstd{s}"))
            t4 = em.do(lambda e, s=s: e.reciprocal(
                st[:, 10 + s:11 + s], st[:, 10 + s:11 + s]))
            em.wait(t4)
            toks[f"xn{s}"] = em.do(lambda e, xt=xt, ot=ot, c=c, s=s: e.tensor_scalar(
                ot[:], xt[:], st[:, c + 2:c + 3], st[:, 10 + s:11 + s],
                ALU.add, ALU.mult))

    @L.sec("scalar")
    def _(h, em):
        for s in (0, 1):
            c = s * 5
            em.wait(toks[f"var{s}"])
            toks[f"rstd{s}"] = em.do(lambda e, c=c, s=s: e.activation(
                st[:, 10 + s:11 + s], st[:, c + 3:c + 4], AF.Sqrt))

    @L.sec("gpsimd")
    def _(h, em):
        for s in (0, 1):
            em.wait(toks[f"xn{s}"])
            t = em.dma(lambda e, s=s: e.dma_start(
                out=xn[ts(h * 2 + s, 128)], in_=o_[s][:]), ch=f"a1o{s}")
            em.wait(t)
            toks[f"stdone{s}"] = em.do(lambda e: e.engine_nop())

    return L.close()


def emit_P0(p):
    cfg = p.cfg
    gb = p.sbuf("gb", [128, 4], F32)
    bo = p.sbuf("bo", [128, 16], F32)
    idb = p.sbuf("idb", [128, 128], BF16)
    idf = p.sbuf("idf", [128, 128], F32)
    ones = p.sbuf("ones", [128, 1], BF16)
    ones1 = p.sbuf("ones1", [1, 128], BF16)
    specs = [("wkvT", 16 * 4), ("wcT", 4 * 2), ("wqT", 2 * 2), ("wkvcT", 2 * 4),
             ("wc2lT", 2 * 4)]
    g = p.straight("gpsimd")
    toks = {}
    for name, n in specs:
        sb = p.sbuf(name, [128, n * 128], BF16)
        inp = p.dram[name + "_t"]
        toks[name] = g.dma(lambda e, sb=sb, inp=inp: e.dma_start(
            out=sb[:].rearrange("p (n f) -> p n f", f=128),
            in_=inp.rearrange("a b p f -> p (a b) f")), ch="w")
    wsum = p.sbuf("wsum", [128, 64 * 128], BF16)
    wsA = p.sbuf("wsA", [128, 16 * 128], F32)
    wsB = p.sbuf("wsB", [128, 16 * 128], F32)
    sy = p.straight("sync")
    v = p.straight("vector")
    for z in range(4):
        ta = sy.dma(lambda e, z=z: e.dma_start(
            out=wsA[:].rearrange("p (n f) -> p n f", f=128),
            in_=p.dram["wsumA_t"][z].rearrange("b p f -> p b f")), ch="wsA")
        tb = sy.dma(lambda e, z=z: e.dma_start(
            out=wsB[:].rearrange("p (n f) -> p n f", f=128),
            in_=p.dram["wsumB_t"][z].rearrange("b p f -> p b f")), ch="wsB")
        v.wait(ta); v.wait(tb)
        tw = v.do(lambda e, z=z: e.tensor_add(
            wsum[:, z * 2048:(z + 1) * 2048], wsA[:], wsB[:]))
        if z < 3:
            sy.wait(tw)
    toks["wsum"] = tw
    toks["gb"] = sy.dma(lambda e: e.dma_start(out=gb[:], in_=p.dram["gb_h"]), ch="gb")
    toks["bo"] = sy.dma(lambda e: e.dma_start(out=bo[:], in_=p.dram["bo_h"]), ch="gb")
    toks["idb"] = sy.dma(lambda e: e.dma_start(out=idb[:], in_=p.dram["ident_b"]), ch="id")
    toks["idf"] = sy.dma(lambda e: e.dma_start(out=idf[:], in_=p.dram["ident_f"]), ch="id")
    toks["ones"] = v.do(lambda e: e.memset(ones[:], 1.0))
    toks["ones1"] = v.do(lambda e: e.memset(ones1[:], 1.0))
    return toks


def emit_A2(p, xn_name, T, cos_name, sin_name, do_kv, kf_name=None,
            vt_name=None, qf_name=None):
    cfg = p.cfg
    NG = T // 1024
    xnt = p.dram[xn_name]
    cost_, sint_ = p.dram[cos_name], p.dram[sin_name]
    wkvT, wcT, wqT, wkvcT = p.sb["wkvT"], p.sb["wcT"], p.sb["wqT"], p.sb["wkvcT"]
    gb, ones, ones1, idb = p.sb["gb"], p.sb["ones"], p.sb["ones1"], p.sb["idb"]
    DCI = 1.0 / cfg.DC

    def get(name, shape, dt):
        return p.sb.get(name) or p.sbuf(name, shape, dt)
    xnf = get("a2_xnf", [128, 16 * 1024], BF16)
    kvl = get("a2_kvl", [128, 4 * 1024], BF16)
    c32 = get("a2_c32", [128, 2 * 1024], F32)
    cbf = get("a2_cbf", [128, 2 * 1024], BF16)
    csq = get("a2_csq", [128, 2 * 1024], BF16)
    rows = get("a2_rows", [1, 8 * 512], F32)
    rowb = get("a2_rowb", [1, 4 * 512], BF16)
    cos_s = get("a2_cos", [128, 2 * 1024], F32)
    sin_s = get("a2_sin", [128, 2 * 1024], F32)
    tA = get("a2_tA", [128, 512], F32)
    tB = get("a2_tB", [128, 512], F32)
    tC = get("a2_tC", [128, 512], F32)
    tD = get("a2_tD", [128, 512], F32)
    ch_s = get("a2_ch", [128, 2 * 1024], BF16)
    kq = get("a2_kq", [128, 2 * 1024], BF16)
    vbf = get("a2_vbf", [128, 2 * 1024], BF16)
    vtm = get("a2_vtm", [128, 4 * 256], BF16)
    PK = [p.ps["pf0"], p.ps["pf1"]]
    PC = [p.ps["pf2"], p.ps["pf3"]]
    PS = [p.ps["pf4"], p.ps["pf5"]]
    PT = [p.ps["pt0"], p.ps["pt1"]]

    def rcol(u, k):
        c = (u * 4 + k) * 512
        return rows[:, c:c + 512]

    def rbcol(u, k):
        c = (u * 2 + k) * 512
        return rowb[:, c:c + 512]

    L = Loop(p, NG, "A2kv" if do_kv else "A2q")
    tk = {}
    nout = 4 if do_kv else 2
    wproj = wkvcT if do_kv else wqT
    pfx = ""
    for cc in range(2):
        for nm in ("cos", "sin"):
            ch = f"qS_{pfx}{nm}{cc}"
            p.sem(ch)
            L.fab[ch] = 16
            tk[f"{nm}{cc}"] = Tok(ch, None, 16, loop=L)
    for dc in range(16):
        ch = f"qS_{pfx}xn{dc}"
        p.sem(ch)
        L.fab[ch] = 16
        tk[f"xn{dc}"] = Tok(ch, None, 16, loop=L)
    nrelay = 10 if do_kv else 2
    L.fab["sSY"] = nrelay
    tk["kqst0"] = Tok("sSY", None, 1, loop=L)
    tk["kqst1"] = Tok("sSY", None, 2, loop=L)
    if do_kv:
        for j in range(6):
            tk[f"vtmrelay{j}"] = Tok("sSY", None, 3 + j, loop=L)
        tk["vtmrelay6"] = Tok("sSY", None, 9, loop=L)
        tk["vtmrelay7"] = Tok("sSY", None, 10, loop=L)

    @L.sec("scalar")
    def _(g, em):
        for u in range(2):
            em.w(tk, f"vareps{u}")
            tk[f"sqrt{u}"] = em.do(lambda e, u=u: e.activation(
                rcol(u, 3), rcol(u, 2), AF.Sqrt))

    @L.sec("tensor")
    def _(g, em):
        for idx in range(8):
            l, u = divmod(idx, 2)
            bank = PK[idx % 2]
            if idx >= 2:
                em.w(tk, f"kvlevac{idx - 2}" if "{" in "kvlevac{idx - 2}" else "kvlevac{idx - 2}")
            else:
                em.w(tk, f"ropePK{idx % 2}_1" if "{" in "ropePK{idx % 2}_1" else "ropePK{idx % 2}_1", lag=1)
            for dc in range(16):
                em.w(tk, f"xn{dc}" if "{" in "xn{dc}" else "xn{dc}")
                mm = em.do(lambda e, l=l, u=u, dc=dc, bank=bank: e.matmul(
                    bank[:], wkvT[:, (dc * 4 + l) * 128:(dc * 4 + l + 1) * 128],
                    xnf[:, dc * 1024 + u * 512:dc * 1024 + (u + 1) * 512],
                    start=(dc == 0), stop=(dc == 15)), inc=(dc == 15))
            tk[f"kvlmm{idx}"] = mm
        for idx in range(4):
            cc, u = divmod(idx, 2)
            bank = PC[idx % 2]
            if idx >= 2:
                em.w(tk, f"cevac{idx - 2}" if "{" in "cevac{idx - 2}" else "cevac{idx - 2}")
            elif do_kv:
                em.w(tk, f"vbfe1_{idx % 2}" if "{" in "vbfe1_{idx % 2}" else "vbfe1_{idx % 2}", lag=1)
            else:
                em.w(tk, f"cevac{idx + 2}" if "{" in "cevac{idx + 2}" else "cevac{idx + 2}", lag=1)
            for l in range(4):
                em.w(tk, f"kvlevac{l * 2 + u}" if "{" in "kvlevac{l * 2 + u}" else "kvlevac{l * 2 + u}")
                mm = em.do(lambda e, cc=cc, u=u, l=l, bank=bank: e.matmul(
                    bank[:], wcT[:, (l * 2 + cc) * 128:(l * 2 + cc + 1) * 128],
                    kvl[:, l * 1024 + u * 512:l * 1024 + (u + 1) * 512],
                    start=(l == 0), stop=(l == 3)), inc=(l == 3))
            tk[f"cmm{idx}"] = mm
        for u in range(2):
            if u == 0:
                em.w(tk, f"chPS0_1" if "{" in "chPS0_1" else "chPS0_1", lag=1)
                em.w(tk, f"chPS1_1" if "{" in "chPS1_1" else "chPS1_1", lag=1)
            else:
                em.w(tk, f"psS0r0" if "{" in "psS0r0" else "psS0r0")
                em.w(tk, f"psS1r0" if "{" in "psS1r0" else "psS1r0")
            em.w(tk, f"cbf{u}" if "{" in "cbf{u}" else "cbf{u}")
            em.do(lambda e, u=u: e.matmul(
                PS[0][0:1, :], ones[:], cbf[:, u * 512:u * 512 + 512],
                start=True, stop=False), inc=False)
            tk[f"S1{u}"] = em.do(lambda e, u=u: e.matmul(
                PS[0][0:1, :], ones[:], cbf[:, 1024 + u * 512:1024 + u * 512 + 512],
                start=False, stop=True))
            em.w(tk, f"csq{u}" if "{" in "csq{u}" else "csq{u}")
            em.do(lambda e, u=u: e.matmul(
                PS[1][0:1, :], ones[:], csq[:, u * 512:u * 512 + 512],
                start=True, stop=False), inc=False)
            tk[f"S2{u}"] = em.do(lambda e, u=u: e.matmul(
                PS[1][0:1, :], ones[:], csq[:, 1024 + u * 512:1024 + u * 512 + 512],
                start=False, stop=True))
        for u in range(2):
            em.w(tk, f"rowb{u}" if "{" in "rowb{u}" else "rowb{u}")
            if u == 0:
                em.w(tk, f"psS0r1" if "{" in "psS0r1" else "psS0r1")
                em.w(tk, f"psS1r1" if "{" in "psS1r1" else "psS1r1")
            else:
                em.w(tk, f"chPS0_0" if "{" in "chPS0_0" else "chPS0_0")
                em.w(tk, f"chPS1_0" if "{" in "chPS1_0" else "chPS1_0")
            tk[f"bcmu{u}"] = em.do(lambda e, u=u: e.matmul(
                PS[0][:], ones1[:], rbcol(u, 0), start=True, stop=True))
            tk[f"bcr{u}"] = em.do(lambda e, u=u: e.matmul(
                PS[1][:], ones1[:], rbcol(u, 1), start=True, stop=True))
        for u in range(2):
            for oc in range(2):
                if u == 0:
                    em.w(tk, f"kvlevac{6 + oc}" if "{" in "kvlevac{6 + oc}" else "kvlevac{6 + oc}")
                else:
                    em.w(tk, f"ropePK{oc}_0" if "{" in "ropePK{oc}_0" else "ropePK{oc}_0")
                for cc in range(2):
                    em.w(tk, f"ch{cc}_{u}" if "{" in "ch{cc}_{u}" else "ch{cc}_{u}")
                    mm = em.do(lambda e, oc=oc, u=u, cc=cc: e.matmul(
                        PK[oc][:], wproj[:, (cc * nout + oc) * 128:(cc * nout + oc + 1) * 128],
                        ch_s[:, cc * 1024 + u * 512:cc * 1024 + (u + 1) * 512],
                        start=(cc == 0), stop=(cc == 1)), inc=(cc == 1))
                tk[f"projk{oc}_{u}"] = mm
            if do_kv:
                for vc in range(2):
                    if u == 0:
                        em.w(tk, f"cevac{2 + vc}" if "{" in "cevac{2 + vc}" else "cevac{2 + vc}")
                    else:
                        em.w(tk, f"vbfe0_{vc}" if "{" in "vbfe0_{vc}" else "vbfe0_{vc}")
                    for cc in range(2):
                        em.w(tk, f"ch{cc}_{u}" if "{" in "ch{cc}_{u}" else "ch{cc}_{u}")
                        mm = em.do(lambda e, vc=vc, u=u, cc=cc: e.matmul(
                            PC[vc][:], wproj[:, (cc * 4 + 2 + vc) * 128:(cc * 4 + 3 + vc) * 128],
                            ch_s[:, cc * 1024 + u * 512:cc * 1024 + (u + 1) * 512],
                            start=(cc == 0), stop=(cc == 1)), inc=(cc == 1))
                    tk[f"projv{vc}_{u}"] = mm
        if do_kv:
            for i8 in range(8):
                u, tt = divmod(i8, 4)
                bank = PT[i8 % 2]
                em.w(tk, f"vbfe{u}_1" if "{" in "vbfe{u}_1" else "vbfe{u}_1")
                if i8 >= 2:
                    em.w(tk, f"vtmevac{i8 - 2}" if "{" in "vtmevac{i8 - 2}" else "vtmevac{i8 - 2}")
                else:
                    em.w(tk, f"vtmevac{6 + i8 % 2}" if "{" in "vtmevac{6 + i8 % 2}" else "vtmevac{6 + i8 % 2}", lag=1)
                for vc in range(2):
                    mm = em.do(lambda e, u=u, tt=tt, vc=vc, bank=bank: e.matmul(
                        bank[:, vc * 128:(vc + 1) * 128],
                        vbf[:, vc * 1024 + u * 512 + tt * 128:vc * 1024 + u * 512 + (tt + 1) * 128],
                        idb[:], start=(vc == 0), stop=(vc == 1), is_transpose=True),
                        inc=(vc == 1))
                tk[f"vT{i8}"] = mm

    @L.sec("vector")
    def _(g, em):
        for idx in range(8):
            l, u = divmod(idx, 2)
            em.w(tk, f"kvlmm{idx}" if "{" in "kvlmm{idx}" else "kvlmm{idx}")
            tk[f"kvlevac{idx}"] = em.do(lambda e, l=l, u=u, idx=idx: e.tensor_copy(
                kvl[:, l * 1024 + u * 512:l * 1024 + (u + 1) * 512], PK[idx % 2][:]))
        for idx in range(4):
            cc, u = divmod(idx, 2)
            em.w(tk, f"cmm{idx}" if "{" in "cmm{idx}" else "cmm{idx}")
            em.do(lambda e, cc=cc, u=u, idx=idx: e.tensor_copy(
                c32[:, cc * 1024 + u * 512:cc * 1024 + (u + 1) * 512], PC[idx % 2][:]))
            t2 = em.do(lambda e, cc=cc, u=u, idx=idx: e.tensor_copy(
                cbf[:, cc * 1024 + u * 512:cc * 1024 + (u + 1) * 512], PC[idx % 2][:]))
            t3 = em.do(lambda e, cc=cc, u=u, idx=idx: e.scalar_tensor_tensor(
                csq[:, cc * 1024 + u * 512:cc * 1024 + (u + 1) * 512],
                c32[:, cc * 1024 + u * 512:cc * 1024 + (u + 1) * 512], 1.0,
                PC[idx % 2][:], ALU.bypass, ALU.mult))
            tk[f"cevac{idx}"] = t3
            if cc == 1:
                tk[f"cbf{u}"] = t2
                tk[f"csq{u}"] = t3
        for u in range(2):
            em.w(tk, f"S1{u}" if "{" in "S1{u}" else "S1{u}")
            t1 = em.do(lambda e, u=u: e.tensor_scalar(
                rcol(u, 1), PS[0][0:1, :], -DCI, None, ALU.mult))
            em.wait(t1)
            t2 = em.do(lambda e, u=u: e.scalar_tensor_tensor(
                rcol(u, 2), PS[0][0:1, :], -DCI, rcol(u, 1), ALU.mult, ALU.mult))
            tk[f"psS0r{u}"] = t2
            em.wait(t2); em.w(tk, f"S2{u}" if "{" in "S2{u}" else "S2{u}")
            t3 = em.do(lambda e, u=u: e.scalar_tensor_tensor(
                rcol(u, 2), PS[1][0:1, :], DCI, rcol(u, 2), ALU.mult, ALU.subtract))
            tk[f"psS1r{u}"] = t3
            em.wait(t3)
            t3b = em.do(lambda e, u=u: e.tensor_scalar(
                rcol(u, 2), rcol(u, 2), cfg.LN_EPS, None, ALU.add))
            tk[f"vareps{u}"] = t3b
            em.w(tk, f"sqrt{u}" if "{" in "sqrt{u}" else "sqrt{u}")
            t4 = em.do(lambda e, u=u: e.reciprocal(rcol(u, 3), rcol(u, 3)))
            em.wait(t4)
            t5 = em.do(lambda e, u=u: e.tensor_copy(rbcol(u, 0), rcol(u, 1)))
            t6 = em.do(lambda e, u=u: e.tensor_copy(rbcol(u, 1), rcol(u, 3)))
            em.wait(t5); em.wait(t6)
            tk[f"rowb{u}"] = t6
        for u in range(2):
            for cc in range(2):
                em.w(tk, f"bcmu{u}" if "{" in "bcmu{u}" else "bcmu{u}"); em.w(tk, f"bcr{u}" if "{" in "bcr{u}" else "bcr{u}")
                em.w(tk, f"cevac{cc * 2 + u}" if "{" in "cevac{cc * 2 + u}" else "cevac{cc * 2 + u}")
                em.w(tk, f"projk1_{u}" if "{" in "projk1_{u}" else "projk1_{u}", lag=1)
                t1 = em.do(lambda e, cc=cc, u=u: e.scalar_tensor_tensor(
                    tA[:], c32[:, cc * 1024 + u * 512:cc * 1024 + (u + 1) * 512],
                    1.0, PS[0][:], ALU.bypass, ALU.add))
                em.wait(t1)
                t2 = em.do(lambda e, cc=cc, u=u: e.scalar_tensor_tensor(
                    tB[:], tA[:], 1.0, PS[1][:], ALU.bypass, ALU.mult))
                em.wait(t2)
                t3 = em.do(lambda e, cc=cc, u=u: e.tensor_scalar(
                    ch_s[:, cc * 1024 + u * 512:cc * 1024 + (u + 1) * 512],
                    tB[:], gb[:, cc:cc + 1], gb[:, 2 + cc:3 + cc], ALU.mult, ALU.add))
                tk[f"ch{cc}_{u}"] = t3
                if cc == 1:
                    tk[f"chPS0_{u}"] = t1
                    tk[f"chPS1_{u}"] = t2
        for u in range(2):
            em.w(tk, f"projk0_{u}" if "{" in "projk0_{u}" else "projk0_{u}"); em.w(tk, f"projk1_{u}" if "{" in "projk1_{u}" else "projk1_{u}")
            for cc in range(2):
                em.w(tk, f"cos{cc}" if "{" in "cos{cc}" else "cos{cc}"); em.w(tk, f"sin{cc}" if "{" in "sin{cc}" else "sin{cc}")
            em.w(tk, f"kqst{u}" if "{" in "kqst{u}" else "kqst{u}", lag=1)
            t1 = em.do(lambda e, u=u: e.scalar_tensor_tensor(
                tA[:], cos_s[:, u * 512:(u + 1) * 512], 1.0, PK[0][:],
                ALU.bypass, ALU.mult))
            t2 = em.do(lambda e, u=u: e.scalar_tensor_tensor(
                tB[:], sin_s[:, u * 512:(u + 1) * 512], 1.0, PK[1][:],
                ALU.bypass, ALU.mult))
            em.wait(t1); em.wait(t2)
            em.do(lambda e, u=u: e.tensor_sub(
                kq[:, u * 512:(u + 1) * 512], tA[:], tB[:]))
            t4 = em.do(lambda e, u=u: e.scalar_tensor_tensor(
                tC[:], cos_s[:, 1024 + u * 512:1024 + (u + 1) * 512], 1.0, PK[1][:],
                ALU.bypass, ALU.mult))
            tk[f"ropePK1_{u}"] = t4
            t5 = em.do(lambda e, u=u: e.scalar_tensor_tensor(
                tD[:], sin_s[:, 1024 + u * 512:1024 + (u + 1) * 512], 1.0, PK[0][:],
                ALU.bypass, ALU.mult))
            tk[f"ropePK0_{u}"] = t5
            em.wait(t4); em.wait(t5)
            t6 = em.do(lambda e, u=u: e.tensor_add(
                kq[:, 1024 + u * 512:1024 + (u + 1) * 512], tC[:], tD[:]))
            tk[f"rope{u}"] = t6
        if do_kv:
            for u in range(2):
                for vc in range(2):
                    em.w(tk, f"projv{vc}_{u}" if "{" in "projv{vc}_{u}" else "projv{vc}_{u}")
                    em.w(tk, f"vT{u * 4 + 3}" if "{" in "vT{u * 4 + 3}" else "vT{u * 4 + 3}", lag=1)
                    tk[f"vbfe{u}_{vc}"] = em.do(lambda e, u=u, vc=vc: e.tensor_copy(
                        vbf[:, vc * 1024 + u * 512:vc * 1024 + (u + 1) * 512],
                        PC[vc][:]))
            for i8 in range(8):
                em.w(tk, f"vT{i8}")
                if i8 >= 4:
                    em.w(tk, f"vtmrelay{i8 - 4}")
                else:
                    em.w(tk, f"vtmrelay{i8 + 4}", lag=1)
                tk[f"vtmevac{i8}"] = em.do(lambda e, i8=i8: e.tensor_copy(
                    vtm[:, (i8 % 4) * 256:(i8 % 4 + 1) * 256], PT[i8 % 2][:, 0:256]))

    L.close()

    sy = p.straight("sync")
    dst = p.dram[kf_name] if do_kv else p.dram[qf_name]
    vt = p.dram[vt_name] if do_kv else None

    def tokval(name, g):
        t = tk[name]
        return L.base[t.sem] + g * L.per[t.sem] + t.off

    for g in range(NG):
        if g >= 1:
            sy.wait_abs("sVE", tokval("rope1", g - 1))
            sy.wait_abs("sPE", tokval("kvlmm7", g - 1))
        for cc in range(2):
            sy.dma(lambda e, g=g, cc=cc: e.dma_start(
                out=cos_s[:, cc * 1024:(cc + 1) * 1024],
                in_=cost_[cc][:, g * 1024:(g + 1) * 1024]), ch=f"{pfx}cos{cc}")
            sy.dma(lambda e, g=g, cc=cc: e.dma_start(
                out=sin_s[:, cc * 1024:(cc + 1) * 1024],
                in_=sint_[cc][:, g * 1024:(g + 1) * 1024]), ch=f"{pfx}sin{cc}")
        for dc in range(16):
            sy.dma(lambda e, g=g, dc=dc: e.dma_start(
                out=xnf[:, dc * 1024:(dc + 1) * 1024],
                in_=xnt[g * 1024:(g + 1) * 1024, dc * 128:(dc + 1) * 128],
                transpose=True), ch=f"{pfx}xn{dc}")
        kqtoks = []
        for u in range(2):
            sy.wait_abs("sVE", tokval(f"rope{u}", g))
            kqtoks.append(sy.dma(lambda e, g=g, u=u: e.dma_start(
                out=dst[(g * 2 + u) * 128:(g * 2 + u + 1) * 128].rearrange(
                    "p (cc f) -> p cc f", f=512),
                in_=kq[:].rearrange("p (cc g2 f) -> p cc (g2 f)", cc=2, g2=2)[
                    :, :, u * 512:(u + 1) * 512]),
                ch=f"{pfx}kq{u}"))
        for t_ in kqtoks:
            sy.wait(t_)
        sy.do(lambda e: e.nop())
        sy.do(lambda e: e.nop())
        if do_kv:
            vtoks = {}
            for i8 in range(8):
                u, tt = divmod(i8, 4)
                sy.wait_abs("sVE", tokval(f"vtmevac{i8}", g))
                vtoks[i8] = sy.dma(lambda e, g=g, u=u, tt=tt, i8=i8: e.dma_start(
                    out=vt[g * 1024 + u * 512 + tt * 128:g * 1024 + u * 512 + (tt + 1) * 128],
                    in_=vtm[:, (i8 % 4) * 256:(i8 % 4 + 1) * 256]), ch=f"vt{i8 % 4}")
                if i8 >= 2:
                    sy.wait(vtoks[i8 - 2])
                    sy.do(lambda e: e.nop())
            for i8 in (6, 7):
                sy.wait(vtoks[i8])
                sy.do(lambda e: e.nop())
    return L


def emit_B(p, qf_name, kf_name, vt_name, xo_name, out_name, NKC, NQT):
    cfg = p.cfg
    QF, KF, VT = p.dram[qf_name], p.dram[kf_name], p.dram[vt_name]
    XO, OUT = p.dram[xo_name], p.dram[out_name]
    DT, DL = p.dram["dtile"], p.dram["dltab"]
    wc2lT, wsum, bo, idb, idf = (p.sb["wc2lT"], p.sb["wsum"], p.sb["bo"],
                                 p.sb["idb"], p.sb["idf"])

    def get(name, shape, dt):
        return p.sb.get(name) or p.sbuf(name, shape, dt)
    q_sb = get("b_q", [128, 1024], BF16)
    k_sb = get("b_k", [128, 4 * 1024], BF16)
    v_sb = get("b_v", [128, 16 * 256], BF16)
    d_sb = get("b_d", [128, 2048], F32)
    dl_sb = get("b_dl", [128, 8], F32)
    p_sb = get("b_p", [128, 2048], BF16)
    msk = get("b_m", [128, 2048], BF16)
    pm_sb = get("b_pm", [128, 2048], BF16)
    pt_sb = get("b_pt", [128, 2048], BF16)
    o_acc = get("b_o", [128, 4 * 256], F32)
    bst = get("b_st", [128, 4 * 8], F32)
    ob = get("b_ob", [128, 1024], BF16)
    o_fm = get("b_ofm", [128, 1024], BF16)
    z_sb = get("b_z", [128, 4 * 512], BF16)
    att = get("b_att", [128, 16 * 512], F32)
    xtm = get("b_xtm", [128, 2048], F32)
    otm = get("b_otm", [128, 2048], F32)
    PO = [p.ps["pf2"], p.ps["pf3"]]
    psS_h = p.ps["psS"]
    PZ = [psS_h[:, 0:512], psS_h[:, 512:1024]]
    PT = [p.ps["pt0"], p.ps["pt1"]]

    def col(sub, k):
        return bst[:, sub * 8 + k:sub * 8 + k + 1]

    for qt in range(NQT):
        sy = p.straight("sync")
        v0 = p.straight("vector")
        tq = sy.dma(lambda e, qt=qt: e.dma_start(
            out=q_sb[:], in_=QF[qt * 128:(qt + 1) * 128]), ch="bq")
        tdl = None
        if qt == 0:
            emit_B.dt_tok = sy.dma(lambda e: e.dma_start(out=d_sb[:], in_=DT), ch="bdt")
        tz0 = v0.do(lambda e: e.memset(o_acc[:], 0.0))
        tz1 = v0.do(lambda e: e.memset(bst[:], 0.0))
        te0 = p.straight("tensor")
        te0.wait_abs("sVE", p.C["sVE"])
        tz2 = v0.do(lambda e: e.tensor_scalar(
            bst[:].rearrange("p (s c) -> p s c", c=8)[:, :, 0:1],
            bst[:].rearrange("p (s c) -> p s c", c=8)[:, :, 0:1],
            0.0, 1.0e30, ALU.mult, ALU.add))
        v0.wait(tz2)
        tk = {"q": tq, "z": tz2}

        NKB = NKC // 4
        L = Loop(p, NKB, f"B{qt}")
        SBIG = p.ps["psS"]

        @L.sec("gpsimd")
        def _(kb_, em, tk=tk):
            em.wait(tk["q"])
            em.wait(emit_B.dt_tok)
            em.w(tk, "mg3", lag=1)
            tk["dl4"] = em.dma(lambda e, qt=qt: e.dma_start(
                out=dl_sb[:, 0:4],
                in_=DL[:, bass.ds((qt * NKB + kb_) * 4, 4)]), ch="bdl0")
            em.w(tk, "Smm3", lag=1)
            tk["kld"] = em.dma(lambda e: e.dma_start(
                out=k_sb[:].rearrange("p (kc f) -> p kc f", f=1024),
                in_=KF[bass.ds(kb_ * 512, 512)].rearrange("(kc p) f -> p kc f", p=128)),
                ch="bk0")
            em.w(tk, "omm3", lag=1)
            tk["vld"] = em.dma(lambda e: e.dma_start(
                out=v_sb[:].rearrange("p (tt c) -> p tt c", c=256),
                in_=VT.rearrange("(kb tt p) c -> kb p tt c", tt=16, p=128)[
                    bass.ds(kb_, 1)]), ch="bv0")
        @L.sec("tensor")
        def _(kb_, em, tk=tk):
            em.w(tk, "kld")
            em.w(tk, "vld")
            for sub in range(4):
                em.w(tk, f"exp{(sub + 3) % 4}", lag=1 if sub == 0 else 0)
                for q4 in range(4):
                    for dc in range(2):
                        mm = em.do(lambda e, sub=sub, q4=q4, dc=dc: e.matmul(
                            SBIG[:, q4 * 512:(q4 + 1) * 512],
                            q_sb[:, dc * 512 + sub * 128:dc * 512 + (sub + 1) * 128],
                            k_sb[:, q4 * 1024 + dc * 512:q4 * 1024 + (dc + 1) * 512],
                            start=(dc == 0), stop=(dc == 1)),
                            inc=(dc == 1 and q4 == 3))
                tk[f"Smm{sub}"] = mm
                em.w(tk, f"pm{sub}")
                for quarter in range(4):
                    pq = sub * 4 + quarter
                    em.w(tk, f"pte{(pq + 14) % 16}", lag=1 if pq < 2 else 0)
                    for j4 in range(4):
                        s4 = quarter * 4 + j4
                        mm = em.do(lambda e, s4=s4, pq=pq: e.matmul(
                            PT[pq % 2][:, (s4 % 4) * 128:(s4 % 4 + 1) * 128],
                            pm_sb[:, s4 * 128:(s4 + 1) * 128],
                            idb[:], start=(j4 == 0), stop=(j4 == 3),
                            is_transpose=True), inc=(j4 == 3))
                    tk[f"ptmm{pq}"] = mm
                em.w(tk, f"ost{(sub + 2) % 4}", lag=1 if sub < 2 else 0)
                for s4 in range(16):
                    if s4 % 4 == 0:
                        em.w(tk, f"pte{sub * 4 + s4 // 4}")
                    mm = em.do(lambda e, sub=sub, s4=s4: e.matmul(
                        PO[sub % 2][:, 0:256],
                        pt_sb[:, s4 * 128:(s4 + 1) * 128],
                        v_sb[:, s4 * 256:(s4 + 1) * 256],
                        start=(s4 == 0), stop=(s4 == 15)), inc=(s4 == 15))
                tk[f"omm{sub}"] = mm

        @L.sec("vector")
        def _(kb_, em, tk=tk):
            for sub in range(4):
                em.w(tk, f"Smm{sub}")
                t1 = em.do(lambda e, sub=sub: e.reduce_max(
                    col(sub, 2), SBIG[:], axis=AX, negate=True))
                em.wait(t1)
                t2 = em.do(lambda e, sub=sub: e.tensor_scalar(
                    col(sub, 2), col(sub, 2), 1.0 / 16, None, ALU.mult))
                em.wait(t2)
                t3 = em.do(lambda e, sub=sub: e.tensor_tensor(
                    col(sub, 3), col(sub, 0), col(sub, 2), ALU.min))
                em.wait(t3)
                t4 = em.do(lambda e, sub=sub: e.tensor_sub(
                    col(sub, 4), col(sub, 3), col(sub, 0)))
                em.wait(t4)
                t5 = em.do(lambda e, sub=sub: e.tensor_copy(col(sub, 0), col(sub, 3)))
                tk[f"nm{sub}"] = t5
                em.wait(tk["dl4"])
                tmg = em.do(lambda e, sub=sub: e.tensor_single_scalar(
                    msk[:], d_sb[:], dl_sb[:, sub:sub + 1], ALU.is_le))
                tk[f"mg{sub}"] = tmg
                em.wait(tmg)
                em.w(tk, f"exp{sub}")
                em.w(tk, f"ptmm{((sub + 3) % 4) * 4 + 3}", lag=1 if sub == 0 else 0)
                t7 = em.do(lambda e, sub=sub: e.scalar_tensor_tensor(
                    pm_sb[:], msk[:], 1.0, p_sb[:],
                    ALU.bypass, ALU.mult, accum_out=col(sub, 6)))
                tk[f"pm{sub}"] = t7
                em.w(tk, f"alpha{sub}")
                t8 = em.do(lambda e, sub=sub: e.scalar_tensor_tensor(
                    col(sub, 1), col(sub, 1), col(sub, 5), col(sub, 6),
                    ALU.mult, ALU.add))
                tk[f"l{sub}"] = t8
                em.w(tk, f"omm{sub}")
                t10 = em.do(lambda e, sub=sub: e.scalar_tensor_tensor(
                    o_acc[:, sub * 256:(sub + 1) * 256],
                    o_acc[:, sub * 256:(sub + 1) * 256], col(sub, 5),
                    PO[sub % 2][:, 0:256], ALU.mult, ALU.add))
                tk[f"ost{sub}"] = t10

        @L.sec("scalar")
        def _(kb_, em, tk=tk):
            for sub in range(4):
                em.w(tk, f"nm{sub}")
                em.w(tk, f"pm{(sub + 3) % 4}", lag=1 if sub == 0 else 0)
                t1 = em.do(lambda e, sub=sub: e.activation(
                    p_sb[:], SBIG[:], AF.Exp, bias=col(sub, 3), scale=1.0 / 16))
                tk[f"exp{sub}"] = t1
                t2 = em.do(lambda e, sub=sub: e.activation(
                    col(sub, 5), col(sub, 4), AF.Exp))
                tk[f"alpha{sub}"] = t2
                for quarter in range(4):
                    pq = sub * 4 + quarter
                    em.w(tk, f"ptmm{pq}")
                    if quarter == 0:
                        em.w(tk, f"omm{(sub + 3) % 4}", lag=1 if sub == 0 else 0)
                    t9 = em.do(lambda e, quarter=quarter, pq=pq: e.activation(
                        pt_sb[:, quarter * 512:(quarter + 1) * 512],
                        PT[pq % 2][:], AF.Copy))
                    tk[f"pte{pq}"] = t9

        L.close()





        v = p.straight("vector")
        sc = p.straight("scalar")
        te = p.straight("tensor")
        sy2 = p.straight("sync")
        g2 = p.straight("gpsimd")
        Ltk = tk
        for sub in range(4):
            v.wait(Tok("sVE", L.final["sVE"], 0))
            t = v.do(lambda e, sub=sub: e.reciprocal(col(sub, 7), col(sub, 1)))
            v.wait(t)
            t = v.do(lambda e, sub=sub: e.tensor_scalar(
                ob[:, sub * 256:(sub + 1) * 256],
                o_acc[:, sub * 256:(sub + 1) * 256], col(sub, 7), None, ALU.mult))
            v.wait(t)
        tob = t
        te.wait(tob)
        for sub in range(4):
            for dv in range(2):
                tt_ = te.do(lambda e, sub=sub, dv=dv: e.matmul(
                    p.ps["pt0"][:, dv * 128:(dv + 1) * 128],
                    ob[:, sub * 256 + dv * 128:sub * 256 + (dv + 1) * 128],
                    idb[:], start=(dv == 0), stop=(dv == 1), is_transpose=True),
                    inc=(dv == 1))
            v.wait(tt_)
            t = v.do(lambda e, sub=sub: e.tensor_copy(
                o_fm[:].rearrange("p (dv f) -> p dv f", dv=2)[:, :, sub * 128:(sub + 1) * 128],
                p.ps["pt0"][:].rearrange("p (dv f) -> p dv f", f=128)[:, 0:2, :]))
            v.wait(t)
            te.wait(t)
        tofm = t
        for zc in range(4):
            te.wait(tofm)
            for dv in range(2):
                mm = te.do(lambda e, zc=zc, dv=dv: e.matmul(
                    PZ[zc % 2][:], wc2lT[:, (dv * 4 + zc) * 128:(dv * 4 + zc + 1) * 128],
                    o_fm[:, dv * 512:(dv + 1) * 512],
                    start=(dv == 0), stop=(dv == 1)), inc=(dv == 1))
            v.wait(mm)
            t = v.do(lambda e, zc=zc: e.tensor_copy(
                z_sb[:, zc * 512:(zc + 1) * 512], PZ[zc % 2][:]))
            v.wait(t)
            te.wait(t)
        tz = t
        for j in range(16):
            bank = PZ[j % 2]
            for zc in range(4):
                mm = te.do(lambda e, j=j, zc=zc, bank=bank: e.matmul(
                    bank[:], wsum[:, (zc * 16 + j) * 128:(zc * 16 + j + 1) * 128],
                    z_sb[:, zc * 512:(zc + 1) * 512],
                    start=(zc == 0), stop=(zc == 3)), inc=(zc == 3))
            v.wait(mm)
            t = v.do(lambda e, j=j: e.tensor_scalar(
                att[:, j * 512:(j + 1) * 512], bank[:] if False else PZ[j % 2][:],
                bo[:, j:j + 1], None, ALU.add))
            v.wait(t)
            te.wait(t)
        tatt = t
        for sub in range(4):
            tx = sy2.dma(lambda e, qt=qt, sub=sub: e.dma_start(
                out=xtm[:], in_=XO[(qt * 512 + sub * 128) * 1:(qt * 512 + sub * 128 + 128),
                                   :]), ch="bx")
            for jg in range(4):
                bank = psS_h[:, 1024:1536] if jg % 2 == 0 else psS_h[:, 1536:2048]
                te.wait(tatt)
                te.wait(Tok("sVE", p.C["sVE"], 0))
                for jj in range(4):
                    j = jg * 4 + jj
                    mm = te.do(lambda e, j=j, jj=jj, sub=sub, bank=bank: e.matmul(
                        bank[:, jj * 128:(jj + 1) * 128],
                        att[:, j * 512 + sub * 128:j * 512 + (sub + 1) * 128],
                        idf[:], start=(jj == 0), stop=(jj == 3), is_transpose=True),
                        inc=(jj == 3))
                v.wait(mm)
                v.wait(tx)
                t = v.do(lambda e, jg=jg, bank=bank: e.scalar_tensor_tensor(
                    otm[:, jg * 512:(jg + 1) * 512], xtm[:, jg * 512:(jg + 1) * 512],
                    1.0, bank[:], ALU.bypass, ALU.add))
                v.wait(t)
            g2.wait(t)
            tst = g2.dma(lambda e, qt=qt, sub=sub: e.dma_start(
                out=OUT[qt * 512 + sub * 128:qt * 512 + sub * 128 + 128, :],
                in_=otm[:]), ch="bout")
            g2.wait(tst)
            v.wait(tst)
            sy2.wait(tst)
    return None


def emit_C(p, xno_name, out_name, TO):
    cfg = p.cfg
    NCG = TO // 1024
    XNO, OUT = p.dram[xno_name], p.dram[out_name]
    gu_t = p.dram["gu_t"]
    d_t = p.dram["d_t"]
    shg_t = p.dram["shg_t"]
    shu_t = p.dram["shu_t"]
    shd_t = p.dram["shd_t"]
    rt_t = p.dram["rt_t"]
    oh_t = p.dram["oh_t"]
    idf = p.sb["idf"]

    def get(name, shape, dt):
        return p.sb.get(name) or p.sbuf(name, shape, dt)
    acc = get("c_acc", [128, 16 * 1024], F32)
    xnf = get("c_xnf", [128, 16 * 1024], BF16)
    ch_ = get("c_h", [128, 16 * 1024], BF16)
    gu_s = [get(f"c_gu{s}", [128, 32 * 128], BF16) for s in range(2)]
    dd_s = [get(f"c_dd{s}", [128, 16 * 128], BF16) for s in range(2)]
    sh_s = [gu_s[0][:, 0:2048], gu_s[1][:, 0:2048],
            gu_s[0][:, 2048:4096], gu_s[1][:, 2048:4096]]
    sd_s = dd_s
    be = get("c_be", [128, 2 * 1024], BF16)
    oh = get("c_oh", [16, 2 * 128], F32)
    rt = get("c_rt", [128, 16 * 16], BF16)
    lgf = get("c_lgf", [16, 1024], F32)
    lgt = get("c_lgt", [128, 8 * 16], F32)
    wtm = get("c_wtm", [128, 16], F32)
    wf = get("c_wf", [16, 1024], F32)
    rst = get("c_rst", [128, 12], F32)
    tS = get("c_tS", [128, 512], F32)
    tT = get("c_tT", [128, 512], F32)
    tU = get("c_tU", [128, 512], F32)
    otm = get("c_otm", [128, 512], F32)
    PA, PB, PC_, PD, PE_, PF_ = (p.ps["pf0"], p.ps["pf1"], p.ps["pf2"],
                                 p.ps["pf3"], p.ps["pf4"], p.ps["pf5"])

    sy0 = p.straight("gpsimd")
    trt = sy0.dma(lambda e: e.dma_start(
        out=rt[:].rearrange("p (c f) -> p c f", f=16),
        in_=rt_t.rearrange("c p f -> p c f")), ch="crt")

    for g in range(NCG):
        t0 = g * 1024
        syg = p.straight("sync")
        syg.wait_abs("sPE", p.C["sPE"])
        xnf_tok = {}
        for dc in range(16):
            xnf_tok[dc] = syg.dma(lambda e, t0=t0, dc=dc: e.dma_start(
                out=xnf[:, dc * 1024:(dc + 1) * 1024],
                in_=XNO[t0:t0 + 1024, dc * 128:(dc + 1) * 128],
                transpose=True), ch=f"xn{dc}")

        te = p.straight("tensor")
        v = p.straight("vector")
        sc = p.straight("scalar")
        te.wait(trt)
        te.wait_abs("sVE", p.C["sVE"])
        for u in range(2):
            for dc in range(16):
                te.wait(xnf_tok[dc])
                mm = te.do(lambda e, u=u, dc=dc: e.matmul(
                    PA[0:16, :], rt[:, dc * 16:(dc + 1) * 16],
                    xnf[:, dc * 1024 + u * 512:dc * 1024 + (u + 1) * 512],
                    start=(dc == 0), stop=(dc == 15)), inc=(dc == 15))
            v.wait(mm)
            tcp = v.do(lambda e, u=u: e.tensor_copy(lgf[:, u * 512:(u + 1) * 512],
                                                    PA[0:16, :]))
            v.wait(tcp)
            te.wait(tcp)
        for sub in range(8):
            tmm = te.do(lambda e, sub=sub: e.matmul(
                PB[0:128, 0:16], lgf[:, sub * 128:(sub + 1) * 128], idf[0:16, 0:16],
                start=True, stop=True, is_transpose=True))
            v.wait(tmm)
            tcp = v.do(lambda e, sub=sub: e.tensor_copy(
                lgt[:, sub * 16:(sub + 1) * 16], PB[0:128, 0:16]))
            v.wait(tcp)
            te.wait(tcp)
        for sub in range(8):
            sl = lgt[:, sub * 16:(sub + 1) * 16]
            for fn in (
                lambda e, sl=sl: e.reduce_max(rst[:, 0:1], sl, axis=AX),
                lambda e, sl=sl: e.tensor_single_scalar(wtm[:], sl, rst[:, 0:1], ALU.is_ge),
                lambda e, sl=sl: e.scalar_tensor_tensor(
                    tS[:, 0:16], wtm[:], -1.0e30, sl, ALU.mult, ALU.add),
                lambda e: e.reduce_max(rst[:, 1:2], tS[:, 0:16], axis=AX),
                lambda e: e.tensor_single_scalar(tS[:, 16:32], tS[:, 0:16],
                                                 rst[:, 1:2], ALU.is_ge),
                lambda e: e.tensor_sub(rst[:, 2:3], rst[:, 0:1], rst[:, 1:2]),
            ):
                tlast = v.do(fn)
                v.wait(tlast)
            sc.wait(tlast)
            t7 = sc.do(lambda e: e.activation(rst[:, 3:4], rst[:, 2:3], AF.Tanh,
                                              scale=0.5))
            sc.wait(t7)
            v.wait(t7)
            for fn in (
                lambda e: e.tensor_scalar(rst[:, 5:6], rst[:, 3:4], 0.5, 0.5,
                                          ALU.mult, ALU.add),
                lambda e: e.tensor_scalar(rst[:, 6:7], rst[:, 5:6], -1.0, 1.0,
                                          ALU.mult, ALU.add),
                lambda e: e.tensor_scalar(wtm[:], wtm[:], rst[:, 5:6], None, ALU.mult),
                lambda e: e.scalar_tensor_tensor(
                    wtm[:], tS[:, 16:32], rst[:, 6:7], wtm[:], ALU.mult, ALU.add),
            ):
                tlast = v.do(fn)
                v.wait(tlast)
            te.wait(tlast)
            tmm = te.do(lambda e: e.matmul(PB[0:16, 0:128], wtm[:], idf[:],
                                           start=True, stop=True, is_transpose=True))
            v.wait(tmm)
            tcp = v.do(lambda e, sub=sub: e.tensor_copy(
                wf[:, sub * 128:(sub + 1) * 128], PB[0:16, 0:128]))
            v.wait(tcp)
            te.wait(tcp)
        rtr_pe_count = p.C["sPE"]
        rtr_ve_count = p.C["sVE"]

        pe_gate_h = p.C["sPE"]
        Lm = Loop(p, 8, f"Csm{g}")
        km = {}

        @Lm.sec("gpsimd")
        def _(mh, em, km=km, pe_gate_h=pe_gate_h):
            em.wait_abs("sPE", pe_gate_h)
            for s in range(2):
                em.w(km, f"hmul{s}", lag=1)
                km[f"wg{s}"] = em.dma(lambda e, s=s: e.dma_start(
                    out=sh_s[s].rearrange("p (n f) -> p n f", f=128),
                    in_=shg_t[bass.ds(mh * 2 + s, 1)].rearrange(
                        "a c p f -> p (a c) f")), ch=f"cgu{s}")
                em.w(km, f"hmul{s}", lag=1)
                km[f"wu{s}"] = em.dma(lambda e, s=s: e.dma_start(
                    out=sh_s[2 + s].rearrange("p (n f) -> p n f", f=128),
                    in_=shu_t[bass.ds(mh * 2 + s, 1)].rearrange(
                        "a c p f -> p (a c) f")), ch=f"cshu{s}")

        @Lm.sec("tensor")
        def _(mh, em, km=km, rtr_ve_count=rtr_ve_count):
            em.wait_abs("sVE", rtr_ve_count)
            for s in range(2):
                for u in range(2):
                    k4 = s * 2 + u
                    ps_, pu_ = divmod((k4 + 3) % 4, 2)
                    em.w(km, f"silu{ps_}{pu_}", lag=1 if k4 == 0 else 0)
                    em.w(km, f"wg{s}")
                    for dc in range(16):
                        mm = em.do(lambda e, s=s, u=u, dc=dc: e.matmul(
                            PA[:], sh_s[s][:, dc * 128:(dc + 1) * 128],
                            xnf[:, dc * 1024 + u * 512:dc * 1024 + (u + 1) * 512],
                            start=(dc == 0), stop=(dc == 15)), inc=(dc == 15))
                    km[f"gmm{s}{u}"] = mm
                    em.w(km, f"tmul{ps_}{pu_}", lag=1 if k4 == 0 else 0)
                    em.w(km, f"wu{s}")
                    for dc in range(16):
                        mm = em.do(lambda e, s=s, u=u, dc=dc: e.matmul(
                            PB[:], sh_s[2 + s][:, dc * 128:(dc + 1) * 128],
                            xnf[:, dc * 1024 + u * 512:dc * 1024 + (u + 1) * 512],
                            start=(dc == 0), stop=(dc == 15)), inc=(dc == 15))
                    km[f"umm{s}{u}"] = mm

        @Lm.sec("scalar")
        def _(mh, em, km=km):
            for s in range(2):
                for u in range(2):
                    em.w(km, f"tmul{(s + 1) % 2}{u}", lag=1 if s == 0 else 0)
                    em.w(km, f"gmm{s}{u}")
                    km[f"silu{s}{u}"] = em.do(lambda e, s=s, u=u: e.activation(
                        tS[:] if u == 0 else tT[:], PA[:], AF.Silu))

        @Lm.sec("vector")
        def _(mh, em, km=km, pe_gate_h=pe_gate_h):
            for s in range(2):
                for u in range(2):
                    em.w(km, f"silu{s}{u}")
                    em.w(km, f"umm{s}{u}")
                    em.wait_abs("sPE", pe_gate_h)
                    km[f"tmul{s}{u}"] = em.do(lambda e, s=s, u=u: e.scalar_tensor_tensor(
                        ch_[:, bass.ds((0 if True else 0) + 0, 1)][0:0] if False else
                        ch_[:, bass.ds((2 * 0 + 0) * 0, 512)] if False else
                        ch_[:, bass.ds(((mh * 2 + s) * 1024 + u * 512), 512)],
                        (tS if u == 0 else tT)[:], 1.0, PB[:],
                        ALU.bypass, ALU.mult))
                km[f"hmul{s}"] = km[f"tmul{s}1"]

        Lm.close()

        Lj = Loop(p, 8, f"Csj{g}")
        kj = {}

        @Lj.sec("gpsimd")
        def _(jh, em, kj=kj):
            for s in range(2):
                em.w(kj, f"dnev{s}1", lag=1)
                kj[f"wd{s}"] = em.dma(lambda e, s=s: e.dma_start(
                    out=sd_s[s][:].rearrange("p (n f) -> p n f", f=128),
                    in_=shd_t[bass.ds(jh * 2 + s, 1)].rearrange(
                        "a c p f -> p (a c) f")), ch=f"cdd{s}")

        @Lj.sec("tensor")
        def _(jh, em, kj=kj):
            for s in range(2):
                bank = PC_ if s == 0 else PD
                for u in range(2):
                    if u == 1:
                        em.w(kj, f"dnev{s}0")
                    else:
                        em.w(kj, f"dnev{s}1", lag=1)
                    em.w(kj, f"wd{s}")
                    for m in range(16):
                        mm = em.do(lambda e, s=s, u=u, m=m, bank=bank: e.matmul(
                            bank[:], sd_s[s][:, m * 128:(m + 1) * 128],
                            ch_[:, m * 1024 + u * 512:m * 1024 + (u + 1) * 512],
                            start=(m == 0), stop=(m == 15)), inc=(m == 15))
                    kj[f"dnmm{s}{u}"] = mm

        @Lj.sec("vector")
        def _(jh, em, kj=kj):
            for s in range(2):
                bank = PC_ if s == 0 else PD
                for u in range(2):
                    em.w(kj, f"dnmm{s}{u}")
                    kj[f"dnev{s}{u}"] = em.do(lambda e, s=s, u=u, bank=bank: e.tensor_copy(
                        acc[:, bass.ds((jh * 2 + s) * 1024 + u * 512, 512)], bank[:]))

        Lj.close()

        Lp = Loop(p, 8, f"Cp{g}")
        kp = {}
        shj_pe_final = Lj.final["sPE"]

        @Lp.sec("sync")
        def _(p8, em, kp=kp):
            for mem in range(2):
                em.w(kp, f"beev{mem}", lag=1)
                kp[f"ohld{mem}"] = em.dma(lambda e, mem=mem: e.dma_start(
                    out=oh[:, mem * 128:(mem + 1) * 128],
                    in_=oh_t[bass.ds(p8 * 2 + mem, 1)]), ch=f"coh{mem}")

        @Lp.sec("gpsimd")
        def _(p8, em, kp=kp, shj_pe_final=shj_pe_final):
            em.wait_abs("sPE", shj_pe_final)
            for st in range(16):
                mem, m = divmod(st, 8)
                sl = st % 2
                em.w(kp, f"humm{(st + 14) % 16}", lag=1 if st < 2 else 0)
                kp[f"guld{st}"] = em.dma(lambda e, st=st, sl=sl: e.dma_start(
                    out=gu_s[sl][:].rearrange("p (n f) -> p n f", f=128),
                    in_=gu_t.rearrange("e m a c p f -> (e m) a c p f")[
                        bass.ds(p8 * 16 + st, 1)].rearrange(
                        "x a c p f -> p (x a c) f")), ch=f"cgu{sl}")
            for j in range(16):
                sl = j % 2
                em.w(kp, f"dnmm{(j + 14) % 16}", lag=1 if j < 2 else 0)
                kp[f"dld{j}"] = em.dma(lambda e, j=j, sl=sl: e.dma_start(
                    out=dd_s[sl][:].rearrange("p (n f) -> p n f", f=128),
                    in_=d_t[bass.ds(p8, 1), j].rearrange(
                        "x a b p f -> p (x a b) f")), ch=f"cdd{sl}")

        @Lp.sec("tensor")
        def _(p8, em, kp=kp, shj_pe_final=shj_pe_final):
            em.wait_abs("sPE", shj_pe_final)
            for mem in range(2):
                em.w(kp, f"ohld{mem}")
                for u in range(2):
                    em.w(kp, f"beevu{mem}{u}", lag=1)
                    if u == 1:
                        em.w(kp, f"beevu{mem}0")
                    elif mem == 1:
                        em.w(kp, "beevu01" if False else "beevu0" + "1")
                    mm = em.do(lambda e, mem=mem, u=u: e.matmul(
                        PE_[:], oh[:, mem * 128:(mem + 1) * 128],
                        wf[:, u * 512:(u + 1) * 512],
                        start=True, stop=True), inc=True)
                    kp[f"bemm{mem}{u}"] = mm
            for st in range(16):
                mem, m = divmod(st, 8)
                sl = st % 2
                em.w(kp, f"guld{st}")
                for u in range(2):
                    if u == 1:
                        em.w(kp, f"esilu{st}0")
                    else:
                        em.w(kp, f"esilu{(st + 15) % 16}1", lag=1 if st < 1 else 0)
                    for dc in range(16):
                        mm = em.do(lambda e, sl=sl, u=u, dc=dc: e.matmul(
                            PA[:], gu_s[sl][:, dc * 128:(dc + 1) * 128],
                            xnf[:, dc * 1024 + u * 512:dc * 1024 + (u + 1) * 512],
                            start=(dc == 0), stop=(dc == 15)), inc=(dc == 15))
                    kp[f"egmm{st}{u}"] = mm
                    if u == 1:
                        em.w(kp, f"etmul{st}0")
                    else:
                        em.w(kp, f"etmul{(st + 15) % 16}1", lag=1 if st < 1 else 0)
                    for dc in range(16):
                        mm = em.do(lambda e, sl=sl, u=u, dc=dc: e.matmul(
                            PB[:], gu_s[sl][:, (16 + dc) * 128:(17 + dc) * 128],
                            xnf[:, dc * 1024 + u * 512:dc * 1024 + (u + 1) * 512],
                            start=(dc == 0), stop=(dc == 15)), inc=(dc == 15))
                    kp[f"eumm{st}{u}"] = mm
                kp[f"humm{st}"] = kp[f"eumm{st}1"]
            for j in range(16):
                sl = j % 2
                bank = PC_ if sl == 0 else PD
                em.w(kp, f"dld{j}")
                for u in range(2):
                    if u == 1:
                        em.w(kp, f"dnev{j}0")
                    else:
                        em.w(kp, f"dnev{(j + 14) % 16}1", lag=1 if j < 2 else 0)
                    for st in range(16):
                        em.w(kp, f"ehmul{st}{u}") if j == 0 else None
                        mm = em.do(lambda e, sl=sl, u=u, st=st, bank=bank: e.matmul(
                            bank[:], dd_s[sl][:, st * 128:(st + 1) * 128],
                            ch_[:, st * 1024 + u * 512:st * 1024 + (u + 1) * 512],
                            start=(st == 0), stop=(st == 15)), inc=(st == 15))
                    kp[f"ednmm{j}{u}"] = mm
                kp[f"dnmm{j}"] = kp[f"ednmm{j}1"]

        @Lp.sec("scalar")
        def _(p8, em, kp=kp):
            for st in range(16):
                for u in range(2):
                    em.w(kp, f"egmm{st}{u}")
                    em.w(kp, f"etmul{(st + 15) % 16}{u}", lag=1 if st < 1 else 0)
                    kp[f"esilu{st}{u}"] = em.do(lambda e, u=u: e.activation(
                        tS[:] if u == 0 else tT[:], PA[:], AF.Silu))

        @Lp.sec("vector")
        def _(p8, em, kp=kp):
            for mem in range(2):
                for u in range(2):
                    em.w(kp, f"bemm{mem}{u}")
                    tb = em.do(lambda e, mem=mem, u=u: e.tensor_copy(
                        be[:, mem * 1024 + u * 512:mem * 1024 + (u + 1) * 512],
                        PE_[:]))
                    kp[f"beevu{mem}{u}"] = tb
                kp[f"beev{mem}"] = tb
            for st in range(16):
                for u in range(2):
                    em.w(kp, f"esilu{st}{u}")
                    em.w(kp, f"eumm{st}{u}")
                    kp[f"etmul{st}{u}"] = em.do(lambda e, u=u: e.scalar_tensor_tensor(
                        tU[:], (tS if u == 0 else tT)[:], 1.0, PB[:],
                        ALU.bypass, ALU.mult))
                    em.w(kp, "dnmm15", lag=1)
                    mem = st // 8
                    kp[f"ehmul{st}{u}"] = em.do(lambda e, st=st, u=u, mem=mem: e.tensor_mul(
                        ch_[:, st * 1024 + u * 512:st * 1024 + (u + 1) * 512],
                        tU[:], be[:, mem * 1024 + u * 512:mem * 1024 + (u + 1) * 512]))
            for j in range(16):
                bank = PC_ if j % 2 == 0 else PD
                for u in range(2):
                    em.w(kp, f"ednmm{j}{u}")
                    kp[f"dnev{j}{u}"] = em.do(lambda e, j=j, u=u, bank=bank: e.scalar_tensor_tensor(
                        acc[:, bass.ds(j * 1024 + u * 512, 512)],
                        acc[:, bass.ds(j * 1024 + u * 512, 512)], 1.0, bank[:],
                        ALU.bypass, ALU.add))

        Lp.close()

        acc_done_ve = p.C["sVE"]
        te2 = p.straight("tensor")
        v2 = p.straight("vector")
        g2 = p.straight("gpsimd")
        te2.wait_abs("sVE", acc_done_ve)
        prev_store = None
        for sub in range(8):
            for jg in range(4):
                bank = PE_ if (sub * 4 + jg) % 2 == 0 else PF_
                for jj in range(4):
                    j = jg * 4 + jj
                    mm = te2.do(lambda e, j=j, jj=jj, sub=sub, bank=bank: e.matmul(
                        bank[:, jj * 128:(jj + 1) * 128],
                        acc[:, j * 1024 + sub * 128:j * 1024 + (sub + 1) * 128],
                        idf[:], start=(jj == 0), stop=(jj == 3), is_transpose=True),
                        inc=(jj == 3))
                v2.wait(mm)
                if prev_store is not None:
                    v2.wait(prev_store)
                tev = v2.do(lambda e, bank=bank: e.tensor_copy(otm[:], bank[:]))
                v2.wait(tev)
                te2.wait(tev)
                g2.wait(tev)
                prev_store = g2.dma(lambda e, t0=t0, sub=sub, jg=jg: e.dma_start(
                    out=OUT[t0 + sub * 128:t0 + sub * 128 + 128,
                            jg * 512:(jg + 1) * 512],
                    in_=otm[:], accum_op=ALU.add), ch="cacc")
        g2.wait(prev_store)
    return None


import concourse.bass_core_rust as _bcr
IOX = _bcr.IndirectOffsetOnAxis
I32 = mybir.dt.int32


def emit_C2(p, xno_name, out_name, TO):
    import os as _os
    _stop = _os.environ.get("C2STOP", "full")
    cfg = p.cfg
    NPAIR = TO // 2048
    CAP = 512
    NSLOT = 16 * CAP
    PARK = NPAIR * NSLOT
    XNO, OUT = p.dram[xno_name], p.dram[out_name]
    XS, DN = p.dram["XS"], p.dram["DN"]
    gu_t, d2_t = p.dram["gu_t"], p.dram["d2_t"]
    shg_t, shu_t, shd_t = p.dram["shg_t"], p.dram["shu_t"], p.dram["shd_t"]
    rt_t = p.dram["rt_t"]
    idf, idb = p.sb["idf"], p.sb["idb"]

    def get(name, shape, dt):
        return p.sb.get(name) or p.sbuf(name, shape, dt)
    acc = get("c_acc", [128, 16 * 1024], F32)
    xnf = get("c_xnf", [128, 16 * 1024], BF16)
    cbig = get("c_big", [128, 16 * 1024], BF16)
    gu_s = [get(f"c_gu{s}", [128, 32 * 128], BF16) for s in range(2)]
    dd_s = [get(f"c_dd{s}", [128, 16 * 128], BF16) for s in range(2)]
    sh_s = [gu_s[0][:, 0:2048], gu_s[1][:, 0:2048],
            gu_s[0][:, 2048:4096], gu_s[1][:, 2048:4096]]
    sd_s = dd_s
    che = get("c_he", [128, 8 * 512], BF16)
    dntm = get("c_dntm", [128, 2 * 2048], BF16)
    otm = get("c_otm", [128, 2 * 512], F32)
    tS = get("c_tS", [128, 512], F32)
    tT = get("c_tT", [128, 512], F32)
    rt = get("c_rt", [128, 16 * 16], BF16)
    lgf = get("c_lgf", [16, 1024], F32)
    lgt = get("c_lgt", [128, 8 * 16], F32)
    e1t = get("c_e1t", [128, 16 * 16], F32)
    e2t = get("c_e2t", [128, 16 * 16], F32)
    wcol = get("c_wcol", [128, 32], F32)
    sbtm = get("c_sbtm", [128, 16], F32)
    scol = get("c_scol", [128, 4], F32)
    sidx = get("c_sidx", [128, 32], I32)
    ecap = get("c_ecap", [16, 1], F32)
    rst = get("c_rst", [128, 12], F32)
    zro = acc[0:16, 0:2048]
    afm = acc[0:16, 2048:4096]
    xsf = cbig
    g1v = cbig[:, 8192:10240]
    g2v = cbig[:, 10240:12288]
    xrow = cbig[:, 12288:14336]
    PA, PB = p.ps["pf0"], p.ps["pf1"]
    PC_, PD = p.ps["pf2"], p.ps["pf3"]
    PE_, PF_ = p.ps["pf4"], p.ps["pf5"]
    PT = [p.ps["pt0"], p.ps["pt1"]]

    p.sem("qG_cg1")
    p.sem("qG_cg2")
    sy0 = p.straight("gpsimd")
    trt = sy0.dma(lambda e: e.dma_start(
        out=rt[:].rearrange("p (c f) -> p c f", f=16),
        in_=rt_t.rearrange("c p f -> p c f")), ch="crt")
    tec = sy0.dma(lambda e: e.dma_start(out=ecap[:], in_=p.dram["ecap_h"]),
                  ch="cxr")

    for gp in range(NPAIR):
        if gp and _os.environ.get("C2BAR"):
            p.barrier()
        t0p = gp * 2048
        sbase = gp * NSLOT
        for gh in range(2):
            t0 = t0p + gh * 1024
            syg = p.straight("sync")
            syg.wait_abs("sPE", p.C["sPE"])
            xnf_tok = {}
            for dc in range(16):
                xnf_tok[dc] = syg.dma(lambda e, t0=t0, dc=dc: e.dma_start(
                    out=xnf[:, dc * 1024:(dc + 1) * 1024],
                    in_=XNO[t0:t0 + 1024, dc * 128:(dc + 1) * 128],
                    transpose=True), ch=f"xn{dc}")
            te = p.straight("tensor")
            v = p.straight("vector")
            sc = p.straight("scalar")
            te.wait(trt)
            te.wait_abs("sVE", p.C["sVE"])
            v.wait_abs("sPE", p.C["sPE"])
            for u in range(2):
                for dc in range(16):
                    te.wait(xnf_tok[dc])
                    mm = te.do(lambda e, u=u, dc=dc: e.matmul(
                        PA[0:16, :], rt[:, dc * 16:(dc + 1) * 16],
                        xnf[:, dc * 1024 + u * 512:dc * 1024 + (u + 1) * 512],
                        start=(dc == 0), stop=(dc == 15)), inc=(dc == 15))
                v.wait(mm)
                tcp = v.do(lambda e, u=u: e.tensor_copy(lgf[:, u * 512:(u + 1) * 512],
                                                        PA[0:16, :]))
                v.wait(tcp)
                te.wait(tcp)
            for sub in range(8):
                tmm = te.do(lambda e, sub=sub: e.matmul(
                    PB[0:128, 0:16], lgf[:, sub * 128:(sub + 1) * 128], idf[0:16, 0:16],
                    start=True, stop=True, is_transpose=True))
                v.wait(tmm)
                tcp = v.do(lambda e, sub=sub: e.tensor_copy(
                    lgt[:, sub * 16:(sub + 1) * 16], PB[0:128, 0:16]))
                v.wait(tcp)
                te.wait(tcp)
            for sub in range(8):
                s16 = gh * 8 + sub
                sl = lgt[:, sub * 16:(sub + 1) * 16]
                for fn in (
                    lambda e, sl=sl: e.reduce_max(rst[:, 0:1], sl, axis=AX),
                    lambda e, sl=sl, s16=s16: e.tensor_single_scalar(
                        e1t[:, s16 * 16:(s16 + 1) * 16], sl, rst[:, 0:1], ALU.is_ge),
                    lambda e, sl=sl, s16=s16: e.scalar_tensor_tensor(
                        tS[:, 0:16], e1t[:, s16 * 16:(s16 + 1) * 16], -1.0e30, sl,
                        ALU.mult, ALU.add),
                    lambda e: e.reduce_max(rst[:, 1:2], tS[:, 0:16], axis=AX),
                    lambda e, s16=s16: e.tensor_single_scalar(
                        e2t[:, s16 * 16:(s16 + 1) * 16], tS[:, 0:16],
                        rst[:, 1:2], ALU.is_ge),
                    lambda e: e.tensor_sub(rst[:, 2:3], rst[:, 0:1], rst[:, 1:2]),
                ):
                    tlast = v.do(fn)
                    v.wait(tlast)
                sc.wait(tlast)
                t7 = sc.do(lambda e: e.activation(rst[:, 3:4], rst[:, 2:3], AF.Tanh,
                                                  scale=0.5))
                sc.wait(t7)
                v.wait(t7)
                for fn in (
                    lambda e, s16=s16: e.tensor_scalar(
                        wcol[:, s16:s16 + 1], rst[:, 3:4], 0.5, 0.5, ALU.mult, ALU.add),
                    lambda e, s16=s16: e.tensor_scalar(
                        wcol[:, 16 + s16:17 + s16], wcol[:, s16:s16 + 1], -1.0, 1.0,
                        ALU.mult, ALU.add),
                    lambda e, s16=s16: e.tensor_add(
                        tS[:, 16:32], e1t[:, s16 * 16:(s16 + 1) * 16],
                        e2t[:, s16 * 16:(s16 + 1) * 16]),
                ):
                    tlast = v.do(fn)
                    v.wait(tlast)
                te.wait(tlast)
                tmm = te.do(lambda e: e.matmul(PB[0:16, 0:128], tS[:, 16:32], idf[:],
                                               start=True, stop=True, is_transpose=True))
                v.wait(tmm)
                tcp = v.do(lambda e, s16=s16: e.tensor_copy(
                    afm[:, s16 * 128:(s16 + 1) * 128], PB[0:16, 0:128]))
                v.wait(tcp)
                te.wait(tcp)

        if _stop == "r":
            continue
        v2 = p.straight("vector")
        te2 = p.straight("tensor")
        g2s = p.straight("gpsimd")
        v2.wait_abs("sPE", p.C["sPE"])
        v2.wait_abs("qG_cg1", p.C.get("qG_cg1", 0))
        v2.wait_abs("qG_cg2", p.C.get("qG_cg2", 0))
        cum = acc[0:16, 4096:6144]
        tz = v2.do(lambda e: e.memset(acc[0:16, 0:2048], 0.0))
        v2.wait(tz)
        tsc = v2.do(lambda e, cum=cum: e.tensor_tensor_scan(
            cum, afm, zro, 0.0, ALU.add, ALU.add))
        v2.wait(tsc)
        tex = v2.do(lambda e, cum=cum: e.tensor_sub(cum, cum, afm))
        v2.wait(tex)
        v2.wait(tec)
        tsb = v2.do(lambda e, cum=cum: e.scalar_tensor_tensor(
            cum, cum, ecap[:], cum, ALU.add, ALU.bypass))
        v2.wait(tsb)
        tcl = v2.do(lambda e, cum=cum, sbase=sbase: e.tensor_scalar(
            cum, cum, float(sbase), float(PARK + 127), ALU.add, ALU.min))
        v2.wait(tcl)
        te2.wait(tcl)
        prev_sc = None
        for s16 in range(16):
            tmm = te2.do(lambda e, s16=s16: e.matmul(
                PB[0:128, 0:16], acc[0:16, 4096 + s16 * 128:4096 + (s16 + 1) * 128],
                idf[0:16, 0:16], start=True, stop=True, is_transpose=True))
            v2.wait(tmm)
            tsb2 = v2.do(lambda e: e.tensor_copy(sbtm[:], PB[0:128, 0:16]))
            v2.wait(tsb2)
            te2.wait(tsb2)
            t4 = None
            for k2, et in ((0, e1t), (1, e2t)):
                t1 = v2.do(lambda e, s16=s16, et=et: e.tensor_mul(
                    lgt[:, 0:16], et[:, s16 * 16:(s16 + 1) * 16], sbtm[:]))
                v2.wait(t1)
                t2 = v2.do(lambda e, k2=k2: e.reduce_sum(
                    scol[:, k2:k2 + 1], lgt[:, 0:16], axis=AX))
                v2.wait(t2)
                t4 = v2.do(lambda e, s16=s16, k2=k2: e.tensor_copy(
                    sidx[:, k2 * 16 + s16:k2 * 16 + s16 + 1], scol[:, k2:k2 + 1]))
                v2.wait(t4)
            g2s.wait(t4)
            if prev_sc is not None:
                g2s.wait(prev_sc)
            txr = g2s.dma(lambda e, s16=s16, t0p=t0p: e.dma_start(
                out=xrow, in_=XNO[t0p + s16 * 128:t0p + (s16 + 1) * 128, :]),
                ch="cxr")
            g2s.wait(txr)
            s1 = g2s.dma(lambda e, s16=s16: e.indirect_dma_start(
                XS, IOX(sidx[:, s16:s16 + 1], 0), xrow, None), ch="csc")
            g2s.wait(s1)
            prev_sc = g2s.dma(lambda e, s16=s16: e.indirect_dma_start(
                XS, IOX(sidx[:, 16 + s16:17 + s16], 0), xrow, None), ch="csc")
        g2s.wait(prev_sc)
        if gp == 0 and _os.environ.get("C2DBG"):
            d1 = g2s.dma(lambda e: e.dma_start(out=p.dram["DBGI"], in_=sidx[:]),
                         ch="cxr")
            d2 = g2s.dma(lambda e: e.dma_start(out=p.dram["DBGC"],
                                               in_=acc[0:16, 0:6144]), ch="cxr")
            g2s.wait(d2)

        if _stop == "slot":
            continue
        slot_pe = p.C["sPE"]
        slot_ve = p.C["sVE"]
        csc_final = p.C["qG_csc"]
        Le = Loop(p, 16, f"Ce{gp}")
        ke = {}

        @Le.sec("sync")
        def _(ei, em, ke=ke, slot_pe=slot_pe, csc_final=csc_final, sbase=sbase):
            em.wait_abs("sPE", slot_pe)
            em.wait_abs("qG_csc", csc_final)
            em.w(ke, "umm7", lag=1)
            for dc in range(16):
                ke[f"xsT{dc}"] = em.dma(lambda e, ei=ei, dc=dc, sbase=sbase: e.dma_start(
                    out=cbig[:, dc * 512:(dc + 1) * 512],
                    in_=XS[bass.ds(sbase + ei * 512, 512), dc * 128:(dc + 1) * 128],
                    transpose=True), ch=f"xn{dc}")

        @Le.sec("gpsimd")
        def _(ei, em, ke=ke, slot_pe=slot_pe):
            em.wait_abs("sPE", slot_pe)
            for m in range(8):
                em.w(ke, f"umm{(m + 6) % 8}", lag=1 if m < 2 else 0)
                ke[f"guld{m}"] = em.dma(lambda e, ei=ei, m=m: e.dma_start(
                    out=gu_s[m % 2][:].rearrange("p (n f) -> p n f", f=128),
                    in_=gu_t[bass.ds(ei, 1), m].rearrange("x a c p f -> p (x a c) f")),
                    ch=f"cgu{m % 2}")
            for j in range(16):
                em.w(ke, f"dnmm{(j + 14) % 16}", lag=1 if j < 2 else 0)
                ke[f"dld{j}"] = em.dma(lambda e, ei=ei, j=j: e.dma_start(
                    out=dd_s[j % 2][:, 0:1024].rearrange("p (n f) -> p n f", f=128),
                    in_=d2_t[bass.ds(ei, 1), j].rearrange("x a p f -> p (x a) f")),
                    ch=f"cdd{j % 2}")

        @Le.sec("tensor")
        def _(ei, em, ke=ke, slot_ve=slot_ve):
            em.wait_abs("sVE", slot_ve)
            for m in range(8):
                em.w(ke, f"guld{m}")
                if m == 0:
                    for dc in range(16):
                        em.w(ke, f"xsT{dc}")
                em.w(ke, f"silu{(m + 7) % 8}", lag=1 if m == 0 else 0)
                for dc in range(16):
                    mm = em.do(lambda e, m=m, dc=dc: e.matmul(
                        PA[:], gu_s[m % 2][:, dc * 128:(dc + 1) * 128],
                        cbig[:, dc * 512:(dc + 1) * 512],
                        start=(dc == 0), stop=(dc == 15)), inc=(dc == 15))
                ke[f"gumm{m}"] = mm
                em.w(ke, f"hstt{(m + 7) % 8}", lag=1 if m == 0 else 0)
                for dc in range(16):
                    mm = em.do(lambda e, m=m, dc=dc: e.matmul(
                        PB[:], gu_s[m % 2][:, (16 + dc) * 128:(17 + dc) * 128],
                        cbig[:, dc * 512:(dc + 1) * 512],
                        start=(dc == 0), stop=(dc == 15)), inc=(dc == 15))
                ke[f"umm{m}"] = mm
            for j in range(16):
                em.w(ke, f"dld{j}")
                if j == 0:
                    for m in range(8):
                        em.w(ke, f"hstt{m}")
                em.w(ke, f"dnev{(j + 14) % 16}", lag=1 if j < 2 else 0)
                bank = PC_ if j % 2 == 0 else PD
                for m in range(8):
                    mm = em.do(lambda e, j=j, m=m, bank=bank: e.matmul(
                        bank[:], dd_s[j % 2][:, m * 128:(m + 1) * 128],
                        che[:, m * 512:(m + 1) * 512],
                        start=(m == 0), stop=(m == 7)), inc=(m == 7))
                ke[f"dnmm{j}"] = mm
            for k in range(4):
                for jg in range(4):
                    q = k * 4 + jg
                    em.w(ke, f"dtev{(q + 14) % 16}", lag=1 if q < 2 else 0)
                    for jj in range(4):
                        j = jg * 4 + jj
                        if k == 0:
                            em.w(ke, f"dnev{j}")
                        mm = em.do(lambda e, j=j, jj=jj, k=k, q=q: e.matmul(
                            PT[q % 2][:, jj * 128:(jj + 1) * 128],
                            cbig[:, 8192 + j * 512 + k * 128:8192 + j * 512 + (k + 1) * 128],
                            idb[:], start=(jj == 0), stop=(jj == 3),
                            is_transpose=True), inc=(jj == 3))
                    ke[f"trq{q}"] = mm
            ke["trlast"] = ke["trq15"]

        @Le.sec("scalar")
        def _(ei, em, ke=ke, sbase=sbase):
            for m in range(8):
                em.w(ke, f"hstt{(m + 7) % 8}", lag=1 if m == 0 else 0)
                em.w(ke, f"gumm{m}")
                ke[f"silu{m}"] = em.do(lambda e: e.activation(tS[:], PA[:], AF.Silu))
            for k in range(4):
                for jg in range(4):
                    em.w(ke, f"dtev{k * 4 + jg}")
                if k >= 2:
                    em.w(ke, f"dnst{k - 2}")
                else:
                    em.wait_prev_iters(ke.get(f"dnst{k + 2}"))
                ke[f"dnst{k}"] = em.dma(lambda e, ei=ei, k=k, sbase=sbase: e.dma_start(
                    out=DN[bass.ds(sbase + ei * 512 + k * 128, 128), :],
                    in_=dntm[:, (k % 2) * 2048:(k % 2 + 1) * 2048]),
                    ch=f"dnst{k % 2}")

        @Le.sec("vector")
        def _(ei, em, ke=ke, csc_final=csc_final):
            em.w(ke, "dnmm15", lag=1)
            for m in range(8):
                em.w(ke, f"silu{m}")
                em.w(ke, f"umm{m}")
                ke[f"hstt{m}"] = em.do(lambda e, m=m: e.scalar_tensor_tensor(
                    che[:, m * 512:(m + 1) * 512], tS[:], 1.0, PB[:],
                    ALU.bypass, ALU.mult))
            em.w(ke, "trlast", lag=1)
            em.wait_abs("qG_csc", csc_final)
            for j in range(16):
                em.w(ke, f"dnmm{j}")
                bank = PC_ if j % 2 == 0 else PD
                ke[f"dnev{j}"] = em.do(lambda e, j=j, bank=bank: e.tensor_copy(
                    cbig[:, 8192 + j * 512:8192 + (j + 1) * 512], bank[:]))
            for k in range(4):
                for jg in range(4):
                    q = k * 4 + jg
                    em.w(ke, f"trq{q}")
                    if jg == 0:
                        if k >= 2:
                            em.w(ke, f"dnst{k - 2}")
                        else:
                            em.wait_prev_iters(ke.get(f"dnst{k + 2}"))
                    ke[f"dtev{q}"] = em.do(lambda e, k=k, jg=jg, q=q: e.tensor_copy(
                        dntm[:, (k % 2) * 2048 + jg * 512:(k % 2) * 2048 + (jg + 1) * 512],
                        PT[q % 2][:]))

        Le.close()

        if _stop == "e":
            continue
        for gh in range(2):
            t0 = t0p + gh * 1024
            sy2 = p.straight("sync")
            if gh == 1:
                sy2.wait_abs("sPE", p.C["sPE"])
            xnf_tok2 = {}
            for dc in range(16):
                sy2.wait_abs(f"qS_xn{dc}", Le.final[f"qS_xn{dc}"])
                xnf_tok2[dc] = sy2.dma(lambda e, t0=t0, dc=dc: e.dma_start(
                    out=xnf[:, dc * 1024:(dc + 1) * 1024],
                    in_=XNO[t0:t0 + 1024, dc * 128:(dc + 1) * 128],
                    transpose=True), ch=f"xn{dc}")
            pe_gate_h = p.C["sPE"]
            ve_gate_h = p.C["sVE"]
            sc_gate_h = p.C["sSC"]
            Lm = Loop(p, 8, f"Csm{gp}{gh}")
            km = {}

            @Lm.sec("gpsimd")
            def _(mh, em, km=km, pe_gate_h=pe_gate_h):
                em.wait_abs("sPE", pe_gate_h)
                for s in range(2):
                    em.w(km, f"hmul{s}", lag=1)
                    km[f"wg{s}"] = em.dma(lambda e, s=s, mh=mh: e.dma_start(
                        out=sh_s[s].rearrange("p (n f) -> p n f", f=128),
                        in_=shg_t[bass.ds(mh * 2 + s, 1)].rearrange(
                            "a c p f -> p (a c) f")), ch=f"cgu{s}")
                    em.w(km, f"hmul{s}", lag=1)
                    km[f"wu{s}"] = em.dma(lambda e, s=s, mh=mh: e.dma_start(
                        out=sh_s[2 + s].rearrange("p (n f) -> p n f", f=128),
                        in_=shu_t[bass.ds(mh * 2 + s, 1)].rearrange(
                            "a c p f -> p (a c) f")), ch=f"cshu{s}")

            @Lm.sec("tensor")
            def _(mh, em, km=km, ve_gate_h=ve_gate_h, sc_gate_h=sc_gate_h,
                  xnf_tok2=xnf_tok2):
                em.wait_abs("sVE", ve_gate_h)
                em.wait_abs("sSC", sc_gate_h)
                for dc in range(16):
                    em.wait(xnf_tok2[dc])
                for s in range(2):
                    for u in range(2):
                        k4 = s * 2 + u
                        ps_, pu_ = divmod((k4 + 3) % 4, 2)
                        em.w(km, f"silu{ps_}{pu_}", lag=1 if k4 == 0 else 0)
                        em.w(km, f"wg{s}")
                        for dc in range(16):
                            mm = em.do(lambda e, s=s, u=u, dc=dc: e.matmul(
                                PA[:], sh_s[s][:, dc * 128:(dc + 1) * 128],
                                xnf[:, dc * 1024 + u * 512:dc * 1024 + (u + 1) * 512],
                                start=(dc == 0), stop=(dc == 15)), inc=(dc == 15))
                        km[f"gmm{s}{u}"] = mm
                        em.w(km, f"tmul{ps_}{pu_}", lag=1 if k4 == 0 else 0)
                        em.w(km, f"wu{s}")
                        for dc in range(16):
                            mm = em.do(lambda e, s=s, u=u, dc=dc: e.matmul(
                                PB[:], sh_s[2 + s][:, dc * 128:(dc + 1) * 128],
                                xnf[:, dc * 1024 + u * 512:dc * 1024 + (u + 1) * 512],
                                start=(dc == 0), stop=(dc == 15)), inc=(dc == 15))
                        km[f"umm{s}{u}"] = mm

            @Lm.sec("scalar")
            def _(mh, em, km=km):
                for s in range(2):
                    for u in range(2):
                        em.w(km, f"tmul{(s + 1) % 2}{u}", lag=1 if s == 0 else 0)
                        em.w(km, f"gmm{s}{u}")
                        km[f"silu{s}{u}"] = em.do(lambda e, s=s, u=u: e.activation(
                            tS[:] if u == 0 else tT[:], PA[:], AF.Silu))

            @Lm.sec("vector")
            def _(mh, em, km=km, pe_gate_h=pe_gate_h):
                for s in range(2):
                    for u in range(2):
                        em.w(km, f"silu{s}{u}")
                        em.w(km, f"umm{s}{u}")
                        em.wait_abs("sPE", pe_gate_h)
                        km[f"tmul{s}{u}"] = em.do(lambda e, s=s, u=u, mh=mh:
                            e.scalar_tensor_tensor(
                                cbig[:, bass.ds(((mh * 2 + s) * 1024 + u * 512), 512)],
                                (tS if u == 0 else tT)[:], 1.0, PB[:],
                                ALU.bypass, ALU.mult))
                    km[f"hmul{s}"] = km[f"tmul{s}1"]

            Lm.close()

            Lj = Loop(p, 8, f"Csj{gp}{gh}")
            kj = {}

            @Lj.sec("gpsimd")
            def _(jh, em, kj=kj, pe_gate_h=pe_gate_h):
                em.wait_abs("sPE", pe_gate_h)
                for s in range(2):
                    em.w(kj, f"dnev{s}1", lag=1)
                    kj[f"wd{s}"] = em.dma(lambda e, s=s, jh=jh: e.dma_start(
                        out=sd_s[s][:].rearrange("p (n f) -> p n f", f=128),
                        in_=shd_t[bass.ds(jh * 2 + s, 1)].rearrange(
                            "a c p f -> p (a c) f")), ch=f"cdd{s}")

            @Lj.sec("tensor")
            def _(jh, em, kj=kj, ve_gate_h=ve_gate_h):
                em.wait_abs("sVE", ve_gate_h)
                for s in range(2):
                    bank = PC_ if s == 0 else PD
                    for u in range(2):
                        if u == 1:
                            em.w(kj, f"dnev{s}0")
                        else:
                            em.w(kj, f"dnev{s}1", lag=1)
                        em.w(kj, f"wd{s}")
                        for m in range(16):
                            mm = em.do(lambda e, s=s, u=u, m=m, bank=bank: e.matmul(
                                bank[:], sd_s[s][:, m * 128:(m + 1) * 128],
                                cbig[:, m * 1024 + u * 512:m * 1024 + (u + 1) * 512],
                                start=(m == 0), stop=(m == 15)), inc=(m == 15))
                        kj[f"dnmm{s}{u}"] = mm

            @Lj.sec("vector")
            def _(jh, em, kj=kj):
                for s in range(2):
                    bank = PC_ if s == 0 else PD
                    for u in range(2):
                        em.w(kj, f"dnmm{s}{u}")
                        kj[f"dnev{s}{u}"] = em.do(lambda e, s=s, u=u, bank=bank, jh=jh:
                            e.tensor_copy(
                                acc[:, bass.ds((jh * 2 + s) * 1024 + u * 512, 512)],
                                bank[:]))

            Lj.close()

            if _stop == "s":
                continue
            _fin = _os.environ.get("C2FIN", "full")
            lj_ve = Lj.final["sVE"]
            dnst_fin = (p.C["qA_dnst0"], p.C["qA_dnst1"])
            g3 = p.straight("gpsimd")
            te3 = p.straight("tensor")
            v3 = p.straight("vector")
            g3.wait_abs("qA_dnst0", dnst_fin[0])
            g3.wait_abs("qA_dnst1", dnst_fin[1])
            g3.wait_abs("sPE", Lj.final["sPE"])
            te3.wait_abs("sVE", lj_ve)
            kf = {}
            for i in range(8):
                s16 = gh * 8 + i
                if _fin != "sh":
                    g3.wait(kf.get(("wstt", i - 1, 3)))
                    kf[("g1", i)] = g3.dma(lambda e, s16=s16: e.indirect_dma_start(
                        g1v, None, DN, IOX(sidx[:, s16:s16 + 1], 0)), ch="cg1")
                    g3.wait(kf.get(("ostt", i - 1, 3)))
                    kf[("g2", i)] = g3.dma(lambda e, s16=s16: e.indirect_dma_start(
                        g2v, None, DN, IOX(sidx[:, 16 + s16:17 + s16], 0)), ch="cg2")
                if _fin == "g":
                    kf[("wstt", i, 3)] = kf[("g1", i)]
                    kf[("ostt", i, 3)] = kf[("g2", i)]
                    continue
                for jg in range(4):
                    bank = PE_ if jg % 2 == 0 else PF_
                    pg = (i, jg - 2) if jg >= 2 else (i - 1, jg + 2)
                    te3.wait(kf.get(("ostt",) + pg))
                    for jj in range(4):
                        j = jg * 4 + jj
                        mm = te3.do(lambda e, i=i, j=j, jj=jj, bank=bank: e.matmul(
                            bank[:, jj * 128:(jj + 1) * 128],
                            acc[:, j * 1024 + i * 128:j * 1024 + (i + 1) * 128],
                            idf[:], start=(jj == 0), stop=(jj == 3),
                            is_transpose=True), inc=(jj == 3))
                    kf[("trf", i, jg)] = mm
                    if _fin == "t":
                        kf[("ostt", i, jg)] = mm
                        kf[("wstt", i, jg)] = mm
                        continue
                    sl = (jg % 2) * 512
                    v3.wait(mm)
                    if jg == 0 and _fin not in ("sh",):
                        v3.wait(kf[("g1", i)])
                        v3.wait(kf[("g2", i)])
                    v3.wait(kf.get(("ost",) + pg))
                    if _fin in ("sh", "shg"):
                        t1 = v3.do(lambda e, bank=bank, sl=sl: e.tensor_copy(
                            otm[:, sl:sl + 512], bank[:]))
                        kf[("wstt", i, jg)] = t1
                        kf[("ostt", i, jg)] = t1
                        t2 = t1
                    else:
                        t1 = v3.do(lambda e, jg=jg, s16=s16, bank=bank, sl=sl:
                                   e.scalar_tensor_tensor(
                            otm[:, sl:sl + 512], g1v[:, jg * 512:(jg + 1) * 512],
                            wcol[:, s16:s16 + 1], bank[:], ALU.mult, ALU.add))
                        kf[("wstt", i, jg)] = t1
                        v3.wait(t1)
                        t2 = v3.do(lambda e, jg=jg, s16=s16, sl=sl:
                                   e.scalar_tensor_tensor(
                            otm[:, sl:sl + 512], g2v[:, jg * 512:(jg + 1) * 512],
                            wcol[:, 16 + s16:17 + s16], otm[:, sl:sl + 512],
                            ALU.mult, ALU.add))
                        kf[("ostt", i, jg)] = t2
                    if _fin == "ts":
                        continue
                    g3.wait(t2)
                    g3.wait(kf.get(("ost",) + pg))
                    kf[("ost", i, jg)] = g3.dma(lambda e, i=i, jg=jg, t0=t0:
                        e.dma_start(
                            out=OUT[t0 + i * 128:t0 + (i + 1) * 128,
                                    jg * 512:(jg + 1) * 512],
                            in_=otm[:, (jg % 2) * 512:(jg % 2 + 1) * 512],
                            accum_op=ALU.add), ch=f"cacc{jg % 2}")
            if _fin == "full":
                for jg in range(4):
                    g3.wait(kf[("ost", 7, jg)])

    return None


TB_FULL = 16384
TO_FULL = 4096
NKC_FULL = TB_FULL // 512
NQT_FULL = TO_FULL // 512


def _tile2(W):
    K, M = W.shape
    return np.ascontiguousarray(
        W.reshape(K // 128, 128, M // 128, 128).transpose(0, 2, 1, 3)).astype(np.float32)


def _tile4(W):
    K, M = W.shape
    return np.ascontiguousarray(W.reshape(K // 128, 128, M // 128, 128).transpose(0, 2, 1, 3))


def build_program():
    cfg = Cfg(TB=TB_FULL, TO=TO_FULL)
    p = Prog(cfg)
    psS_h = p.psum("psS", [128, 2048], F32)
    for i in (2, 3):
        p.psum(f"pf{i}", [128, 512], F32)
    for i in range(2):
        p.psum(f"pt{i}", [128, 512], BF16)
    p.ps["pf0"] = psS_h[:, 0:512]
    p.ps["pf1"] = psS_h[:, 512:1024]
    p.ps["pf4"] = psS_h[:, 1024:1536]
    p.ps["pf5"] = psS_h[:, 1536:2048]
    p.inp("xb", [TB_FULL, 2048], F32)
    p.inp("xo", [TO_FULL, 2048], F32)
    for nm, sh in (("wkvT_t", [16, 4, 128, 128]), ("wcT_t", [4, 2, 128, 128]),
                   ("wqT_t", [2, 2, 128, 128]), ("wkvcT_t", [2, 4, 128, 128]),
                   ("wc2lT_t", [2, 4, 128, 128]), ("wsumA_t", [4, 16, 128, 128]),
                   ("wsumB_t", [4, 16, 128, 128])):
        p.inp(nm, sh, F32)
    p.inp("gb_h", [128, 4], F32)
    p.inp("bo_h", [128, 16], F32)
    p.inp("ident_b", [128, 128], BF16)
    p.inp("ident_f", [128, 128], F32)
    p.inp("cosb", [2, 128, TB_FULL], F32)
    p.inp("sinb", [2, 128, TB_FULL], F32)
    p.inp("coso", [2, 128, TO_FULL], F32)
    p.inp("sino", [2, 128, TO_FULL], F32)
    p.inp("dtile", [128, 2048], F32)
    p.inp("dltab", [128, NQT_FULL * (NKC_FULL // 4) * 4], F32)
    p.inp("gu_t", [16, 8, 2, 16, 128, 128], F32)
    p.inp("d2_t", [16, 16, 8, 128, 128], F32)
    p.inp("shg_t", [16, 16, 128, 128], F32)
    p.inp("shu_t", [16, 16, 128, 128], F32)
    p.inp("shd_t", [16, 16, 128, 128], F32)
    p.inp("rt_t", [16, 128, 16], F32)
    p.inp("ecap_h", [16, 1], F32)
    p.scratch("XNB", [TB_FULL, 2048], BF16)
    p.dram["XNO"] = p.dram["XNB"]
    p.scratch("KF", [NKC_FULL * 128, 1024], BF16)
    p.scratch("VT", [TB_FULL, 256], BF16)
    p.scratch("QF", [NQT_FULL * 128, 1024], BF16)
    p.scratch("XS", [(TO_FULL // 2048) * 8192 + 128, 2048], BF16)
    p.scratch("DN", [(TO_FULL // 2048) * 8192 + 128, 2048], BF16)
    p.out("OUT", [TO_FULL, 2048], F32)

    emit_P0(p)
    emit_A1(p, "xb", TB_FULL, "XNB")
    p.barrier()
    emit_A2(p, "XNB", TB_FULL, "cosb", "sinb", do_kv=True, kf_name="KF", vt_name="VT")
    p.barrier()
    emit_A2(p, "XNO", TO_FULL, "coso", "sino", do_kv=False, qf_name="QF")
    p.barrier()
    p.free_sbufs(("a1_", "a2_", "wsA", "wsB"))
    emit_B(p, "QF", "KF", "VT", "xo", "OUT", NKC_FULL, NQT_FULL)
    p.barrier()
    p.free_sbufs(("b_", "wkvT", "wcT", "wqT", "wkvcT", "wc2lT", "wsum"))
    emit_C2(p, "XNO", "OUT", TO_FULL)
    p.barrier()
    return p.finalize()


def kernel(L0_attn_x, L0_attn_W_to_kv, L0_attn_W_compress, L0_attn_ln_gamma,
           L0_attn_ln_beta, L0_attn_W_q_comp, L0_attn_W_kv_comp, L0_attn_W_comp2lat,
           L0_attn_W_out_res, L0_attn_b_out_res, L0_ffn_router_w, L0_ffn_gate_w,
           L0_ffn_up_w, L0_ffn_down_w, L0_ffn_shared_gate_w, L0_ffn_shared_up_w,
           L0_ffn_shared_down_w):
    from concourse.bass_utils import run_bass_kernel_spmd
    x = np.ascontiguousarray(np.asarray(L0_attn_x, dtype=np.float32))
    gamma = np.asarray(L0_attn_ln_gamma, np.float32)
    beta = np.asarray(L0_attn_ln_beta, np.float32)
    b_out = np.asarray(L0_attn_b_out_res, np.float32)

    shared = dict(
        wkvT_t=_tile2(np.asarray(L0_attn_W_to_kv, np.float32).T),
        wcT_t=_tile2(np.asarray(L0_attn_W_compress, np.float32).T),
        wqT_t=_tile2(np.asarray(L0_attn_W_q_comp, np.float32).T),
        wkvcT_t=_tile2(np.asarray(L0_attn_W_kv_comp, np.float32).T),
        wc2lT_t=_tile2(np.asarray(L0_attn_W_comp2lat, np.float32).T),
        wsumA_t=_tile2(np.asarray(L0_attn_W_to_kv, np.float32)),
        wsumB_t=_tile2(np.asarray(L0_attn_W_out_res, np.float32).T),
        gb_h=np.ascontiguousarray(
            np.stack([gamma[:128], gamma[128:], beta[:128], beta[128:]], 1)).astype(np.float32),
        bo_h=np.ascontiguousarray(b_out.reshape(16, 128).T).astype(np.float32),
        ident_b=np.eye(128, dtype=BF),
        ident_f=np.eye(128, dtype=np.float32),
        dtile=(np.arange(2048)[None, :] - np.arange(128)[:, None]).astype(np.float32),
        rt_t=np.ascontiguousarray(
            np.asarray(L0_ffn_router_w, np.float32).T.reshape(16, 128, 16)),
    )
    T = TB_FULL
    inv = 1.0 / 10000.0 ** (np.arange(0, 256, 2, dtype=np.float64) / 256.0)
    fr = np.arange(T, dtype=np.float64)[:, None] * inv[None, :]
    cos = np.concatenate([np.cos(fr), np.cos(fr)], 1).T.astype(np.float32)
    sin = np.concatenate([np.sin(fr), np.sin(fr)], 1).T.astype(np.float32)
    shared["cosb"] = np.ascontiguousarray(cos.reshape(2, 128, T))
    shared["sinb"] = np.ascontiguousarray(sin.reshape(2, 128, T))
    gate_w = np.asarray(L0_ffn_gate_w, np.float32)
    up_w = np.asarray(L0_ffn_up_w, np.float32)
    down_w = np.asarray(L0_ffn_down_w, np.float32)
    gu_t = np.zeros((16, 8, 2, 16, 128, 128), np.float32)
    d2_t = np.zeros((16, 16, 8, 128, 128), np.float32)
    for e in range(16):
        gu_t[e, :, 0] = _tile4(gate_w[e]).transpose(1, 0, 2, 3)
        gu_t[e, :, 1] = _tile4(up_w[e]).transpose(1, 0, 2, 3)
        d2_t[e] = _tile4(down_w[e]).transpose(1, 0, 2, 3)
    shared["gu_t"] = gu_t
    shared["d2_t"] = d2_t
    shared["ecap_h"] = (np.arange(16, dtype=np.float32) * 512.0).reshape(16, 1)
    shared["shg_t"] = np.ascontiguousarray(
        _tile4(np.asarray(L0_ffn_shared_gate_w, np.float32)).transpose(1, 0, 2, 3)).astype(np.float32)
    shared["shu_t"] = np.ascontiguousarray(
        _tile4(np.asarray(L0_ffn_shared_up_w, np.float32)).transpose(1, 0, 2, 3)).astype(np.float32)
    shared["shd_t"] = np.ascontiguousarray(
        _tile4(np.asarray(L0_ffn_shared_down_w, np.float32)).transpose(1, 0, 2, 3)).astype(np.float32)
    in_maps = []
    for c in range(8):
        b, q = divmod(c, 4)
        m = dict(shared)
        chunks = [2 * q, 2 * q + 1] + [ch for ch in range(8) if ch // 2 != q]
        row_perm = np.concatenate(
            [np.arange(ch * 2048, (ch + 1) * 2048) for ch in chunks])
        m["xb"] = np.ascontiguousarray(x[b][row_perm])
        m["cosb"] = np.ascontiguousarray(shared["cosb"][:, :, row_perm])
        m["sinb"] = np.ascontiguousarray(shared["sinb"][:, :, row_perm])
        m["xo"] = np.ascontiguousarray(x[b, q * TO_FULL:(q + 1) * TO_FULL])
        m["coso"] = np.ascontiguousarray(shared["cosb"][:, :, q * TO_FULL:(q + 1) * TO_FULL])
        m["sino"] = np.ascontiguousarray(shared["sinb"][:, :, q * TO_FULL:(q + 1) * TO_FULL])
        NKB = NKC_FULL // 4
        dl = np.zeros((128, NQT_FULL * NKB * 4), np.float32)
        for qt in range(NQT_FULL):
            for kb_ in range(NKB):
                for sub in range(4):
                    dl[:, (qt * NKB + kb_) * 4 + sub] = (
                        q * TO_FULL + qt * 512 + sub * 128 - chunks[kb_] * 2048)
        m["dltab"] = dl
        in_maps.append(m)

    nc = build_program()
    res = run_bass_kernel_spmd(nc, in_maps, core_ids=list(range(8)))
    out = np.empty((2, TB_FULL, 2048), dtype=np.float32)
    for c in range(8):
        b, q = divmod(c, 4)
        out[b, q * TO_FULL:(q + 1) * TO_FULL] = res.results[c]["OUT"]
    return out
